# version 06
# baseline (speedup 1.0000x reference)
import numpy as np
import concourse.bass as bass
import concourse.mybir as mybir
from concourse.bass_utils import run_bass_kernel_spmd


from collections import defaultdict

ENGINES = ("sync", "tensor", "vector", "scalar", "gpsimd")


class Buf:
    __slots__ = ("name", "writer", "readers")

    def __init__(self, name="buf"):
        self.name = name
        self.writer = None
        self.readers = []

    def merge_from(self, olds):
        rd = []
        for o in olds:
            rd.extend(o.readers)
            if o.writer is not None:
                rd.append(o.writer)
        self.readers = rd
        self.writer = None
        return self


class OpRec:
    __slots__ = ("engine", "fn", "ins", "sem_updates", "idx", "tok", "is_wait")

    def __init__(self, engine, fn, idx, is_wait=False):
        self.engine = engine
        self.fn = fn
        self.ins = None
        self.sem_updates = []
        self.idx = idx
        self.tok = None
        self.is_wait = is_wait


class Emitter:
    def __init__(self, nc):
        self.nc = nc
        self.q = {e: [] for e in ENGINES}
        self.sems = {}
        self.cnt = defaultdict(int)
        self.last_wait = defaultdict(lambda: -1)
        self.pending = {e: [] for e in ENGINES}
        self.last_real = {e: None for e in ENGINES}
        self._idx = 0
        self.n_ops = 0
        self.dma_sems = defaultdict(set)

    def sem(self, name):
        if name not in self.sems:
            self.sems[name] = self.nc.alloc_semaphore(name)
        return self.sems[name]

    def _queue(self, engine, fn, is_wait=False):
        rec = OpRec(engine, fn, self._idx, is_wait)
        self._idx += 1
        self.q[engine].append(rec)
        if not is_wait:
            self.last_real[engine] = rec
            self.n_ops += 1
        return rec

    def raw_wait(self, engine, sem_name, value):
        if value <= 0 or self.last_wait[(engine, sem_name)] >= value:
            return
        self.last_wait[(engine, sem_name)] = value
        h = self.sem(sem_name)
        def fn(eng, h=h, v=value):
            eng.wait_ge(h, v)
            return None
        self._queue(engine, fn, is_wait=True)

    def _wait_tok(self, engine, tok):
        if tok is not None:
            self.raw_wait(engine, tok[0], tok[1])

    def fence(self, engine, sem_name=None, by=1):
        sem_name = sem_name or ("s_" + engine)
        h = self.sem(sem_name)
        last = self.last_real[engine]
        assert last is not None, f"fence on engine {engine} with no ops"
        self.cnt[sem_name] += by
        tok = (sem_name, self.cnt[sem_name])
        last.sem_updates.append((h, by))
        for rec in self.pending[engine]:
            rec.tok = tok
        self.pending[engine].clear()
        return tok

    def op(self, engine, fn, reads=(), writes=(), sig=None, dma_sem=None):
        for b in reads:
            self._dep(engine, b.writer)
        for b in writes:
            self._dep(engine, b.writer)
            for r in b.readers:
                self._dep(engine, r)
        if dma_sem is not None and dma_sem.startswith("ds_"):
            self.raw_wait(engine, dma_sem, self.cnt[dma_sem])
        rec = self._queue(engine, fn)
        if dma_sem is not None:
            h = self.sem(dma_sem)
            self.cnt[dma_sem] += 16
            rec.sem_updates.append((h, 16))
            rec.tok = (dma_sem, self.cnt[dma_sem])
            self.dma_sems[engine].add(dma_sem)
        else:
            self.pending[engine].append(rec)
        for b in reads:
            b.readers.append((engine, rec))
        for b in writes:
            b.writer = (engine, rec)
            b.readers = []
        if sig:
            return self.fence(engine, sig if sig is not True else None)
        return rec

    def _dep(self, engine, dep):
        if dep is None:
            return
        p_eng, rec = dep
        if rec.tok is None:
            if p_eng == "tensor" and engine == "tensor":
                return
            assert rec in self.pending[p_eng], "dep on unfenced, non-pending op"
            self.fence(p_eng)
        if p_eng == engine == "tensor":
            return
        self._wait_tok(engine, rec.tok)

    def barrier_note(self):
        for e in ENGINES:
            if self.pending[e]:
                self.fence(e)
        for e in ENGINES:
            for s in list(self.cnt):
                if self.last_wait[(e, s)] < self.cnt[s]:
                    self.last_wait[(e, s)] = self.cnt[s]

    def play_block(self):
        emitter = self
        for e in ENGINES:
            for s in sorted(self.dma_sems[e]):
                self.raw_wait(e, s, self.cnt[s])
        for e in ENGINES:
            if self.pending[e]:
                self.fence(e)

        with self.nc.Block() as block:
            for e in ENGINES:
                if not self.q[e]:
                    continue

                def make_body(eng_name):
                    def body(eng):
                        for rec in emitter.q[eng_name]:
                            ins = rec.fn(eng)
                            rec.ins = ins
                            for (h, by) in rec.sem_updates:
                                assert ins is not None, \
                                    f"sem update on wait-op ({eng_name})"
                                ins.then_inc(h, by)
                    return body

                getattr(block, e)(make_body(e))
        for e in ENGINES:
            self.q[e] = []
            self.pending[e] = []
            self.last_real[e] = None
        self.barrier_note()



F32 = mybir.dt.float32
BF16 = mybir.dt.bfloat16
AF = mybir.ActivationFunctionType
ALU = mybir.AluOpType
AXX = mybir.AxisListType.X

T = 4096
SEQ = 2048
C = 1024
NT = T // 128
NSEQ = 2

SB_BASE = 33024
SB_TOP = 229344


class SB:
    def __init__(self, nc):
        self.nc = nc
        self.off = SB_BASE
        self.perm_off = SB_TOP
        self.n = 0

    def _alloc(self, shape, dtype, off):
        name = f"sb{self.n}"
        self.n += 1
        return self.nc.alloc_sbuf_tensor_at(name, list(shape), dtype, offset=off)

    def take(self, shape, dtype):
        itemsize = 2 if dtype == BF16 else 4
        bytes_pp = int(np.prod(shape[1:])) * itemsize
        off = (self.off + 63) // 64 * 64
        self.last_off = off
        self.off = off + bytes_pp
        assert self.off <= self.perm_off, f"SBUF overflow {self.off}"
        return self._alloc(shape, dtype, off)

    def overlay(self, off, shape, dtype):
        itemsize = 2 if dtype == BF16 else 4
        bytes_pp = int(np.prod(shape[1:])) * itemsize
        assert off + bytes_pp <= self.perm_off
        return self._alloc(shape, dtype, off)

    def perm(self, shape, dtype):
        itemsize = 2 if dtype == BF16 else 4
        bytes_pp = int(np.prod(shape[1:])) * itemsize
        self.perm_off = (self.perm_off - bytes_pp) // 64 * 64
        assert self.off <= self.perm_off, "SBUF overflow (perm)"
        return self._alloc(shape, dtype, self.perm_off)

    def reset(self, to=SB_BASE):
        self.off = to


class P:
    def __init__(self, nc, dbg=False):
        self.nc = nc
        self.em = Emitter(nc)
        self.sb = SB(nc)
        self.dbg = dbg
        self.dram = {}
        self.psum_f = nc.alloc_psum_tensor("ps_f", [128, 8, 512], F32)
        self.bank = [Buf(f"bank{i}") for i in range(8)]
        self.identb = self.sb.perm([128, 128], BF16)
        self.b_identb = Buf("identb")
        self.identf = self.sb.perm([128, 128], F32)
        self.b_identf = Buf("identf")
        self.one128 = self.sb.perm([128, 128], F32)
        self.b_one128 = Buf("one128")
        self.onesrow = self.sb.perm([1, 512], F32)
        self.b_onesrow = Buf("onesrow")

    def din(self, name, shape, dtype=F32):
        t = self.nc.dram_tensor(name, list(shape), dtype, kind="ExternalInput")
        self.dram[name] = t.ap()
        return self.dram[name]

    def dout(self, name, shape, dtype=F32):
        t = self.nc.dram_tensor(name, list(shape), dtype, kind="ExternalOutput")
        self.dram[name] = t.ap()
        return self.dram[name]

    def dint(self, name, shape, dtype=F32):
        t = self.nc.dram_tensor(name, list(shape), dtype, kind="Internal")
        self.dram[name] = t.ap()
        return self.dram[name]


def declare_io(p: P, last_phase):
    d = p.din
    d("L0_attn_x", [T, C])
    for w in ["Wr", "Wk", "Wv", "Wo"]:
        d(f"L0_attn_{w}", [C, C])
    for v in ["br", "bk", "bv", "bo", "time_decay", "U"]:
        d(f"L0_attn_{v}", [C])
    d("L0_ffn_wqkv", [2, 1024, 1536]); d("L0_ffn_q_scale", [2, 64])
    d("L0_ffn_k_scale", [2, 64]); d("L0_ffn_wo", [2, 512, 1024])
    d("L0_ffn_ln_g", [2, 1024]); d("L0_ffn_ln_b", [2, 1024])
    d("L0_ffn_w1", [2, 1024, 5460]); d("L0_ffn_b1", [2, 5460])
    d("L0_ffn_w2", [2, 2730, 1024]); d("L0_ffn_b2", [2, 1024])
    d("L1_attn_w_attn", [1024, 3072]); d("L1_attn_b_attn", [3072])
    d("L1_attn_w_proj", [1024, 1024]); d("L1_attn_b_proj", [1024])
    d("L1_ffn_router_w", [64, 1024]); d("L1_ffn_e_bias", [64])
    d("L1_ffn_Wg", [64, 1024, 512]); d("L1_ffn_Wu", [64, 1024, 512])
    d("L1_ffn_Wd", [64, 512, 1024])
    d("L1_ffn_SWg", [1024, 1024]); d("L1_ffn_SWu", [1024, 1024]); d("L1_ffn_SWd", [1024, 1024])
    d("c_ident", [128, 128])
    d("c_ones", [1, 512])
    d("c_trem", [1, SEQ])
    d("c_rot", [4, SEQ, 512])
    d("c_slopepos", [16, SEQ])
    d("c_maskdiag", [4, 128, 512])
    d("c_one128", [128, 128])
    d("c_eidx", [128, 64])
    d("c_maskfar", [128, 512])
    d("c_mtri", [4, 128, 512])
    d("c_ftri", [128, 512])
    d("c_masktri", [128, 128])
    p.dint("d_res", [T, C])
    p.dint("d_xlnT", [C, T], BF16)
    p.dint("d_y", [T, C])
    p.dint("d_coefT", [64, T])
    p.dint("d_wT", [4, C, C], BF16)
    p.dint("d_sT", [C, T], BF16)
    p.dint("d_scoT", [64, T])
    p.dint("d_dbg", [128, 4096])
    p.dint("d_zT", [C, T], BF16)
    p.dout("out", [T, C])
    p.b_res = Buf("d_res")
    p.b_xlnTd = Buf("d_xlnT")
    p.b_out = Buf("out")


def load_consts(p: P):
    if getattr(p, "consts_loaded", False):
        return
    p.consts_loaded = True
    em = p.em
    em.op("gpsimd", lambda g: g.dma_start(out=p.identb[:], in_=p.dram["c_ident"][:]),
          writes=[p.b_identb], dma_sem="ds_gp")
    em.op("sync", lambda s: s.dma_start(out=p.identf[:], in_=p.dram["c_ident"][:]),
          writes=[p.b_identf], dma_sem="ds_sy")
    em.op("sync", lambda s: s.dma_start(out=p.one128[:], in_=p.dram["c_one128"][:]),
          writes=[p.b_one128], dma_sem="ds_sy")
    em.op("sync", lambda s: s.dma_start(out=p.onesrow[:], in_=p.dram["c_ones"][:]),
          writes=[p.b_onesrow], dma_sem="ds_sy")


def emit_ln_tile(p, eng_small, xin_ap, b_xin, sq_ap, b_sq, sm_ap, b_sm):
    em = p.em
    em.op("scalar", lambda e: e.activation(sq_ap, xin_ap, AF.Square,
                                           accum_out=sm_ap[:, 1:2]),
          reads=[b_xin], writes=[b_sq, b_sm], sig=True)
    em.op("vector", lambda e: e.reduce_sum(sm_ap[:, 0:1], xin_ap, axis=AXX),
          reads=[b_xin], writes=[b_sm])
    em.op("vector", lambda e: e.tensor_scalar(sm_ap[:, 2:3], sm_ap[:, 0:1],
                                              1.0 / C, None, ALU.mult),
          reads=[b_sm], writes=[b_sm])
    em.op("vector", lambda e: e.tensor_tensor(sm_ap[:, 3:4], sm_ap[:, 2:3],
                                              sm_ap[:, 2:3], ALU.mult),
          reads=[b_sm], writes=[b_sm])
    em.op("vector", lambda e: e.tensor_scalar(sm_ap[:, 4:5], sm_ap[:, 1:2],
                                              1.0 / C, None, ALU.mult),
          reads=[b_sm], writes=[b_sm])
    em.op("vector", lambda e: e.tensor_tensor(sm_ap[:, 4:5], sm_ap[:, 4:5],
                                              sm_ap[:, 3:4], ALU.subtract),
          reads=[b_sm], writes=[b_sm])
    em.op("vector", lambda e: e.tensor_scalar(sm_ap[:, 4:5], sm_ap[:, 4:5],
                                              1e-5, None, ALU.add),
          reads=[b_sm], writes=[b_sm], sig=True)
    em.op("scalar", lambda e: e.activation(sm_ap[:, 5:6], sm_ap[:, 4:5], AF.Sqrt),
          reads=[b_sm], writes=[b_sm], sig=True)
    em.op("vector", lambda e: e.reciprocal(sm_ap[:, 6:7], sm_ap[:, 5:6]),
          reads=[b_sm], writes=[b_sm], sig=True)


def ln_transpose_store(p: P, src_dram, b_src, ln=True, shift=False,
                       router=None, b_router=None):
    em, sb = p.em, p.sb
    save = sb.off
    xin = sb.take([128, 2, C], F32)
    sq = sb.take([128, 2, C], F32)
    xbf = sb.take([128, 2, C], BF16)
    evT = sb.take([128, 2, C], BF16)
    sm = sb.take([128, 2, 8], F32)
    b_xin = [Buf("xin0"), Buf("xin1")]
    b_sq = [Buf(), Buf()]
    b_xbf = [Buf(), Buf()]
    b_ev = [Buf(), Buf()]
    b_sm = [Buf(), Buf()]
    if router is not None:
        xf = sb.take([128, 2, C], F32)
        xfT = sb.take([128, 2, C], F32)
        scv = sb.take([64, 2, 128], F32)
        b_xf = [Buf(), Buf()]; b_xfT = [Buf(), Buf()]; b_scv = [Buf(), Buf()]
        b_scoT = Buf("d_scoT")
        scoT_d = p.dram["d_scoT"]
    dstT = p.dram["d_xlnT"].rearrange("(k p) t -> p k t", p=128)
    for i in range(NT):
        sl = i % 2
        if not shift:
            em.op("sync", lambda s, i=i, sl=sl: s.dma_start(
                out=xin[:, sl], in_=src_dram[i * 128:(i + 1) * 128, :]),
                reads=[b_src], writes=[b_xin[sl]], dma_sem=f"d_ln{sl}")
        else:
            r0 = i * 128
            em.op("sync", lambda s, r0=r0, sl=sl: s.dma_start(
                out=xin[:, sl, 0:512], in_=src_dram[r0:r0 + 128, 0:512]),
                reads=[b_src], writes=[b_xin[sl]], dma_sem=f"d_ln{sl}")
            if i % 16 == 0:
                em.op("sync", lambda s, r0=r0, sl=sl: s.dma_start(
                    out=xin[1:128, sl, 512:1024], in_=src_dram[r0:r0 + 127, 512:1024]),
                    reads=[b_src], writes=[b_xin[sl]], dma_sem=f"d_ln{sl}")
                em.op("vector", lambda e, sl=sl: e.memset(xin[0:1, sl, 512:1024], 0.0),
                      writes=[b_xin[sl]], sig=True)
            else:
                em.op("sync", lambda s, r0=r0, sl=sl: s.dma_start(
                    out=xin[:, sl, 512:1024], in_=src_dram[r0 - 1:r0 + 127, 512:1024]),
                    reads=[b_src], writes=[b_xin[sl]], dma_sem=f"d_ln{sl}")
        if ln:
            emit_ln_tile(p, None, xin[:, sl], b_xin[sl], sq[:, sl], b_sq[sl],
                         sm[:, sl], b_sm[sl])
            em.op("vector", lambda e, sl=sl: e.tensor_scalar(
                xbf[:, sl], xin[:, sl], sm[:, sl, 2:3], sm[:, sl, 6:7],
                ALU.subtract, ALU.mult),
                reads=[b_xin[sl], b_sm[sl]], writes=[b_xbf[sl]], sig=True)
        else:
            em.op("vector", lambda e, sl=sl: e.tensor_copy(xbf[:, sl], xin[:, sl]),
                  reads=[b_xin[sl]], writes=[b_xbf[sl]], sig=True)
        banks = [p.bank[2 * sl], p.bank[2 * sl + 1]]
        for k in range(8):
            em.op("tensor", lambda t, sl=sl, k=k: t.matmul(
                p.psum_f[:, 2 * sl + k // 4, (k % 4) * 128:(k % 4 + 1) * 128],
                xbf[:, sl, k * 128:(k + 1) * 128], p.identb[:], start=True, stop=True),
                reads=[b_xbf[sl], p.b_identb], writes=[banks[k // 4]])
        em.fence("tensor")
        em.op("vector", lambda e, sl=sl: e.tensor_copy(
            evT[:, sl].rearrange("p (a b) -> p a b", a=2), p.psum_f[:, 2 * sl:2 * sl + 2, :]),
            reads=banks, writes=[b_ev[sl]], sig=True)
        em.op("sync", lambda s, i=i, sl=sl: s.dma_start(
            out=dstT[:, :, i * 128:(i + 1) * 128],
            in_=evT[:, sl].rearrange("p (k f) -> p k f", k=8)),
            reads=[b_ev[sl]], writes=[p.b_xlnTd], dma_sem=f"d_lnst{sl}")
        if router is not None:
            em.op("vector", lambda e, sl=sl: e.tensor_scalar(
                xf[:, sl], xin[:, sl], sm[:, sl, 2:3], sm[:, sl, 6:7],
                ALU.subtract, ALU.mult),
                reads=[b_xin[sl], b_sm[sl]], writes=[b_xf[sl]], sig=True)
            banksf = [p.bank[4 + 2 * sl], p.bank[5 + 2 * sl]]
            for k in range(8):
                em.op("tensor", lambda t, sl=sl, k=k: t.matmul(
                    p.psum_f[:, 4 + 2 * sl + k // 4, (k % 4) * 128:(k % 4 + 1) * 128],
                    xf[:, sl, k * 128:(k + 1) * 128], p.identf[:], start=True, stop=True),
                    reads=[b_xf[sl], p.b_identf], writes=[banksf[k // 4]])
            em.fence("tensor")
            em.op("vector", lambda e, sl=sl: e.tensor_copy(
                xfT[:, sl].rearrange("p (a b) -> p a b", a=2),
                p.psum_f[:, 4 + 2 * sl:6 + 2 * sl, :]),
                reads=banksf, writes=[b_xfT[sl]], sig=True)
            for k in range(8):
                em.op("tensor", lambda t, sl=sl, k=k: t.matmul(
                    p.psum_f[0:64, 4 + 2 * sl, 0:128], router[:, k, :],
                    xfT[:, sl, k * 128:(k + 1) * 128], start=(k == 0), stop=(k == 7)),
                    reads=[b_router, b_xfT[sl]], writes=[banksf[0]], sig=True)
            em.op("vector", lambda e, sl=sl: e.tensor_copy(
                scv[:, sl], p.psum_f[0:64, 4 + 2 * sl, 0:128]),
                reads=[banksf[0]], writes=[b_scv[sl]], sig=True)
            em.op("sync", lambda s, i=i, sl=sl: s.dma_start(
                out=scoT_d[:, i * 128:(i + 1) * 128], in_=scv[:, sl]),
                reads=[b_scv[sl]], writes=[b_scoT], dma_sem=f"d_sco{sl}")
    em.play_block()
    sb.reset(save)


def sandwich_tail_tile(p, tt, psum_view, banks, src_res_dram, b_srcres, dst_dram,
                       b_dst, bufs, extra_reads=(), slot=None):
    em = p.em
    sl = slot if slot is not None else tt % 2
    ao, b_ao = bufs["ao"], bufs["b_ao"]
    sq, b_sq = bufs["sq"], bufs["b_sq"]
    sm, b_sm = bufs["sm"], bufs["b_sm"]
    xo, b_xo = bufs["xo"], bufs["b_xo"]
    em.op("vector", lambda e, sl=sl: e.tensor_copy(ao[:, sl], psum_view),
          reads=list(banks) + list(extra_reads), writes=[b_ao[sl]], sig=True)
    em.op("sync", lambda s, tt=tt, sl=sl: s.dma_start(
        out=xo[:, sl], in_=src_res_dram[tt * 128:(tt + 1) * 128, :]),
        reads=[b_srcres], writes=[b_xo[sl]], dma_sem=f"d_tail{sl}")
    emit_ln_tile(p, None, ao[:, sl].rearrange("p a b -> p (a b)"), b_ao[sl],
                 sq[:, sl], b_sq[sl], sm[:, sl], b_sm[sl])
    aof = ao[:, sl].rearrange("p a b -> p (a b)")
    em.op("vector", lambda e, sl=sl: e.tensor_scalar(
        aof, aof, sm[:, sl, 2:3], sm[:, sl, 6:7], ALU.subtract, ALU.mult),
        reads=[b_ao[sl], b_sm[sl]], writes=[b_ao[sl]], sig=True)
    em.op("vector", lambda e, sl=sl: e.tensor_tensor(
        aof, aof, xo[:, sl], ALU.add),
        reads=[b_ao[sl], b_xo[sl]], writes=[b_ao[sl]], sig=True)
    em.op("sync", lambda s, tt=tt, sl=sl: s.dma_start(
        out=dst_dram[tt * 128:(tt + 1) * 128, :], in_=aof),
        reads=[b_ao[sl]], writes=[b_dst], dma_sem=f"d_tailo{sl}")


def make_tail_bufs(p, slots=2, at=None, guards=()):
    sb = p.sb
    state = {"at": at}

    def grab(shape, dtype):
        if state["at"] is None:
            return sb.take(shape, dtype)
        itemsize = 2 if dtype == BF16 else 4
        off = (state["at"] + 63) // 64 * 64
        t = sb.overlay(off, shape, dtype)
        state["at"] = off + int(np.prod(shape[1:])) * itemsize
        return t

    def mb():
        return [Buf().merge_from(list(guards)) for _ in range(slots)]

    return dict(
        ao=grab([128, slots, 2, 512], F32), b_ao=mb(),
        sq=grab([128, slots, C], F32), b_sq=mb(),
        sm=grab([128, slots, 8], F32), b_sm=mb(),
        xo=grab([128, slots, C], F32), b_xo=mb(),
    )


def phase1(p: P):
    em, sb = p.em, p.sb
    EPSb = float(np.exp(-1.0 / 12.0))
    EPS_R, EPS_K, EPS_V = EPSb / 2.0, EPSb + 0.3 / 11.0, EPSb

    x_in = p.dram["L0_attn_x"]
    b_x = Buf("x_in")
    load_consts(p)
    ln_transpose_store(p, x_in, b_x)

    trep = sb.take([128, SEQ], F32); b_trep = Buf("trep")
    Upp = sb.take([128, 8], F32); b_Upp = Buf("Upp")
    nwpp = sb.take([128, 8], F32); b_nwpp = Buf("nwpp")
    persist = sb.off
    wtmp = sb.take([128, 8, 1024], BF16); b_wtmp = Buf("wtmp")
    wev = sb.take([128, 2, 1024], BF16); b_wev = [Buf(), Buf()]
    trow = sb.take([1, SEQ], F32); b_trow = Buf("trow")
    b_wTd = Buf("d_wT")
    wT_d = p.dram["d_wT"].rearrange("w (k p) c -> w p k c", p=128)

    em.op("sync", lambda s: s.dma_start(
        out=Upp[:], in_=p.dram["L0_attn_U"].rearrange("(k p) -> p k", p=128),
        allow_slow_non_contiguous=True),
        writes=[b_Upp], dma_sem="ds_sy")
    em.op("sync", lambda s: s.dma_start(
        out=nwpp[:], in_=p.dram["L0_attn_time_decay"].rearrange("(k p) -> p k", p=128),
        allow_slow_non_contiguous=True),
        writes=[b_nwpp], dma_sem="ds_sy")
    em.op("vector", lambda e: e.tensor_scalar(
        nwpp[:], nwpp[:], 0.0, -1.0, ALU.max, ALU.mult),
        reads=[b_nwpp], writes=[b_nwpp], sig=True)
    em.op("sync", lambda s: s.dma_start(out=trow[:], in_=p.dram["c_trem"][:]),
          writes=[b_trow], dma_sem="ds_sy")
    for tc in range(4):
        bk = p.bank[tc % 2]
        em.op("tensor", lambda t, tc=tc: t.matmul(
            p.psum_f[:, tc % 2, :], p.onesrow[0:1, 0:128],
            trow[0:1, tc * 512:(tc + 1) * 512], start=True, stop=True),
            reads=[b_trow, p.b_onesrow], writes=[bk], sig=True)
        em.op("vector", lambda e, tc=tc: e.tensor_copy(
            trep[:, tc * 512:(tc + 1) * 512], p.psum_f[:, tc % 2, :]),
            reads=[bk], writes=[b_trep], sig=True)
    for wi, w in enumerate(["Wr", "Wk", "Wv", "Wo"]):
        em.op("gpsimd", lambda g, w=w: g.dma_start(
            out=wtmp[:], in_=p.dram["L0_attn_" + w].rearrange("(k p) c -> p k c", p=128)),
            writes=[b_wtmp], dma_sem="ds_gp")
        for k in range(8):
            sl = k % 2
            banks = [p.bank[4 + 2 * sl], p.bank[5 + 2 * sl]]
            for j in range(8):
                em.op("tensor", lambda t, k=k, j=j, sl=sl: t.matmul(
                    p.psum_f[:, 4 + 2 * sl + j // 4, (j % 4) * 128:(j % 4 + 1) * 128],
                    wtmp[:, k, j * 128:(j + 1) * 128], p.identb[:], start=True, stop=True),
                    reads=[b_wtmp, p.b_identb], writes=[banks[j // 4]])
            em.fence("tensor")
            em.op("vector", lambda e, sl=sl: e.tensor_copy(
                wev[:, sl].rearrange("p (a b) -> p a b", a=2),
                p.psum_f[:, 4 + 2 * sl:6 + 2 * sl, :]),
                reads=banks, writes=[b_wev[sl]], sig=True)
            em.op("sync", lambda s, wi=wi, k=k, sl=sl: s.dma_start(
                out=wT_d[wi, :, :, k * 128:(k + 1) * 128],
                in_=wev[:, sl].rearrange("p (j f) -> p j f", j=8)),
                reads=[b_wev[sl]], writes=[b_wTd], dma_sem=f"d_wst{sl}")
    em.play_block()
    sb.reset(persist)

    xlnT_d = p.dram["d_xlnT"].rearrange("(k p) t -> p k t", p=128)
    sT_d = p.dram["d_sT"].rearrange("(k p) t -> p k t", p=128)
    zT_d = p.dram["d_zT"].rearrange("(k p) t -> p k t", p=128)

    WREG = sb.take([128, 2, 8, 1024], BF16)
    b_W0, b_W1 = Buf("W0"), Buf("W1")
    rows = sb.take([1, 2, 1024], F32)
    b_rows = Buf("rows")
    seq_base = sb.off

    for seq in range(NSEQ):
        sb.reset(seq_base)
        t0 = seq * SEQ
        PR = sb.take([128, 5, SEQ], F32)
        pr_off = sb.last_off
        xlnT = sb.overlay(pr_off, [128, 8, SEQ], BF16)
        ik = sb.take([128, 8, SEQ], BF16); b_ik = Buf("ik")
        iv = sb.take([128, 8, SEQ], BF16); b_iv = Buf("iv")
        kc = sb.take([128, SEQ], F32); b_kc = Buf("kc")
        vc = sb.take([128, SEQ], F32); b_vc = Buf("vc")
        sg = sb.take([128, 2, SEQ], BF16); b_sg = [Buf(), Buf()]
        zc = sb.take([128, 2, SEQ], BF16); b_zc = [Buf(), Buf()]
        sst = sb.take([128, 2, 512], BF16); b_sst = [Buf(), Buf()]
        b_xs = Buf("xlnT_s")
        bp = [Buf(f"pool{i}") for i in range(5)]
        b_sTd = Buf("d_sT"); b_zTd = Buf("d_zT")

        em.op("sync", lambda s, t0=t0: s.dma_start(
            out=xlnT[:], in_=xlnT_d[:, :, t0:t0 + SEQ]),
            reads=[p.b_xlnTd], writes=[b_xs], dma_sem="ds_sy")

        def interp(dst, b_dst, eps, scr_ap, b_scr, scr2_ap, b_scr2):
            a, b = float(eps), float(1.0 - eps)
            for k in range(8):
                em.op("vector", lambda e, k=k: e.tensor_scalar(
                    scr_ap[:, 0:SEQ - 2], xlnT[:, k, 2:SEQ], b, None, ALU.mult),
                    reads=[b_xs], writes=[b_scr], sig=True)
                em.op("vector", lambda e, k=k: e.tensor_scalar(
                    scr2_ap[:, 0:SEQ - 2], xlnT[:, k, 0:SEQ - 2], a, None, ALU.mult),
                    reads=[b_xs], writes=[b_scr2], sig=True)
                em.op("vector", lambda e, k=k: e.tensor_tensor(
                    dst[:, k, 1:SEQ - 1], scr_ap[:, 0:SEQ - 2], scr2_ap[:, 0:SEQ - 2], ALU.add),
                    reads=[b_scr, b_scr2], writes=[b_dst])
                em.op("vector", lambda e, k=k: e.tensor_scalar(
                    dst[:, k, 0:1], xlnT[:, k, 1:2], b, None, ALU.mult),
                    reads=[b_xs], writes=[b_dst])
                em.op("vector", lambda e, k=k: e.tensor_scalar(
                    dst[:, k, SEQ - 1:SEQ], xlnT[:, k, SEQ - 2:SEQ - 1], a, None, ALU.mult),
                    reads=[b_xs], writes=[b_dst], sig=True)

        interp(ik, b_ik, EPS_R, kc, b_kc, vc, b_vc)

        em.op("sync", lambda s: s.dma_start(out=WREG[:, 0], in_=wT_d[0]),
              reads=[b_wTd, b_W0], writes=[b_W0], dma_sem="d_wr0")
        em.op("sync", lambda s: s.dma_start(
            out=rows[0:1, 0], in_=p.dram["L0_attn_br"][None, :]),
            reads=[b_rows], writes=[b_rows], dma_sem="ds_sy")
        for tc in range(4):
            for m in range(8):
                bank = p.bank[m]
                for k in range(8):
                    em.op("tensor", lambda t, m=m, k=k, tc=tc: t.matmul(
                        p.psum_f[:, m, :], WREG[:, 0, k, m * 128:(m + 1) * 128],
                        ik[:, k, tc * 512:(tc + 1) * 512], start=(k == 0), stop=False),
                        reads=[b_W0, b_ik], writes=[bank])
                em.op("tensor", lambda t, m=m: t.matmul(
                    p.psum_f[:, m, :], rows[0:1, 0, m * 128:(m + 1) * 128],
                    p.onesrow[0:1, 0:512], start=False, stop=True),
                    reads=[b_rows, p.b_onesrow], writes=[bank], sig=True)
                sl = m % 2
                em.op("scalar", lambda e, m=m, sl=sl: e.activation(
                    sst[:, sl], p.psum_f[:, m, :], AF.Sigmoid),
                    reads=[bank], writes=[b_sst[sl]], sig=True)
                em.op("sync", lambda s, m=m, tc=tc, sl=sl, t0=t0: s.dma_start(
                    out=sT_d[:, m, t0 + tc * 512:t0 + (tc + 1) * 512], in_=sst[:, sl]),
                    reads=[b_sst[sl]], writes=[b_sTd], dma_sem=f"d_sst{sl}")

        interp(ik, b_ik, EPS_K, kc, b_kc, vc, b_vc)
        interp(iv, b_iv, EPS_V, kc, b_kc, vc, b_vc)

        em.op("sync", lambda s: s.dma_start(out=WREG[:, 0], in_=wT_d[1]),
              reads=[b_wTd, b_W0], writes=[b_W0], dma_sem="d_wr0")
        em.op("sync", lambda s: s.dma_start(out=WREG[:, 1], in_=wT_d[2]),
              reads=[b_wTd, b_W1], writes=[b_W1], dma_sem="d_wr1")
        em.op("sync", lambda s: s.dma_start(
            out=rows[0:1, 0], in_=p.dram["L0_attn_bk"][None, :]),
            reads=[b_rows], writes=[b_rows], dma_sem="ds_sy")
        em.op("sync", lambda s: s.dma_start(
            out=rows[0:1, 1], in_=p.dram["L0_attn_bv"][None, :]),
            reads=[b_rows], writes=[b_rows], dma_sem="ds_sy")

        for ci in range(8):
            sl = ci % 2
            em.op("sync", lambda s, ci=ci, sl=sl, t0=t0: s.dma_start(
                out=sg[:, sl], in_=sT_d[:, ci, t0:t0 + SEQ]),
                reads=[b_sTd], writes=[b_sg[sl]], dma_sem=f"d_sg{sl}")
            for (wsl, b_W, src, b_src, row, bk0) in [
                    (0, b_W0, ik, b_ik, 0, 0), (1, b_W1, iv, b_iv, 1, 4)]:
                for tc in range(4):
                    bank = p.bank[bk0 + tc]
                    for k in range(8):
                        em.op("tensor", lambda t, wsl=wsl, src=src, ci=ci, k=k, tc=tc, bk0=bk0: t.matmul(
                            p.psum_f[:, bk0 + tc, :], WREG[:, wsl, k, ci * 128:(ci + 1) * 128],
                            src[:, k, tc * 512:(tc + 1) * 512], start=(k == 0), stop=False),
                            reads=[b_W, b_src], writes=[bank])
                    em.op("tensor", lambda t, ci=ci, tc=tc, row=row, bk0=bk0: t.matmul(
                        p.psum_f[:, bk0 + tc, :], rows[0:1, row, ci * 128:(ci + 1) * 128],
                        p.onesrow[0:1, 0:512], start=False, stop=True),
                        reads=[b_rows, p.b_onesrow], writes=[bank])
            em.fence("tensor")
            em.op("vector", lambda e: e.tensor_copy(
                kc.rearrange("p (a b) -> p a b", a=4), p.psum_f[:, 0:4, :]),
                reads=[p.bank[i] for i in range(4)], writes=[b_kc], sig=True)
            em.op("vector", lambda e: e.tensor_copy(
                vc.rearrange("p (a b) -> p a b", a=4), p.psum_f[:, 4:8, :]),
                reads=[p.bank[4 + i] for i in range(4)], writes=[b_vc], sig=True)
            P0, P1, P2, P3, P4 = (PR[:, i, :] for i in range(5))
            B0, B1, B2, B3, B4 = bp
            if ci == 0:
                for b in bp:
                    b.merge_from([b_xs])
            em.op("vector", lambda e, ci=ci: e.scalar_tensor_tensor(
                P0, trep[:], nwpp[:, ci:ci + 1], kc[:], ALU.mult, ALU.add),
                reads=[b_trep, b_nwpp, b_kc], writes=[B0])
            em.op("vector", lambda e: e.tensor_scalar(
                P0, P0, 10.0, -20.0, ALU.min, ALU.max),
                reads=[B0], writes=[B0], sig=True)
            em.op("scalar", lambda e: e.activation(P1, P0, AF.Exp),
                  reads=[B0], writes=[B1], sig=True)
            em.op("vector", lambda e, ci=ci: e.tensor_scalar(
                P2, kc[:], Upp[:, ci:ci + 1], 10.0, ALU.add, ALU.min),
                reads=[b_kc, b_Upp], writes=[B2])
            em.op("vector", lambda e: e.tensor_scalar(
                P2, P2, -20.0, None, ALU.max),
                reads=[B2], writes=[B2], sig=True)
            em.op("scalar", lambda e: e.activation(P3, P2, AF.Exp),
                  reads=[B2], writes=[B3], sig=True)
            em.op("vector", lambda e: e.tensor_tensor(P2, P1, vc[:], ALU.mult),
                  reads=[B1, b_vc], writes=[B2], sig=True)
            em.op("vector", lambda e: e.tensor_tensor_scan(
                P4, P2, P2, 0.0, ALU.add, ALU.bypass),
                reads=[B2], writes=[B4], sig=True)
            em.op("vector", lambda e: e.tensor_tensor_scan(
                P0, P1, P1, 0.0, ALU.add, ALU.bypass),
                reads=[B1], writes=[B0], sig=True)
            em.op("vector", lambda e: e.tensor_tensor(P2, P3, vc[:], ALU.mult),
                  reads=[B3, b_vc], writes=[B2], sig=True)
            em.op("vector", lambda e: e.tensor_tensor(
                P1[:, 1:SEQ], P2[:, 1:SEQ], P4[:, 0:SEQ - 1], ALU.add),
                reads=[B2, B4], writes=[B1])
            em.op("vector", lambda e: e.tensor_copy(P1[:, 0:1], P2[:, 0:1]),
                  reads=[B2], writes=[B1], sig=True)
            em.op("vector", lambda e: e.tensor_tensor(
                P2[:, 1:SEQ], P3[:, 1:SEQ], P0[:, 0:SEQ - 1], ALU.add),
                reads=[B3, B0], writes=[B2])
            em.op("vector", lambda e: e.tensor_copy(P2[:, 0:1], P3[:, 0:1]),
                  reads=[B3], writes=[B2], sig=True)
            em.op("vector", lambda e: e.reciprocal(P3, P2),
                  reads=[B2], writes=[B3], sig=True)
            em.op("vector", lambda e: e.tensor_tensor(P0, P1, P3, ALU.mult),
                  reads=[B1, B3], writes=[B0], sig=True)
            em.op("vector", lambda e, ci=ci, sl=sl: e.tensor_tensor(
                zc[:, sl], P0, sg[:, sl], ALU.mult),
                reads=[B0, b_sg[sl]], writes=[b_zc[sl]], sig=True)
            em.op("sync", lambda s, ci=ci, sl=sl, t0=t0: s.dma_start(
                out=zT_d[:, ci, t0:t0 + SEQ], in_=zc[:, sl]),
                reads=[b_zc[sl]], writes=[b_zTd], dma_sem=f"d_zst{sl}")

        em.op("sync", lambda s: s.dma_start(out=WREG[:, 0], in_=wT_d[3]),
              reads=[b_wTd, b_W0], writes=[b_W0], dma_sem="d_wr0")
        em.op("sync", lambda s: s.dma_start(
            out=rows[0:1, 0], in_=p.dram["L0_attn_bo"][None, :]),
            reads=[b_rows], writes=[b_rows], dma_sem="ds_sy")
        tails = dict(
            ao=sb.overlay(pr_off, [128, 2, 2, 512], F32), b_ao=[Buf(), Buf()],
            sq=sb.overlay(pr_off + 8192, [128, 2, C], F32), b_sq=[Buf(), Buf()],
            sm=sb.overlay(pr_off + 16384, [128, 2, 8], F32), b_sm=[Buf(), Buf()],
            xo=sb.overlay(pr_off + 16448, [128, 2, C], F32), b_xo=[Buf(), Buf()],
        )
        for bb in tails["b_ao"] + tails["b_sq"] + tails["b_sm"] + tails["b_xo"]:
            bb.merge_from(bp)
        zt = sb.overlay(pr_off + 24640, [128, 2, 8, 128], BF16)
        b_zt = [Buf(), Buf()]
        for bb in b_zt:
            bb.merge_from(bp)
        for lt in range(16):
            tt = seq * 16 + lt
            sl = lt % 2
            em.op("sync", lambda s, lt=lt, sl=sl, t0=t0: s.dma_start(
                out=zt[:, sl], in_=zT_d[:, :, t0 + lt * 128:t0 + (lt + 1) * 128]),
                reads=[b_zTd], writes=[b_zt[sl]], dma_sem=f"d_zl{sl}")
            banks = [p.bank[2 * sl], p.bank[2 * sl + 1]]
            for h in range(2):
                for k in range(8):
                    em.op("tensor", lambda t, sl=sl, h=h, k=k: t.matmul(
                        p.psum_f[:, 2 * sl + h, :], zt[:, sl, k],
                        WREG[:, 0, k, h * 512:(h + 1) * 512], start=(k == 0), stop=False),
                        reads=[b_zt[sl], b_W0], writes=[banks[h]])
                em.op("tensor", lambda t, sl=sl, h=h: t.matmul(
                    p.psum_f[:, 2 * sl + h, :], p.onesrow[0:1, 0:128],
                    rows[0:1, 0, h * 512:(h + 1) * 512], start=False, stop=True),
                    reads=[p.b_onesrow, b_rows], writes=[banks[h]])
            em.fence("tensor")
            sandwich_tail_tile(
                p, tt, p.psum_f[:, 2 * sl:2 * sl + 2, :], banks,
                x_in, b_x, p.dram["d_res"], p.b_res, tails)
        em.play_block()
    sb.reset()



def ln_pass(p: P, src_dram, b_src, dst_dram, b_dst):
    em, sb = p.em, p.sb
    save = sb.off
    xin = sb.take([128, 2, C], F32)
    sq = sb.take([128, 2, C], F32)
    sm = sb.take([128, 2, 8], F32)
    b_xin = [Buf(), Buf()]; b_sq = [Buf(), Buf()]; b_sm = [Buf(), Buf()]
    for i in range(NT):
        sl = i % 2
        em.op("sync", lambda s, i=i, sl=sl: s.dma_start(
            out=xin[:, sl], in_=src_dram[i * 128:(i + 1) * 128, :]),
            reads=[b_src], writes=[b_xin[sl]], dma_sem=f"d_lp{sl}")
        emit_ln_tile(p, None, xin[:, sl], b_xin[sl], sq[:, sl], b_sq[sl],
                     sm[:, sl], b_sm[sl])
        em.op("vector", lambda e, sl=sl: e.tensor_scalar(
            xin[:, sl], xin[:, sl], sm[:, sl, 2:3], sm[:, sl, 6:7],
            ALU.subtract, ALU.mult),
            reads=[b_xin[sl], b_sm[sl]], writes=[b_xin[sl]], sig=True)
        em.op("sync", lambda s, i=i, sl=sl: s.dma_start(
            out=dst_dram[i * 128:(i + 1) * 128, :], in_=xin[:, sl]),
            reads=[b_xin[sl]], writes=[b_dst], dma_sem=f"d_lpo{sl}")
    em.play_block()
    sb.reset(save)


def ln_add_pass(p: P, f_dram, b_f, res_dram, b_res):
    em, sb = p.em, p.sb
    save = sb.off
    xin = sb.take([128, 2, C], F32)
    xo = sb.take([128, 2, C], F32)
    sq = sb.take([128, 2, C], F32)
    sm = sb.take([128, 2, 8], F32)
    b_xin = [Buf(), Buf()]; b_xo = [Buf(), Buf()]
    b_sq = [Buf(), Buf()]; b_sm = [Buf(), Buf()]
    for i in range(NT):
        sl = i % 2
        em.op("sync", lambda s, i=i, sl=sl: s.dma_start(
            out=xin[:, sl], in_=f_dram[i * 128:(i + 1) * 128, :]),
            reads=[b_f], writes=[b_xin[sl]], dma_sem=f"d_la{sl}")
        em.op("sync", lambda s, i=i, sl=sl: s.dma_start(
            out=xo[:, sl], in_=res_dram[i * 128:(i + 1) * 128, :]),
            reads=[b_res], writes=[b_xo[sl]], dma_sem=f"d_lb{sl}")
        emit_ln_tile(p, None, xin[:, sl], b_xin[sl], sq[:, sl], b_sq[sl],
                     sm[:, sl], b_sm[sl])
        em.op("vector", lambda e, sl=sl: e.tensor_scalar(
            xin[:, sl], xin[:, sl], sm[:, sl, 2:3], sm[:, sl, 6:7],
            ALU.subtract, ALU.mult),
            reads=[b_xin[sl], b_sm[sl]], writes=[b_xin[sl]])
        em.op("vector", lambda e, sl=sl: e.tensor_tensor(
            xin[:, sl], xin[:, sl], xo[:, sl], ALU.add),
            reads=[b_xin[sl], b_xo[sl]], writes=[b_xin[sl]], sig=True)
        em.op("sync", lambda s, i=i, sl=sl: s.dma_start(
            out=res_dram[i * 128:(i + 1) * 128, :], in_=xin[:, sl]),
            reads=[b_xin[sl]], writes=[b_res], dma_sem=f"d_lao{sl}")
    em.play_block()
    sb.reset(save)


def _seg(ap, h=2, u=32):
    return ap.rearrange("p (g h u) -> p g h u", h=h, u=u)


def phase2(p: P, dbg_stop=None):
    em, sb = p.em, p.sb
    HEADS, DH, WIN = 8, 64, 512
    NW = SEQ // WIN
    d_y = p.dram["d_y"]; b_y = Buf("d_y")
    xlnT_d = p.dram["d_xlnT"].rearrange("(k p) t -> p k t", p=128)

    ln_pass(p, p.dram["d_res"], p.b_res, d_y, b_y)

    for l in range(2):
        layer0 = sb.off
        wqkvb = sb.take([128, 8, 1536], BF16); b_wqkv = Buf("wqkv")
        wob = sb.take([128, 4, 1024], BF16); b_wo = Buf("wob")
        qsr = sb.take([128, 512], F32); b_qsr = Buf("qsr")
        ksr = sb.take([128, 512], F32); b_ksr = Buf("ksr")
        srow = sb.take([1, 2, 512], F32); b_srow = Buf("srow")
        mtri2 = sb.take([128, 4, 512], BF16); b_mtri2 = Buf("mtri2")
        ftri2 = sb.take([128, 512], BF16); b_ftri2 = Buf("ftri2")
        idneg2 = sb.take([128, 128], BF16); b_idneg2 = Buf("idneg2")
        prep_end = sb.off

        em.op("gpsimd", lambda g, l=l: g.dma_start(
            out=wqkvb[:], in_=p.dram["L0_ffn_wqkv"][l].rearrange("(k p) n -> p k n", p=128)),
            reads=[b_wqkv], writes=[b_wqkv], dma_sem="ds_gp")
        em.op("gpsimd", lambda g, l=l: g.dma_start(
            out=wob[:], in_=p.dram["L0_ffn_wo"][l].rearrange("(k p) n -> p k n", p=128)),
            reads=[b_wo], writes=[b_wo], dma_sem="ds_gp")
        em.op("gpsimd", lambda g: g.dma_start(
            out=mtri2[:], in_=p.dram["c_mtri"].rearrange("o q t -> q o t")),
            reads=[b_mtri2], writes=[b_mtri2], dma_sem="ds_gp")
        em.op("gpsimd", lambda g: g.dma_start(
            out=ftri2[:], in_=p.dram["c_ftri"][:]),
            reads=[b_ftri2], writes=[b_ftri2], dma_sem="ds_gp")
        em.op("vector", lambda e: e.tensor_scalar(
            idneg2[:], p.identb[:], -1e30, None, ALU.mult),
            reads=[p.b_identb, b_idneg2], writes=[b_idneg2], sig=True)
        for s8 in range(8):
            em.op("sync", lambda s, l=l, s8=s8: s.dma_start(
                out=srow[0:1, 0, s8 * 64:(s8 + 1) * 64],
                in_=p.dram["L0_ffn_q_scale"][l][None, :]),
                reads=[b_srow], writes=[b_srow], dma_sem="ds_sy")
            em.op("sync", lambda s, l=l, s8=s8: s.dma_start(
                out=srow[0:1, 1, s8 * 64:(s8 + 1) * 64],
                in_=p.dram["L0_ffn_k_scale"][l][None, :]),
                reads=[b_srow], writes=[b_srow], dma_sem="ds_sy")
        em.op("vector", lambda e: e.tensor_scalar(
            srow[0:1, 0], srow[0:1, 0], 8.0, None, ALU.mult),
            reads=[b_srow], writes=[b_srow], sig=True)
        for (dst, b_dst, r) in [(qsr, b_qsr, 0), (ksr, b_ksr, 1)]:
            em.op("tensor", lambda t, r=r: t.matmul(
                p.psum_f[:, 6, :], p.onesrow[0:1, 0:128], srow[0:1, r],
                start=True, stop=True),
                reads=[b_srow, p.b_onesrow], writes=[p.bank[6]], sig=True)
            em.op("vector", lambda e, dst=dst: e.tensor_copy(dst[:], p.psum_f[:, 6, :]),
                  reads=[p.bank[6]], writes=[b_dst], sig=True)
        em.play_block()

        ln_transpose_store(p, d_y, b_y, ln=False, shift=True)

        for seq in range(NSEQ):
            sb.reset(prep_end)
            t0 = seq * SEQ
            hT = sb.take([128, 8, 2, 128], BF16); b_hT = [Buf(), Buf()]
            rot = sb.take([128, 2, 4, 512], F32); b_rot = [Buf(), Buf()]
            qf = sb.take([128, 2, 512], F32); b_qf = [Buf(), Buf()]
            kf = sb.take([128, 2, 512], F32); b_kf = [Buf(), Buf()]
            wA = sb.take([128, 2, 512], F32); b_wA = [Buf(), Buf()]
            wB = sb.take([128, 2, 512], F32); b_wB = [Buf(), Buf()]
            wC = sb.take([128, 2, 512], F32); b_wC = [Buf(), Buf()]
            n2 = sb.take([128, 2, 2, 8], F32); b_n2 = [Buf(), Buf()]
            xbt = sb.take([128, 2, 2, 512], BF16); b_xbt = [Buf(), Buf()]
            qTt = sb.take([128, 4, SEQ], BF16); b_qT = Buf("qTt")
            kTt = sb.take([128, 4, SEQ], BF16); b_kT = Buf("kTt")
            Vag = sb.take([128, 16, 8, 65], BF16); b_V = Buf("Vag")
            stA_end = sb.off

            em.op("vector", lambda e: e.memset(Vag[:, :, :, 64:65], 1.0),
                  writes=[b_V], sig=True)

            for lt in range(16):
                sl = lt % 2
                tg = t0 + lt * 128
                banksA = [p.bank[3 * sl + j] for j in range(3)]
                em.op("sync", lambda s, sl=sl, tg=tg: s.dma_start(
                    out=hT[:, :, sl, :], in_=xlnT_d[:, :, tg:tg + 128]),
                    reads=[p.b_xlnTd], writes=[b_hT[sl]], dma_sem=f"d_h{sl}")
                em.op("sync", lambda s, sl=sl, lt=lt: s.dma_start(
                    out=rot[:, sl], in_=p.dram["c_rot"][:, lt * 128 + (0 if True else 0):(lt + 1) * 128, :].rearrange("j t n -> t j n"),
                    allow_slow_non_contiguous=True),
                    reads=[b_rot[sl]], writes=[b_rot[sl]], dma_sem=f"d_r{sl}")
                for n3 in range(3):
                    for k in range(8):
                        em.op("tensor", lambda t, sl=sl, n3=n3, k=k: t.matmul(
                            p.psum_f[:, 3 * sl + n3, :], hT[:, k, sl, :],
                            wqkvb[:, k, n3 * 512:(n3 + 1) * 512],
                            start=(k == 0), stop=(k == 7)),
                            reads=[b_hT[sl], b_wqkv], writes=[banksA[n3]])
                em.fence("tensor")
                em.op("vector", lambda e, sl=sl: e.tensor_copy(qf[:, sl], p.psum_f[:, 3 * sl, :]),
                      reads=[banksA[0]], writes=[b_qf[sl]], sig=True)
                em.op("vector", lambda e, sl=sl: e.tensor_copy(kf[:, sl], p.psum_f[:, 3 * sl + 1, :]),
                      reads=[banksA[1]], writes=[b_kf[sl]], sig=True)
                em.op("vector", lambda e, sl=sl, lt=lt: e.tensor_copy(
                    Vag[:, lt, :, 0:64],
                    p.psum_f[:, 3 * sl + 2, :].rearrange("p (h d) -> p h d", h=8)),
                    reads=[banksA[2]], writes=[b_V], sig=True)
                for (xf, b_xf, scl, b_scl, ri, ns) in [
                        (qf, b_qf, qsr, b_qsr, 0, 0),
                        (kf, b_kf, ksr, b_ksr, 2, 1)]:
                    em.op("gpsimd", lambda e, sl=sl, xf=xf: e.tensor_tensor(
                        wA[:, sl], xf[:, sl], xf[:, sl], ALU.mult),
                        reads=[b_xf[sl]], writes=[b_wA[sl]], sig=True)
                    em.op("vector", lambda e, sl=sl, ns=ns: e.tensor_reduce(
                        n2[:, sl, ns], wA[:, sl].rearrange("p (g u) -> p g u", u=64),
                        axis=AXX, op=ALU.add),
                        reads=[b_wA[sl]], writes=[b_n2[sl]], sig=True)
                    em.op("scalar", lambda e, sl=sl, ns=ns: e.activation(
                        n2[:, sl, ns], n2[:, sl, ns], AF.Sqrt),
                        reads=[b_n2[sl]], writes=[b_n2[sl]], sig=True)
                    em.op("vector", lambda e, sl=sl, ns=ns: e.tensor_scalar(
                        n2[:, sl, ns], n2[:, sl, ns], 1e-12, None, ALU.max),
                        reads=[b_n2[sl]], writes=[b_n2[sl]])
                    em.op("vector", lambda e, sl=sl, ns=ns: e.reciprocal(
                        n2[:, sl, ns], n2[:, sl, ns]),
                        reads=[b_n2[sl]], writes=[b_n2[sl]], sig=True)
                    em.op("vector", lambda e, sl=sl, xf=xf, ns=ns: e.tensor_tensor(
                        wA[:, sl].rearrange("p (g u) -> p g u", u=64),
                        xf[:, sl].rearrange("p (g u) -> p g u", u=64),
                        n2[:, sl, ns].rearrange("p (g u) -> p g u", u=1).broadcast_to([128, 8, 64]),
                        ALU.mult),
                        reads=[b_xf[sl], b_n2[sl]], writes=[b_wA[sl]], sig=True)
                    em.op("vector", lambda e, sl=sl, scl=scl: e.tensor_tensor(
                        wA[:, sl], wA[:, sl], scl[:], ALU.mult),
                        reads=[b_wA[sl], b_scl], writes=[b_wA[sl]], sig=True)
                    em.op("vector", lambda e, sl=sl, ri=ri: e.tensor_tensor(
                        wB[:, sl], wA[:, sl], rot[:, sl, ri, :], ALU.mult),
                        reads=[b_wA[sl], b_rot[sl]], writes=[b_wB[sl]], sig=True)
                    em.op("vector", lambda e, sl=sl, ri=ri: e.tensor_tensor(
                        _seg(wC[:, sl])[:, :, 0, :], _seg(wA[:, sl])[:, :, 1, :],
                        _seg(rot[:, sl, ri + 1, :])[:, :, 0, :], ALU.mult),
                        reads=[b_wA[sl], b_rot[sl]], writes=[b_wC[sl]])
                    em.op("vector", lambda e, sl=sl, ri=ri: e.tensor_tensor(
                        _seg(wC[:, sl])[:, :, 1, :], _seg(wA[:, sl])[:, :, 0, :],
                        _seg(rot[:, sl, ri + 1, :])[:, :, 1, :], ALU.mult),
                        reads=[b_wA[sl], b_rot[sl]], writes=[b_wC[sl]], sig=True)
                    em.op("vector", lambda e, sl=sl, ns=ns: e.tensor_tensor(
                        xbt[:, sl, ns], wB[:, sl], wC[:, sl], ALU.add),
                        reads=[b_wB[sl], b_wC[sl]], writes=[b_xbt[sl]], sig=True)
                bt = p.bank[6 + sl % 2] if False else p.bank[6]
                for (ns, dstT, b_dstT) in [(0, qTt, b_qT), (1, kTt, b_kT)]:
                    bankT = p.bank[6 + ns]
                    for j in range(4):
                        em.op("tensor", lambda t, sl=sl, ns=ns, j=j: t.matmul(
                            p.psum_f[:, 6 + ns, j * 128:(j + 1) * 128],
                            xbt[:, sl, ns, j * 128:(j + 1) * 128], p.identb[:],
                            start=True, stop=True),
                            reads=[b_xbt[sl], p.b_identb], writes=[bankT])
                    em.fence("tensor")
                    em.op("vector", lambda e, ns=ns, lt=lt, dstT=dstT: e.tensor_copy(
                        dstT[:, :, lt * 128:(lt + 1) * 128],
                        p.psum_f[:, 6 + ns, :].rearrange("p (j f) -> p j f", j=4)),
                        reads=[bankT], writes=[b_dstT], sig=True)

            PT = sb.take([128, 6, 512], BF16)
            b_PTs = [Buf(f"PT2{i}") for i in range(6)]
            rr = sb.take([128, 2, 512], F32); b_rr = Buf("rr")
            rrep = sb.take([128, 2, 512], F32); b_rrep = Buf("rrep")
            mx = sb.take([128, 8], F32); b_mx = Buf("mx")
            oTall = sb.take([128, 4, SEQ], BF16); b_oT = Buf("oTall")
            for kt in range(4):
                for w in range(NW):
                    jlist = list(range(max(0, 4 * w - 4), 4 * w + 4))
                    nst = len(jlist)
                    s0 = jlist[0] * 128
                    ns_ = nst * 128
                    nch = (ns_ + 511) // 512
                    for tt4 in range(4):
                        base = 4 * (tt4 % 2)
                        for r in range(2):
                            pb = 64 * r
                            for sc in range(nch):
                                nsc = min(512, ns_ - sc * 512)
                                bk = base + r * nch + sc
                                em.op("tensor", lambda t, w=w, tt4=tt4, sc=sc, nsc=nsc, pb=pb, kt=kt, s0=s0, bk=bk: t.matmul(
                                    p.psum_f[:, bk, 0:nsc],
                                    qTt[pb:pb + 64, kt, w * 512 + tt4 * 128:w * 512 + (tt4 + 1) * 128],
                                    kTt[pb:pb + 64, kt, s0 + sc * 512:s0 + sc * 512 + nsc],
                                    start=True, stop=True),
                                    reads=[b_qT, b_kT], writes=[p.bank[bk]],
                                    sig=(r == 1 and sc == nch - 1))
                        em.op("vector", lambda e, tt4=tt4, base=base, nch=nch: e.reduce_max(
                            mx[:, tt4:tt4 + 1],
                            p.psum_f[:, base:base + 2 * nch, :].rearrange("p a b -> p (a b)"),
                            axis=AXX),
                            reads=[p.bank[base + j] for j in range(2 * nch)],
                            writes=[b_mx], sig=True)
                    em.op("vector", lambda e: e.reduce_max(
                        mx[:, 4:5], mx[:, 0:4], axis=AXX),
                        reads=[b_mx], writes=[b_mx], sig=True)
                    em.op("tensor", lambda t: t.matmul(
                        p.psum_f[0:1, 5, 0:128], mx[:, 4:5], p.identf[:],
                        start=True, stop=True),
                        reads=[b_mx, p.b_identf], writes=[p.bank[5]], sig=True)
                    em.op("vector", lambda e: e.reduce_max(
                        mx[0:1, 5:6], p.psum_f[0:1, 5, 0:128], axis=AXX),
                        reads=[p.bank[5]], writes=[b_mx])
                    em.op("vector", lambda e: e.tensor_scalar(
                        mx[0:1, 6:7], mx[0:1, 5:6], -1.0, None, ALU.mult),
                        reads=[b_mx], writes=[b_mx], sig=True)
                    em.op("tensor", lambda t: t.matmul(
                        p.psum_f[:, 5, 0:1], p.one128[0:1, :], mx[0:1, 6:7],
                        start=True, stop=True),
                        reads=[b_mx, p.b_one128], writes=[p.bank[5]], sig=True)
                    em.op("vector", lambda e: e.tensor_copy(
                        mx[:, 7:8], p.psum_f[:, 5, 0:1]),
                        reads=[p.bank[5]], writes=[b_mx], sig=True)
                    nb = (nst + 2) // 3

                    def jr2(b_):
                        return range(3 * b_, min(3 * b_ + 3, nst))

                    for b_ in range(nb):
                        for idx in jr2(b_):
                            j = jlist[idx]
                            o = j - 4 * w
                            masked = (o >= 0) or (o == -4)
                            for r in range(2):
                                pb = 64 * r
                                sbk = 3 * r + idx % 3
                                em.op("tensor", lambda t, w=w, j=j, pb=pb, kt=kt, sbk=sbk, masked=masked: t.matmul(
                                    p.psum_f[:, sbk, :],
                                    kTt[pb:pb + 64, kt, j * 128:(j + 1) * 128],
                                    qTt[pb:pb + 64, kt, w * 512:(w + 1) * 512],
                                    start=True, stop=not masked),
                                    reads=[b_kT, b_qT], writes=[p.bank[sbk]],
                                    sig=not masked)
                        for idx in jr2(b_):
                            j = jlist[idx]
                            o = j - 4 * w
                            for r in range(2):
                                sbk = 3 * r + idx % 3
                                if o >= 0:
                                    em.op("tensor", lambda t, sbk=sbk, o=o: t.matmul(
                                        p.psum_f[:, sbk, :], idneg2[:], mtri2[:, o, :],
                                        start=False, stop=True),
                                        reads=[b_idneg2, b_mtri2], writes=[p.bank[sbk]],
                                        sig=True)
                                elif o == -4:
                                    em.op("tensor", lambda t, sbk=sbk: t.matmul(
                                        p.psum_f[:, sbk, :], idneg2[:], ftri2[:],
                                        start=False, stop=True),
                                        reads=[b_idneg2, b_ftri2], writes=[p.bank[sbk]],
                                        sig=True)
                        if b_ > 0:
                            for idx in jr2(b_ - 1):
                                j = jlist[idx]
                                for r in range(2):
                                    h = 2 * kt + r
                                    sl = 3 * r + idx % 3
                                    em.op("tensor", lambda t, j=j, h=h, r=r, sl=sl, idx=idx, n=nst: t.matmul(
                                        p.psum_f[0:65, 6 + r, :], Vag[:, j, h, :],
                                        PT[:, sl, :],
                                        start=(idx == 0), stop=(idx == n - 1)),
                                        reads=[b_V, b_PTs[sl]], writes=[p.bank[6 + r]],
                                        sig=(r == 1 and idx == 3 * b_ - 1))
                        for idx in jr2(b_):
                            for r in range(2):
                                sbk = 3 * r + idx % 3
                                em.op("scalar", lambda e, sbk=sbk: e.activation(
                                    PT[:, sbk, :], p.psum_f[:, sbk, :], AF.Exp,
                                    bias=mx[:, 7:8]),
                                    reads=[p.bank[sbk], b_mx], writes=[b_PTs[sbk]],
                                    sig=True)
                    for idx in jr2(nb - 1):
                        j = jlist[idx]
                        for r in range(2):
                            h = 2 * kt + r
                            sl = 3 * r + idx % 3
                            em.op("tensor", lambda t, j=j, h=h, r=r, sl=sl, idx=idx, n=nst: t.matmul(
                                p.psum_f[0:65, 6 + r, :], Vag[:, j, h, :],
                                PT[:, sl, :],
                                start=(idx == 0), stop=(idx == n - 1)),
                                reads=[b_V, b_PTs[sl]], writes=[p.bank[6 + r]],
                                sig=(r == 1 and idx == nst - 1))
                    for r in range(2):
                        pb = 64 * r
                        em.op("vector", lambda e, r=r: e.reciprocal(
                            rr[64:65, r, :], p.psum_f[64:65, 6 + r, :]),
                            reads=[p.bank[6 + r]], writes=[b_rr], sig=True)
                        bb = p.bank[3 * r]
                        em.op("tensor", lambda t, r=r: t.matmul(
                            p.psum_f[:, 3 * r, :], p.one128[64:65, :], rr[64:65, r, :],
                            start=True, stop=True),
                            reads=[b_rr, p.b_one128], writes=[bb], sig=True)
                        em.op("vector", lambda e, r=r: e.tensor_copy(
                            rrep[:, r, :], p.psum_f[:, 3 * r, :]),
                            reads=[bb], writes=[b_rrep], sig=True)
                        em.op("vector", lambda e, r=r, kt=kt, w=w, pb=pb: e.tensor_tensor(
                            oTall[pb:pb + 64, kt, w * 512:(w + 1) * 512],
                            p.psum_f[0:64, 6 + r, :], rrep[0:64, r, :], ALU.mult),
                            reads=[p.bank[6 + r], b_rrep], writes=[b_oT], sig=True)

            yup = sb.take([128, 2, C], F32); b_yup = [Buf(), Buf()]
            for lt in range(16):
                sl = lt % 2
                tg = t0 + lt * 128
                banks = [p.bank[2 * sl], p.bank[2 * sl + 1]]
                em.op("sync", lambda s, sl=sl, tg=tg: s.dma_start(
                    out=yup[:, sl], in_=d_y[tg:tg + 128, :]),
                    reads=[b_y], writes=[b_yup[sl]], dma_sem=f"d_yu{sl}")
                for hh in range(2):
                    for k in range(4):
                        em.op("tensor", lambda t, sl=sl, hh=hh, k=k, lt=lt: t.matmul(
                            p.psum_f[:, 2 * sl + hh, :],
                            oTall[:, k, lt * 128:(lt + 1) * 128],
                            wob[:, k, hh * 512:(hh + 1) * 512],
                            start=(k == 0), stop=(k == 3)),
                            reads=[b_oT, b_wo], writes=[banks[hh]])
                em.fence("tensor")
                em.op("vector", lambda e, sl=sl: e.tensor_tensor(
                    yup[:, sl].rearrange("p (a b) -> p a b", a=2),
                    yup[:, sl].rearrange("p (a b) -> p a b", a=2),
                    p.psum_f[:, 2 * sl:2 * sl + 2, :], ALU.add),
                    reads=[b_yup[sl]] + banks, writes=[b_yup[sl]], sig=True)
                em.op("sync", lambda s, sl=sl, tg=tg: s.dma_start(
                    out=d_y[tg:tg + 128, :], in_=yup[:, sl]),
                    reads=[b_yup[sl]], writes=[b_y], dma_sem=f"d_yo{sl}")
            em.play_block()
        if dbg_stop == "l0_attn":
            sb.reset()
            return

        sb.reset(layer0)
        ln_transpose_store(p, d_y, b_y, ln=True, shift=True)

        HID = 2730
        NCH = 22
        w1b = sb.take([128, 8, 5460], BF16); b_w1b = Buf("w1b")
        w2b = sb.take([128, 22, 1024], BF16); b_w2b = Buf("w2b")
        h2T = sb.take([128, 22, 512], BF16); b_h2T = Buf("h2T")
        hTc = sb.take([128, 8, 2, 512], BF16); b_hTc = [Buf(), Buf()]
        ga = sb.take([128, 2, 512], F32); b_ga = [Buf(), Buf()]
        b1pp = sb.take([128, 44], F32); b_b1pp = Buf("b1pp")
        lgpp = sb.take([128, 8], F32); b_lgpp = Buf("lgpp")
        lbpp = sb.take([128, 8], BF16); b_lbpp = Buf("lbpp")
        b2row = sb.take([1, 1024], F32); b_b2row = Buf("b2row")
        yffn = sb.take([128, 2, C], F32); b_yffn = [Buf(), Buf()]

        em.op("gpsimd", lambda g, l=l: g.dma_start(
            out=w1b[:], in_=p.dram["L0_ffn_w1"][l].rearrange("(k p) n -> p k n", p=128)),
            writes=[b_w1b], dma_sem="ds_gp")
        em.op("gpsimd", lambda g, l=l: g.dma_start(
            out=w2b[:, 0:21, :],
            in_=p.dram["L0_ffn_w2"][l][0:2688].rearrange("(k p) n -> p k n", p=128)),
            writes=[b_w2b], dma_sem="ds_gp")
        em.op("gpsimd", lambda g, l=l: g.dma_start(
            out=w2b[0:42, 21, :], in_=p.dram["L0_ffn_w2"][l][2688:2730]),
            writes=[b_w2b], dma_sem="ds_gp")
        em.op("sync", lambda s, l=l: s.dma_start(
            out=lgpp[:], in_=p.dram["L0_ffn_ln_g"][l].rearrange("(k p) -> p k", p=128),
            allow_slow_non_contiguous=True),
            writes=[b_lgpp], dma_sem="ds_sy")
        em.op("gpsimd", lambda g, l=l: g.dma_start(
            out=lbpp[:], in_=p.dram["L0_ffn_ln_b"][l].rearrange("(k p) -> p k", p=128),
            allow_slow_non_contiguous=True),
            writes=[b_lbpp], dma_sem="ds_gp")
        em.op("sync", lambda s, l=l: s.dma_start(
            out=b2row[:], in_=p.dram["L0_ffn_b2"][l][None, :]),
            writes=[b_b2row], dma_sem="ds_sy")
        em.op("vector", lambda e: e.memset(b1pp[:], 0.0), writes=[b_b1pp], sig=True)
        for m in range(44):
            half, mm = divmod(m, 22)
            c0 = half * HID + mm * 128
            n = min(128, half * HID + HID - c0)
            em.op("sync", lambda s, l=l, m=m, c0=c0, n=n: s.dma_start(
                out=b1pp[0:n, m:m + 1], in_=p.dram["L0_ffn_b1"][l][c0:c0 + n][:, None],
                allow_slow_non_contiguous=True),
                reads=[b_b1pp], writes=[b_b1pp], dma_sem="ds_sy")
        for m in range(44):
            half, mm = divmod(m, 22)
            c0 = half * HID + mm * 128
            n = min(128, half * HID + HID - c0)
            for k in range(8):
                em.op("tensor", lambda t, m=m, k=k, c0=c0, n=n: t.matmul(
                    p.psum_f[0:n, 7, 0:1], w1b[:, k, c0:c0 + n], lbpp[:, k:k + 1],
                    start=(k == 0), stop=(k == 7)),
                    reads=[b_w1b, b_lbpp], writes=[p.bank[7]], sig=(k == 7))
            em.op("vector", lambda e, m=m, n=n: e.tensor_tensor(
                b1pp[0:n, m:m + 1], b1pp[0:n, m:m + 1], p.psum_f[0:n, 7, 0:1], ALU.add),
                reads=[p.bank[7], b_b1pp], writes=[b_b1pp], sig=True)
        for k in range(8):
            em.op("vector", lambda e, k=k: e.tensor_scalar(
                w1b[:, k, :], w1b[:, k, :], lgpp[:, k:k + 1], None, ALU.mult),
                reads=[b_w1b, b_lgpp], writes=[b_w1b], sig=True)

        for tc in range(8):
            sl = tc % 2
            tg = tc * 512
            em.op("sync", lambda s, sl=sl, tg=tg: s.dma_start(
                out=hTc[:, :, sl, :], in_=xlnT_d[:, :, tg:tg + 512]),
                reads=[p.b_xlnTd], writes=[b_hTc[sl]], dma_sem=f"d_hc{sl}")
            for mm in range(NCH):
                n = 128 if mm < 21 else 42
                cp = mm % 2
                bka, bkg = p.bank[2 * cp], p.bank[2 * cp + 1]
                for k in range(8):
                    em.op("tensor", lambda t, cp=cp, mm=mm, k=k, n=n, sl=sl: t.matmul(
                        p.psum_f[0:n, 2 * cp, :], w1b[:, k, mm * 128:mm * 128 + n],
                        hTc[:, k, sl, :], start=(k == 0), stop=(k == 7)),
                        reads=[b_w1b, b_hTc[sl]], writes=[bka])
                for k in range(8):
                    em.op("tensor", lambda t, cp=cp, mm=mm, k=k, n=n, sl=sl: t.matmul(
                        p.psum_f[0:n, 2 * cp + 1, :],
                        w1b[:, k, HID + mm * 128:HID + mm * 128 + n],
                        hTc[:, k, sl, :], start=(k == 0), stop=(k == 7)),
                        reads=[b_w1b, b_hTc[sl]], writes=[bkg], sig=(k == 7))
                em.op("scalar", lambda e, cp=cp, mm=mm, n=n, sl=sl: e.activation(
                    ga[:, cp], p.psum_f[:, 2 * cp + 1, :], AF.Gelu,
                    bias=b1pp[:, 22 + mm:23 + mm]),
                    reads=[bkg, b_b1pp], writes=[b_ga[cp]], sig=True)
                em.op("vector", lambda e, cp=cp, mm=mm, n=n, sl=sl: e.scalar_tensor_tensor(
                    h2T[0:n, mm, :], p.psum_f[0:n, 2 * cp, :], b1pp[0:n, mm:mm + 1],
                    ga[0:n, cp], ALU.add, ALU.mult),
                    reads=[bka, b_b1pp, b_ga[cp]], writes=[b_h2T], sig=True)
            for lt4 in range(4):
                tg2 = tg + lt4 * 128
                sl2 = lt4 % 2
                banks = [p.bank[4 + 2 * sl2], p.bank[5 + 2 * sl2]]
                em.op("sync", lambda s, sl2=sl2, tg2=tg2: s.dma_start(
                    out=yffn[:, sl2], in_=d_y[tg2:tg2 + 128, :]),
                    reads=[b_y], writes=[b_yffn[sl2]], dma_sem=f"d_yf{sl2}")
                for hh in range(2):
                    for kk in range(NCH):
                        n = 128 if kk < 21 else 42
                        em.op("tensor", lambda t, sl2=sl2, hh=hh, kk=kk, lt4=lt4, n=n: t.matmul(
                            p.psum_f[:, 4 + 2 * sl2 + hh, :],
                            h2T[0:n, kk, lt4 * 128:(lt4 + 1) * 128],
                            w2b[0:n, kk, hh * 512:(hh + 1) * 512],
                            start=(kk == 0), stop=False),
                            reads=[b_h2T, b_w2b], writes=[banks[hh]])
                    em.op("tensor", lambda t, sl2=sl2, hh=hh: t.matmul(
                        p.psum_f[:, 4 + 2 * sl2 + hh, :], p.onesrow[0:1, 0:128],
                        b2row[0:1, hh * 512:(hh + 1) * 512], start=False, stop=True),
                        reads=[p.b_onesrow, b_b2row], writes=[banks[hh]],
                        sig=(hh == 1))
                em.op("vector", lambda e, sl2=sl2: e.tensor_tensor(
                    yffn[:, sl2].rearrange("p (a b) -> p a b", a=2),
                    yffn[:, sl2].rearrange("p (a b) -> p a b", a=2),
                    p.psum_f[:, 4 + 2 * sl2:6 + 2 * sl2, :], ALU.add),
                    reads=[b_yffn[sl2]] + banks, writes=[b_yffn[sl2]], sig=True)
                em.op("sync", lambda s, sl2=sl2, tg2=tg2: s.dma_start(
                    out=d_y[tg2:tg2 + 128, :], in_=yffn[:, sl2]),
                    reads=[b_yffn[sl2]], writes=[b_y], dma_sem=f"d_yfo{sl2}")
        em.play_block()
        sb.reset(layer0)
        if dbg_stop == "l0_ffn":
            sb.reset()
            return

    ln_add_pass(p, d_y, b_y, p.dram["d_res"], p.b_res)


def phase3(p: P):
    em, sb = p.em, p.sb
    H3, DH = 16, 64
    d_res = p.dram["d_res"]
    xlnT_d = p.dram["d_xlnT"].rearrange("(k p) t -> p k t", p=128)

    ln_transpose_store(p, d_res, p.b_res, ln=True, shift=False)

    wab = sb.take([128, 8, 3072], BF16); b_wab = Buf("wab")
    wpb = sb.take([128, 8, 1024], BF16); b_wpb = Buf("wpb")
    bapp = sb.take([128, 24], F32); b_bapp = Buf("bapp")
    bprow = sb.take([1, 1024], F32); b_bprow = Buf("bprow")
    bvrep = sb.take([128, 1024], F32); b_bvrep = Buf("bvrep")
    alpp = sb.take([128, 16, 16], F32); b_alpp = Buf("alpp")
    mtri = sb.take([128, 4, 512], BF16); b_mtri = Buf("mtri3")
    idneg = sb.take([128, 128], BF16); b_idneg = Buf("idneg3")
    onesb = sb.take([128, 128], BF16); b_onesb = Buf("onesb3")
    kT = sb.take([128, 8, SEQ], BF16); b_kT = Buf("kT3")
    Vag = sb.take([128, 16, 16, 65], BF16); b_V = Buf("Vag3")
    qTc = sb.take([128, 8, 512], BF16); b_qTc = Buf("qTc")
    oTc = sb.take([128, 8, 512], BF16); b_oTc = Buf("oTc")
    hTc = sb.take([128, 8, 512], BF16); b_hTc = Buf("hTc3")
    ind2 = sb.take([128, 2], BF16); b_ind2 = Buf("ind2")
    knsq = sb.take([1, 2, 8], F32); b_knsq = Buf("knsq")
    knrow = sb.take([1, 2, 8], F32); b_knrow = Buf("knrow")
    b_cnqs = Buf("cnqs")
    qcsq = sb.take([1, 2, 8], F32); b_qcsq = Buf("qcsq")
    cnq = sb.take([1, 2, 8], F32); b_cnq = Buf("cnq")
    mrowZ0 = sb.take([128, 512], BF16)
    mrowZ1 = sb.take([128, 512], BF16)
    b_mz = [Buf("mz0"), Buf("mz1")]
    ov0 = sb.off
    PT = sb.take([128, 6, 512], BF16)
    srow = sb.take([128, 512], F32)
    cnqs = sb.take([128, 8], F32)
    rrep = sb.take([128, 2, 512], F32)
    sqb = sb.take([128, 512], BF16)
    tails = dict(
        ao=sb.overlay(ov0, [128, 1, 2, 512], F32),
        sq=sb.overlay(ov0 + 4096, [128, 1, C], F32),
        sm=sb.overlay(ov0 + 8192, [128, 1, 8], F32),
        xo=sb.overlay(ov0 + 8256, [128, 1, C], F32),
    )

    em.op("gpsimd", lambda g: g.dma_start(
        out=wab[:], in_=p.dram["L1_attn_w_attn"].rearrange("(k p) n -> p k n", p=128)),
        writes=[b_wab], dma_sem="ds_gp")
    em.op("gpsimd", lambda g: g.dma_start(
        out=wpb[:], in_=p.dram["L1_attn_w_proj"].rearrange("(k p) n -> p k n", p=128)),
        writes=[b_wpb], dma_sem="ds_gp")
    em.op("sync", lambda s: s.dma_start(
        out=bapp[:], in_=p.dram["L1_attn_b_attn"].rearrange("(k p) -> p k", p=128),
        allow_slow_non_contiguous=True),
        writes=[b_bapp], dma_sem="ds_sy")
    em.op("sync", lambda s: s.dma_start(
        out=bprow[:], in_=p.dram["L1_attn_b_proj"][None, :]),
        writes=[b_bprow], dma_sem="ds_sy")
    em.op("sync", lambda s: s.dma_start(
        out=bvrep[:], in_=p.dram["L1_attn_b_attn"][None, 2048:3072]
        .partition_broadcast(128)[:, 0, :]),
        writes=[b_bvrep], dma_sem="ds_sy")
    em.op("sync", lambda s: s.dma_start(
        out=alpp[:], in_=p.dram["c_slopepos"].rearrange("h (j q) -> q h j", q=128),
        allow_slow_non_contiguous=True),
        writes=[b_alpp], dma_sem="ds_sy")
    em.op("gpsimd", lambda g: g.dma_start(
        out=mtri[:], in_=p.dram["c_mtri"].rearrange("o q t -> q o t")),
        writes=[b_mtri], dma_sem="ds_gp")
    em.op("vector", lambda e: e.tensor_scalar(
        idneg[:], p.identb[:], -1e30, None, ALU.mult),
        reads=[p.b_identb], writes=[b_idneg], sig=True)
    em.op("vector", lambda e: e.memset(onesb[:], 1.0),
        writes=[b_onesb], sig=True)
    em.op("vector", lambda e: e.memset(mrowZ0[:], 0.0), writes=[b_mz[0]])
    em.op("vector", lambda e: e.memset(mrowZ1[:], 0.0), writes=[b_mz[1]], sig=True)
    em.op("vector", lambda e: e.tensor_scalar(
        bapp[:, 0:8], bapp[:, 0:8], 0.125, None, ALU.mult),
        reads=[b_bapp], writes=[b_bapp], sig=True)
    em.op("vector", lambda e: e.memset(ind2[0:64, 0:1], 1.0),
          writes=[b_ind2])
    em.op("vector", lambda e: e.memset(ind2[64:128, 0:1], 0.0),
          writes=[b_ind2])
    em.op("vector", lambda e: e.memset(ind2[0:64, 1:2], 0.0),
          writes=[b_ind2])
    em.op("vector", lambda e: e.memset(ind2[64:128, 1:2], 1.0),
          writes=[b_ind2], sig=True)

    for seq in range(NSEQ):
        t0 = seq * SEQ
        em.op("vector", lambda e: e.memset(Vag[:, :, :, 64:65], 1.0),
              writes=[b_V], sig=True)
        em.op("vector", lambda e: e.memset(knsq[:], 0.0),
              writes=[b_knsq], sig=True)
        b_tails_prev = None
        for tc in range(4):
            tg = t0 + tc * 512
            b_PT = Buf("PT3"); b_mrep = Buf("mrep3"); b_mcol = Buf("mcol3")
            b_mrow = Buf("mrow3"); b_srow = Buf("srow3")
            b_rr = Buf("rr3"); b_rrep = Buf("rrep3")
            if b_tails_prev is not None:
                for bb in (b_PT, b_mrep, b_mcol, b_mrow, b_srow, b_rr, b_rrep):
                    bb.merge_from(b_tails_prev)

            em.op("sync", lambda s, tg=tg: s.dma_start(
                out=hTc[:], in_=xlnT_d[:, :, tg:tg + 512]),
                reads=[p.b_xlnTd, b_qTc, b_kT], writes=[b_hTc], dma_sem="d_h3")
            for m in range(16):
                bank = p.bank[m % 4]
                for k in range(8):
                    em.op("tensor", lambda t, m=m, k=k: t.matmul(
                        p.psum_f[:, m % 4, :], wab[:, k, m * 128:(m + 1) * 128],
                        hTc[:, k, :], start=(k == 0), stop=(k == 7)),
                        reads=[b_wab, b_hTc], writes=[bank], sig=True)
                if m < 8:
                    em.op("scalar", lambda e, m=m: e.activation(
                        qTc[:, m, :], p.psum_f[:, m % 4, :], AF.Identity,
                        bias=bapp[:, m:m + 1], scale=0.125),
                        reads=[bank, b_bapp], writes=[b_qTc], sig=True)
                else:
                    em.op("scalar", lambda e, m=m, tc=tc: e.activation(
                        kT[:, m - 8, tc * 512:(tc + 1) * 512], p.psum_f[:, m % 4, :],
                        AF.Identity, bias=bapp[:, m:m + 1]),
                        reads=[bank, b_bapp], writes=[b_kT], sig=True)
            for tt4 in range(4):
                for nn in range(2):
                    bank = p.bank[4 + nn]
                    for k in range(8):
                        em.op("tensor", lambda t, tt4=tt4, nn=nn, k=k: t.matmul(
                            p.psum_f[:, 4 + nn, :], hTc[:, k, tt4 * 128:(tt4 + 1) * 128],
                            wab[:, k, 2048 + nn * 512:2048 + (nn + 1) * 512],
                            start=(k == 0), stop=(k == 7)),
                            reads=[b_hTc, b_wab], writes=[bank], sig=True)
                    em.op("vector", lambda e, tt4=tt4, nn=nn, tc=tc: e.tensor_tensor(
                        Vag[:, 4 * tc + tt4, nn * 8:(nn + 1) * 8, 0:64],
                        p.psum_f[:, 4 + nn, :].rearrange("p (h d) -> p h d", h=8),
                        bvrep[:, nn * 512:(nn + 1) * 512].rearrange("p (h d) -> p h d", h=8),
                        ALU.add),
                        reads=[bank, b_bvrep], writes=[b_V], sig=True)

            b_sqb = Buf("sqb3")
            if b_tails_prev is not None:
                b_sqb.merge_from(b_tails_prev)
            for kt in range(8):
                em.op("vector", lambda e, kt=kt: e.tensor_tensor(
                    sqb[:], qTc[:, kt, :], qTc[:, kt, :], ALU.mult),
                    reads=[b_qTc], writes=[b_sqb], sig=True)
                for r in range(2):
                    em.op("tensor", lambda t, r=r: t.matmul(
                        p.psum_f[0:1, 4 + r, :], ind2[64 * r:64 * r + 64, r:r + 1],
                        sqb[64 * r:64 * r + 64, :], start=True, stop=True),
                        reads=[b_ind2, b_sqb], writes=[p.bank[4 + r]], sig=True)
                for r in range(2):
                    em.op("vector", lambda e, kt=kt, r=r: e.reduce_max(
                        qcsq[0:1, r, kt:kt + 1], p.psum_f[0:1, 4 + r, :], axis=AXX),
                        reads=[p.bank[4 + r]], writes=[b_qcsq], sig=True)
                em.op("vector", lambda e, kt=kt, tc=tc: e.tensor_tensor(
                    sqb[:], kT[:, kt, tc * 512:(tc + 1) * 512],
                    kT[:, kt, tc * 512:(tc + 1) * 512], ALU.mult),
                    reads=[b_kT], writes=[b_sqb], sig=True)
                for r in range(2):
                    em.op("tensor", lambda t, r=r: t.matmul(
                        p.psum_f[0:1, 4 + r, :], ind2[64 * r:64 * r + 64, r:r + 1],
                        sqb[64 * r:64 * r + 64, :], start=True, stop=True),
                        reads=[b_ind2, b_sqb], writes=[p.bank[4 + r]], sig=True)
                for r in range(2):
                    em.op("vector", lambda e, kt=kt, r=r: e.reduce_max(
                        knrow[0:1, r, kt:kt + 1], p.psum_f[0:1, 4 + r, :], axis=AXX),
                        reads=[p.bank[4 + r]], writes=[b_knrow], sig=True)
            em.op("vector", lambda e: e.tensor_tensor(
                knsq[:], knsq[:], knrow[:], ALU.max),
                reads=[b_knsq, b_knrow], writes=[b_knsq], sig=True)
            em.op("scalar", lambda e: e.activation(cnq[:], qcsq[:], AF.Sqrt),
                  reads=[b_qcsq], writes=[b_cnq], sig=True)
            em.op("scalar", lambda e: e.activation(knrow[:], knsq[:], AF.Sqrt),
                  reads=[b_knsq], writes=[b_knrow], sig=True)
            em.op("vector", lambda e: e.tensor_tensor(
                cnq[:], cnq[:], knrow[:], ALU.mult),
                reads=[b_cnq, b_knrow], writes=[b_cnq], sig=True)
            em.op("vector", lambda e: e.tensor_copy(
                cnqs[64:65, :], cnq[0:1, 1, :]),
                reads=[b_cnq], writes=[b_cnqs], sig=True)

            nst = 4 * tc + 4
            loc = tc * 512
            b_PTs = [Buf(f"PT3{i}") for i in range(6)]
            b_srow = Buf("srow3")
            b_rrep = Buf("rrep3")
            if b_tails_prev is not None:
                for bb in b_PTs + [b_srow, b_rrep]:
                    bb.merge_from(b_tails_prev)
            for kt in range(8):
                for r in range(2):
                    em.op("sync", lambda s, kt=kt, loc=loc, r=r: s.dma_start(
                        out=srow[64 * r:64 * r + 1, :],
                        in_=p.dram["c_slopepos"][2 * kt + r][None, loc:loc + 512]),
                        reads=[b_srow], writes=[b_srow], dma_sem="d_sr3")
                em.op("vector", lambda e, kt=kt: e.tensor_scalar(
                    mrowZ0[0:1, :], srow[0:1, :], cnq[0:1, 0, kt:kt + 1], -1.0,
                    ALU.add, ALU.mult),
                    reads=[b_srow, b_cnq], writes=[b_mz[0]], sig=True)
                em.op("vector", lambda e, kt=kt: e.tensor_scalar(
                    mrowZ1[64:65, :], srow[64:65, :], cnqs[64:65, kt:kt + 1], -1.0,
                    ALU.add, ALU.mult),
                    reads=[b_srow, b_cnqs], writes=[b_mz[1]], sig=True)
                mzb = [mrowZ0, mrowZ1]
                nb = (nst + 2) // 3

                def jr(b_):
                    return range(3 * b_, min(3 * b_ + 3, nst))

                def bcast_blk(b_):
                    for j in jr(b_):
                        for r in range(2):
                            sbk = 3 * r + j % 3
                            em.op("tensor", lambda t, sbk=sbk, r=r: t.matmul(
                                p.psum_f[:, sbk, :], onesb[:, 0:128], mzb[r][:],
                                start=True, stop=False),
                                reads=[b_onesb, b_mz[r]], writes=[p.bank[sbk]])

                bcast_blk(0)
                for b_ in range(nb):
                    for j in jr(b_):
                        o = j - 4 * tc
                        for r in range(2):
                            pb = 64 * r
                            sbk = 3 * r + j % 3
                            em.op("tensor", lambda t, j=j, pb=pb, kt=kt, sbk=sbk, o=o: t.matmul(
                                p.psum_f[:, sbk, :],
                                kT[pb:pb + 64, kt, j * 128:(j + 1) * 128],
                                qTc[pb:pb + 64, kt, :],
                                start=False, stop=(o < 0)),
                                reads=[b_kT, b_qTc], writes=[p.bank[sbk]],
                                sig=(o < 0))
                    for j in jr(b_):
                        o = j - 4 * tc
                        if o >= 0:
                            for r in range(2):
                                sbk = 3 * r + j % 3
                                em.op("tensor", lambda t, sbk=sbk, o=o: t.matmul(
                                    p.psum_f[:, sbk, :], idneg[:], mtri[:, o, :],
                                    start=False, stop=True),
                                    reads=[b_idneg, b_mtri], writes=[p.bank[sbk]],
                                    sig=True)
                    if b_ > 0:
                        for j in jr(b_ - 1):
                            for r in range(2):
                                h = 2 * kt + r
                                sl = 3 * r + j % 3
                                em.op("tensor", lambda t, j=j, h=h, r=r, sl=sl, nst=nst: t.matmul(
                                    p.psum_f[0:65, 6 + r, :], Vag[:, j, h, :],
                                    PT[:, sl, :],
                                    start=(j == 0), stop=(j == nst - 1)),
                                    reads=[b_V, b_PTs[sl]], writes=[p.bank[6 + r]],
                                    sig=(r == 1 and j == 3 * b_ - 1))
                    for j in jr(b_):
                        for r in range(2):
                            sbk = 3 * r + j % 3
                            h = 2 * kt + r
                            em.op("scalar", lambda e, j=j, h=h, sbk=sbk: e.activation(
                                PT[:, sbk, :], p.psum_f[:, sbk, :], AF.Exp,
                                bias=alpp[:, h, j:j + 1]),
                                reads=[p.bank[sbk], b_alpp], writes=[b_PTs[sbk]],
                                sig=True)
                    if b_ + 1 < nb:
                        bcast_blk(b_ + 1)
                for j in jr(nb - 1):
                    for r in range(2):
                        h = 2 * kt + r
                        sl = 3 * r + j % 3
                        em.op("tensor", lambda t, j=j, h=h, r=r, sl=sl, nst=nst: t.matmul(
                            p.psum_f[0:65, 6 + r, :], Vag[:, j, h, :],
                            PT[:, sl, :],
                            start=(j == 0), stop=(j == nst - 1)),
                            reads=[b_V, b_PTs[sl]], writes=[p.bank[6 + r]],
                            sig=(r == 1 and j == nst - 1))
                for r in range(2):
                    em.op("vector", lambda e, r=r: e.reciprocal(
                        rrep[64:65, r, :], p.psum_f[64:65, 6 + r, :]),
                        reads=[p.bank[6 + r]], writes=[b_rrep], sig=True)
                    bb = p.bank[2 + 3 * r]
                    em.op("tensor", lambda t, r=r: t.matmul(
                        p.psum_f[:, 2 + 3 * r, :], p.one128[64:65, :], rrep[64:65, r, :],
                        start=True, stop=True),
                        reads=[b_rrep, p.b_one128], writes=[bb], sig=True)
                    em.op("vector", lambda e, r=r: e.tensor_copy(
                        rrep[:, r, :], p.psum_f[:, 2 + 3 * r, :]),
                        reads=[bb], writes=[b_rrep], sig=True)
                    em.op("vector", lambda e, r=r, kt=kt: e.tensor_tensor(
                        oTc[64 * r:64 * r + 64, kt, :], p.psum_f[0:64, 6 + r, :],
                        rrep[0:64, r, :], ALU.mult),
                        reads=[p.bank[6 + r], b_rrep], writes=[b_oTc], sig=True)

            b_tails = [Buf("t3a"), Buf("t3b"), Buf("t3c"), Buf("t3d")]
            for bb in b_tails:
                bb.merge_from(b_PTs + [b_srow, b_rrep, b_sqb])
            tl = dict(ao=tails["ao"], b_ao=[b_tails[0]], sq=tails["sq"], b_sq=[b_tails[1]],
                      sm=tails["sm"], b_sm=[b_tails[2]], xo=tails["xo"], b_xo=[b_tails[3]])
            for tt4 in range(4):
                tt = seq * 16 + tc * 4 + tt4
                banks = [p.bank[0], p.bank[1]]
                for hh in range(2):
                    for kk in range(8):
                        em.op("tensor", lambda t, hh=hh, kk=kk, tt4=tt4: t.matmul(
                            p.psum_f[:, hh, :], oTc[:, kk, tt4 * 128:(tt4 + 1) * 128],
                            wpb[:, kk, hh * 512:(hh + 1) * 512],
                            start=(kk == 0), stop=False),
                            reads=[b_oTc, b_wpb], writes=[banks[hh]])
                    em.op("tensor", lambda t, hh=hh: t.matmul(
                        p.psum_f[:, hh, :], p.onesrow[0:1, 0:128],
                        bprow[0:1, hh * 512:(hh + 1) * 512], start=False, stop=True),
                        reads=[p.b_onesrow, b_bprow], writes=[banks[hh]])
                em.fence("tensor")
                sandwich_tail_tile(p, tt, p.psum_f[:, 0:2, :], banks,
                                   d_res, p.b_res, d_res, p.b_res, tl, slot=0)
            b_tails_prev = b_tails
        em.play_block()
    sb.reset()


def phase4(p: P, n_experts=64, dbg_stop=None):
    em, sb = p.em, p.sb
    E = n_experts
    d_res = p.dram["d_res"]
    xlnT_d = p.dram["d_xlnT"].rearrange("(k p) t -> p k t", p=128)
    coefT_d = p.dram["d_coefT"]
    b_coefT = Buf("d_coefT")

    rWTf = sb.take([128, 8, 64], F32); b_rWTf = Buf("rWTf")
    rtmpf = sb.take([64, 8, 128], F32); b_rtmpf = Buf("rtmpf")
    p4base = sb.off
    em.op("sync", lambda s: s.dma_start(
        out=rtmpf[:], in_=p.dram["L1_ffn_router_w"].rearrange("e (k c) -> e k c", k=8)),
        writes=[b_rtmpf], dma_sem="ds_sy")
    for k in range(8):
        em.op("tensor", lambda t, k=k: t.matmul(
            p.psum_f[0:128, 4 + k % 2, 0:64], rtmpf[:, k, :], p.identf[0:64, 0:64],
            start=True, stop=True),
            reads=[b_rtmpf, p.b_identf], writes=[p.bank[4 + k % 2]], sig=True)
        em.op("vector", lambda e, k=k: e.tensor_copy(rWTf[:, k, :], p.psum_f[:, 4 + k % 2, 0:64]),
              reads=[p.bank[4 + k % 2]], writes=[b_rWTf], sig=True)
    em.play_block()

    ln_transpose_store(p, d_res, p.b_res, ln=True, shift=False,
                       router=rWTf, b_router=b_rWTf)

    sb.reset(p4base)
    save0 = sb.off
    ebrep = sb.take([128, 64], F32); b_ebrep = Buf("ebrep")
    ebrow = sb.take([1, 64], F32); b_ebrow = Buf("ebrow")
    eidx = sb.take([128, 64], F32); b_eidx = Buf("eidx")
    hRt = sb.take([64, 4, 128], F32); b_hRt = [Buf() for _ in range(4)]
    sco = sb.take([128, 4, 64], F32); b_sco = [Buf() for _ in range(4)]
    sfc = sb.take([128, 4, 8, 8], F32); b_sfc = [Buf() for _ in range(4)]
    rnk = sb.take([128, 4, 8, 8], F32); b_rnk = [Buf() for _ in range(4)]
    tmpr = sb.take([128, 4, 8, 8], F32); b_tmpr = [Buf() for _ in range(4)]
    gsc = sb.take([128, 4, 8], F32); b_gsc = [Buf() for _ in range(4)]
    grk = sb.take([128, 4, 8], F32); b_grk = [Buf() for _ in range(4)]
    vv = sb.take([128, 4, 64], F32); b_vv = [Buf() for _ in range(4)]
    vk = sb.take([128, 4, 64], F32); b_vk = [Buf() for _ in range(4)]
    th = sb.take([128, 4, 4], F32); b_th = [Buf() for _ in range(4)]
    cf = sb.take([128, 4, 64], F32); b_cf = [Buf() for _ in range(4)]
    cev = sb.take([64, 4, 128], F32); b_cev = [Buf() for _ in range(4)]

    em.op("sync", lambda s: s.dma_start(
        out=ebrow[:], in_=p.dram["L1_ffn_e_bias"][None, :]),
        writes=[b_ebrow], dma_sem="ds_sy")
    em.op("sync", lambda s: s.dma_start(
        out=eidx[:], in_=p.dram["c_eidx"][:]),
        writes=[b_eidx], dma_sem="ds_sy")
    em.op("tensor", lambda t: t.matmul(
        p.psum_f[:, 6, 0:64], p.onesrow[0:1, 0:128], ebrow[0:1, :],
        start=True, stop=True),
        reads=[b_ebrow, p.b_onesrow], writes=[p.bank[6]], sig=True)
    em.op("vector", lambda e: e.tensor_copy(ebrep[:], p.psum_f[:, 6, 0:64]),
          reads=[p.bank[6]], writes=[b_ebrep], sig=True)

    for it in range(NT):
        sl = it % 4
        em.op("sync", lambda s, it=it, sl=sl: s.dma_start(
            out=hRt[:, sl], in_=p.dram["d_scoT"][:, it * 128:(it + 1) * 128]),
            reads=[Buf("d_scoT")], writes=[b_hRt[sl]], dma_sem=f"d_hr{sl}")
        bank = p.bank[sl]
        em.op("tensor", lambda t, sl=sl: t.matmul(
            p.psum_f[:, sl, 0:64], hRt[:, sl], p.identf[0:64, 0:64],
            start=True, stop=True),
            reads=[b_hRt[sl], p.b_identf], writes=[bank], sig=True)
        em.op("scalar", lambda e, sl=sl: e.activation(
            sco[:, sl], p.psum_f[:, sl, 0:64], AF.Sigmoid),
            reads=[bank], writes=[b_sco[sl]], sig=True)
        em.op("vector", lambda e, sl=sl: e.tensor_tensor(
            sfc[:, sl].rearrange("p a b -> p (a b)"), sco[:, sl], ebrep[:], ALU.add),
            reads=[b_sco[sl], b_ebrep], writes=[b_sfc[sl]], sig=True)
        em.op("vector", lambda e, sl=sl: e.memset(rnk[:, sl], 0.0),
              writes=[b_rnk[sl]], sig=True)
        for d in range(1, 8):
            em.op("vector", lambda e, sl=sl, d=d: e.tensor_tensor(
                tmpr[:, sl, :, d:8], sfc[:, sl, :, d:8], sfc[:, sl, :, 0:8 - d], ALU.is_lt),
                reads=[b_sfc[sl]], writes=[b_tmpr[sl]], sig=True)
            em.op("vector", lambda e, sl=sl, d=d: e.tensor_tensor(
                rnk[:, sl, :, d:8], rnk[:, sl, :, d:8], tmpr[:, sl, :, d:8], ALU.add),
                reads=[b_tmpr[sl], b_rnk[sl]], writes=[b_rnk[sl]], sig=True)
            em.op("vector", lambda e, sl=sl, d=d: e.tensor_tensor(
                tmpr[:, sl, :, 0:8 - d], sfc[:, sl, :, 0:8 - d], sfc[:, sl, :, d:8], ALU.is_lt),
                reads=[b_sfc[sl]], writes=[b_tmpr[sl]], sig=True)
            em.op("vector", lambda e, sl=sl, d=d: e.tensor_tensor(
                rnk[:, sl, :, 0:8 - d], rnk[:, sl, :, 0:8 - d], tmpr[:, sl, :, 0:8 - d], ALU.add),
                reads=[b_tmpr[sl], b_rnk[sl]], writes=[b_rnk[sl]], sig=True)
            em.op("vector", lambda e, sl=sl, d=d: e.tensor_tensor(
                tmpr[:, sl, :, d:8], sfc[:, sl, :, d:8], sfc[:, sl, :, 0:8 - d], ALU.is_equal),
                reads=[b_sfc[sl]], writes=[b_tmpr[sl]], sig=True)
            em.op("vector", lambda e, sl=sl, d=d: e.tensor_tensor(
                rnk[:, sl, :, d:8], rnk[:, sl, :, d:8], tmpr[:, sl, :, d:8], ALU.add),
                reads=[b_tmpr[sl], b_rnk[sl]], writes=[b_rnk[sl]], sig=True)
        em.op("vector", lambda e, sl=sl: e.tensor_scalar(
            tmpr[:, sl], rnk[:, sl], 1.0, None, ALU.is_le),
            reads=[b_rnk[sl]], writes=[b_tmpr[sl]])
        em.op("vector", lambda e, sl=sl: e.tensor_tensor(
            tmpr[:, sl], tmpr[:, sl], sfc[:, sl], ALU.mult),
            reads=[b_tmpr[sl], b_sfc[sl]], writes=[b_tmpr[sl]], sig=True)
        em.op("vector", lambda e, sl=sl: e.tensor_reduce(
            gsc[:, sl], tmpr[:, sl], axis=AXX, op=ALU.add),
            reads=[b_tmpr[sl]], writes=[b_gsc[sl]], sig=True)
        em.op("vector", lambda e, sl=sl: e.memset(grk[:, sl], 0.0),
              writes=[b_grk[sl]], sig=True)
        for d in range(1, 8):
            em.op("vector", lambda e, sl=sl, d=d: e.tensor_tensor(
                tmpr[:, sl, 0, d:8], gsc[:, sl, d:8], gsc[:, sl, 0:8 - d], ALU.is_lt),
                reads=[b_gsc[sl]], writes=[b_tmpr[sl]], sig=True)
            em.op("vector", lambda e, sl=sl, d=d: e.tensor_tensor(
                grk[:, sl, d:8], grk[:, sl, d:8], tmpr[:, sl, 0, d:8], ALU.add),
                reads=[b_tmpr[sl], b_grk[sl]], writes=[b_grk[sl]], sig=True)
            em.op("vector", lambda e, sl=sl, d=d: e.tensor_tensor(
                tmpr[:, sl, 0, 0:8 - d], gsc[:, sl, 0:8 - d], gsc[:, sl, d:8], ALU.is_lt),
                reads=[b_gsc[sl]], writes=[b_tmpr[sl]], sig=True)
            em.op("vector", lambda e, sl=sl, d=d: e.tensor_tensor(
                grk[:, sl, 0:8 - d], grk[:, sl, 0:8 - d], tmpr[:, sl, 0, 0:8 - d], ALU.add),
                reads=[b_tmpr[sl], b_grk[sl]], writes=[b_grk[sl]], sig=True)
            em.op("vector", lambda e, sl=sl, d=d: e.tensor_tensor(
                tmpr[:, sl, 0, d:8], gsc[:, sl, d:8], gsc[:, sl, 0:8 - d], ALU.is_equal),
                reads=[b_gsc[sl]], writes=[b_tmpr[sl]], sig=True)
            em.op("vector", lambda e, sl=sl, d=d: e.tensor_tensor(
                grk[:, sl, d:8], grk[:, sl, d:8], tmpr[:, sl, 0, d:8], ALU.add),
                reads=[b_tmpr[sl], b_grk[sl]], writes=[b_grk[sl]], sig=True)
        em.op("vector", lambda e, sl=sl: e.tensor_scalar(
            grk[:, sl], grk[:, sl], 3.0, None, ALU.is_le),
            reads=[b_grk[sl]], writes=[b_grk[sl]], sig=True)
        em.op("vector", lambda e, sl=sl: e.tensor_tensor(
            vv[:, sl].rearrange("p (g u) -> p g u", u=8),
            sfc[:, sl],
            grk[:, sl].rearrange("p (g u) -> p g u", u=1).broadcast_to([128, 8, 8]),
            ALU.mult),
            reads=[b_sfc[sl], b_grk[sl]], writes=[b_vv[sl]], sig=True)
        em.op("vector", lambda e, sl=sl: e.scalar_tensor_tensor(
            vv[:, sl], eidx[:], -1e-20, vv[:, sl], ALU.mult, ALU.add),
            reads=[b_eidx, b_vv[sl]], writes=[b_vv[sl]])
        em.op("vector", lambda e, sl=sl: e.tensor_copy(vk[:, sl], vv[:, sl]),
              reads=[b_vv[sl]], writes=[b_vk[sl]], sig=True)
        for i8 in range(8):
            em.op("vector", lambda e, sl=sl, i8=i8: e.reduce_max(
                th[:, sl, 0:1], vk[:, sl], axis=AXX),
                reads=[b_vk[sl]], writes=[b_th[sl]], sig=True)
            if i8 < 7:
                em.op("vector", lambda e, sl=sl: e.tensor_scalar(
                    tmpr[:, sl].rearrange("p a b -> p (a b)"), vv[:, sl] if False else vk[:, sl],
                    th[:, sl, 0:1], None, ALU.is_equal),
                    reads=[b_vk[sl], b_th[sl]], writes=[b_tmpr[sl]], sig=True)
                em.op("vector", lambda e, sl=sl: e.scalar_tensor_tensor(
                    vk[:, sl], tmpr[:, sl].rearrange("p a b -> p (a b)"), -1e30,
                    vk[:, sl], ALU.mult, ALU.add),
                    reads=[b_tmpr[sl], b_vk[sl]], writes=[b_vk[sl]], sig=True)
        em.op("vector", lambda e, sl=sl: e.tensor_scalar(
            vv[:, sl], vv[:, sl], th[:, sl, 0:1], None, ALU.is_ge),
            reads=[b_vv[sl], b_th[sl]], writes=[b_vv[sl]], sig=True)
        em.op("vector", lambda e, sl=sl: e.tensor_tensor(
            cf[:, sl], sco[:, sl], vv[:, sl], ALU.mult),
            reads=[b_sco[sl], b_vv[sl]], writes=[b_cf[sl]], sig=True)
        em.op("vector", lambda e, sl=sl: e.reduce_sum(
            th[:, sl, 1:2], cf[:, sl], axis=AXX),
            reads=[b_cf[sl]], writes=[b_th[sl]])
        em.op("vector", lambda e, sl=sl: e.tensor_scalar(
            th[:, sl, 1:2], th[:, sl, 1:2], 1e-20, None, ALU.add),
            reads=[b_th[sl]], writes=[b_th[sl]], sig=True)
        em.op("vector", lambda e, sl=sl: e.reciprocal(th[:, sl, 2:3], th[:, sl, 1:2]),
              reads=[b_th[sl]], writes=[b_th[sl]])
        em.op("vector", lambda e, sl=sl: e.tensor_scalar(
            th[:, sl, 2:3], th[:, sl, 2:3], 2.5, None, ALU.mult),
            reads=[b_th[sl]], writes=[b_th[sl]], sig=True)
        em.op("vector", lambda e, sl=sl: e.tensor_scalar(
            cf[:, sl], cf[:, sl], th[:, sl, 2:3], None, ALU.mult),
            reads=[b_cf[sl], b_th[sl]], writes=[b_cf[sl]], sig=True)
        em.op("tensor", lambda t, sl=sl: t.matmul(
            p.psum_f[0:64, 4 + sl, 0:128], cf[:, sl], p.identf[:],
            start=True, stop=True),
            reads=[b_cf[sl], p.b_identf], writes=[p.bank[4 + sl]], sig=True)
        em.op("vector", lambda e, sl=sl: e.tensor_copy(cev[:, sl], p.psum_f[0:64, 4 + sl, 0:128]),
              reads=[p.bank[4 + sl]], writes=[b_cev[sl]], sig=True)
        em.op("sync", lambda s, it=it, sl=sl: s.dma_start(
            out=coefT_d[:, it * 128:(it + 1) * 128], in_=cev[:, sl]),
            reads=[b_cev[sl]], writes=[b_coefT], dma_sem=f"d_cfo{sl}")
    em.play_block()
    sb.reset(save0)
    if dbg_stop == "routing":
        sb.reset()
        return

    HTOK = T // 2
    for half in range(2):
        sb.reset(save0)
        h0 = half * HTOK
        yac = sb.take([128, 8, HTOK], F32); b_yac = Buf("yac")
        xh = sb.take([128, 8, HTOK], BF16); b_xh = Buf("xh")
        WREG = sb.take([128, 3, 8, 1024], BF16)
        ovl0 = sb.off
        hS = sb.take([128, 8, 256], BF16)
        t1 = sb.take([128, 4, 512], F32)
        crep2 = sb.take([128, 2, HTOK], F32)
        b_creps = [Buf("crep0"), Buf("crep1")]
        b_we = [Buf("we0"), Buf("we1")]
        b_wd = [Buf("wd0"), Buf("wd1")]
        b_t1s = [Buf("t1a"), Buf("t1b")]
        b_hSs = [Buf("hSa"), Buf("hSb")]

        em.op("sync", lambda s, h0=h0: s.dma_start(
            out=xh[:], in_=xlnT_d[:, :, h0:h0 + HTOK]),
            reads=[p.b_xlnTd], writes=[b_xh], dma_sem="ds_sy")

        NU = E + 2
        NTC = HTOK // 256

        def unit_w(u):
            ws = u % 2
            if u < 2:
                em.op("gpsimd", lambda g, u=u, ws=ws: g.dma_start(
                    out=WREG[:, ws, :, 0:512],
                    in_=p.dram["L1_ffn_SWg"].rearrange(
                        "(k p) n -> p k n", p=128)[:, :, u * 512:(u + 1) * 512]),
                    writes=[b_we[ws]], dma_sem=f"d_we{ws}")
                em.op("gpsimd", lambda g, u=u, ws=ws: g.dma_start(
                    out=WREG[:, ws, :, 512:1024],
                    in_=p.dram["L1_ffn_SWu"].rearrange(
                        "(k p) n -> p k n", p=128)[:, :, u * 512:(u + 1) * 512]),
                    writes=[b_we[ws]], dma_sem=f"d_we{ws}")
                em.op("gpsimd", lambda g, u=u, ws=ws: g.dma_start(
                    out=WREG[:, 2, 4 * ws:4 * ws + 4, :],
                    in_=p.dram["L1_ffn_SWd"][u * 512:(u + 1) * 512, :].rearrange(
                        "(k p) n -> p k n", p=128)),
                    writes=[b_wd[ws]], dma_sem=f"d_wd{ws}")
            else:
                e_ = u - 2
                em.op("gpsimd", lambda g, e_=e_, ws=ws: g.dma_start(
                    out=WREG[:, ws, :, 0:512],
                    in_=p.dram["L1_ffn_Wg"][e_].rearrange("(k p) n -> p k n", p=128)),
                    writes=[b_we[ws]], dma_sem=f"d_we{ws}")
                em.op("gpsimd", lambda g, e_=e_, ws=ws: g.dma_start(
                    out=WREG[:, ws, :, 512:1024],
                    in_=p.dram["L1_ffn_Wu"][e_].rearrange("(k p) n -> p k n", p=128)),
                    writes=[b_we[ws]], dma_sem=f"d_we{ws}")
                em.op("gpsimd", lambda g, e_=e_, ws=ws: g.dma_start(
                    out=WREG[:, 2, 4 * ws:4 * ws + 4, :],
                    in_=p.dram["L1_ffn_Wd"][e_].rearrange("(k p) n -> p k n", p=128)),
                    writes=[b_wd[ws]], dma_sem=f"d_wd{ws}")

        def moe_crep(u):
            e_, sl = u - 2, u % 2
            em.op("sync", lambda s, e_=e_, h0=h0, sl=sl: s.dma_start(
                out=crep2[:, sl], in_=coefT_d[e_, h0:h0 + HTOK][None, :]
                .partition_broadcast(128)[:, 0, :]),
                reads=[b_coefT], writes=[b_creps[sl]], dma_sem=f"d_cr{sl}")

        def moe_gu(u, tc, which):
            ws = u % 2
            b0, off = (0, 0) if which == 0 else (2, 512)
            for m in range(4):
                bank = p.bank[b0 + m // 2]
                for k in range(8):
                    em.op("tensor", lambda t, ws=ws, tc=tc, m=m, k=k, b0=b0, off=off: t.matmul(
                        p.psum_f[:, b0 + m // 2, (m % 2) * 256:(m % 2 + 1) * 256],
                        WREG[:, ws, k, off + m * 128:off + (m + 1) * 128],
                        xh[:, k, tc * 256:(tc + 1) * 256],
                        start=(k == 0), stop=(k == 7)),
                        reads=[b_we[ws], b_xh], writes=[bank],
                        sig=(m == 3 and k == 7))

        def moe_h(u, tc):
            sl = tc % 2
            t1v = t1[:, 2 * sl:2 * sl + 2, :]
            em.op("scalar", lambda e, t1v=t1v: e.activation(
                t1v.rearrange("p a b -> p (a b)"),
                p.psum_f[:, 0:2, :].rearrange("p a b -> p (a b)"), AF.Silu),
                reads=[p.bank[0], p.bank[1]], writes=[b_t1s[sl]], sig=True)
            if u >= 2:
                esl = u % 2
                em.op("vector", lambda e, t1v=t1v, tc=tc, esl=esl: e.tensor_tensor(
                    t1v.rearrange("p a (b f) -> p (a b) f", f=256),
                    t1v.rearrange("p a (b f) -> p (a b) f", f=256),
                    crep2[:, esl, tc * 256:(tc + 1) * 256].rearrange(
                        "p (a f) -> p a f", a=1).broadcast_to([128, 4, 256]),
                    ALU.mult),
                    reads=[b_t1s[sl], b_creps[esl]], writes=[b_t1s[sl]], sig=True)
            em.op("vector", lambda e, t1v=t1v, sl=sl: e.tensor_tensor(
                hS[:, 4 * sl:4 * sl + 4, :],
                t1v.rearrange("p a (b f) -> p (a b) f", f=256),
                p.psum_f[:, 2:4, :].rearrange("p a (b f) -> p (a b) f", f=256),
                ALU.mult),
                reads=[b_t1s[sl], p.bank[2], p.bank[3]], writes=[b_hSs[sl]], sig=True)

        def moe_down(u, tc):
            ws = u % 2
            sl = tc % 2
            for mo in range(8):
                bank = p.bank[4 + mo // 2]
                for k in range(4):
                    em.op("tensor", lambda t, ws=ws, mo=mo, k=k, sl=sl: t.matmul(
                        p.psum_f[:, 4 + mo // 2, (mo % 2) * 256:(mo % 2 + 1) * 256],
                        WREG[:, 2, 4 * ws + k, mo * 128:(mo + 1) * 128],
                        hS[:, 4 * sl + k, :],
                        start=(k == 0), stop=(k == 3)),
                        reads=[b_wd[ws], b_hSs[sl]], writes=[bank],
                        sig=(mo == 7 and k == 3))
            pv = p.psum_f[:, 4:8, :].rearrange("p a (b f) -> p (a b) f", f=256)
            if u == 0:
                em.op("vector", lambda e, tc=tc, pv=pv: e.tensor_copy(
                    yac[:, :, tc * 256:(tc + 1) * 256], pv),
                    reads=[p.bank[4 + j] for j in range(4)],
                    writes=[b_yac], sig=True)
            else:
                em.op("vector", lambda e, tc=tc, pv=pv: e.tensor_tensor(
                    yac[:, :, tc * 256:(tc + 1) * 256],
                    yac[:, :, tc * 256:(tc + 1) * 256], pv, ALU.add),
                    reads=[p.bank[4 + j] for j in range(4)] + [b_yac],
                    writes=[b_yac], sig=True)

        unit_w(0)
        if NU > 1:
            unit_w(1)
        if NU > 2:
            moe_crep(2)
        if NU > 3:
            moe_crep(3)
        prev = None
        for u in range(NU):
            for tc in range(NTC):
                moe_gu(u, tc, 0)
                if prev is not None:
                    moe_down(*prev)
                    prev = None
                    if tc == 0:
                        if u + 1 < NU:
                            unit_w(u + 1)
                        if 4 <= u + 1 < NU:
                            moe_crep(u + 1)
                moe_gu(u, tc, 1)
                moe_h(u, tc)
                prev = (u, tc)
        moe_down(*prev)

        tails = make_tail_bufs(p, slots=1, at=ovl0,
                               guards=b_creps + b_t1s + b_hSs)

        for lt in range(HTOK // 128):
            tt = half * 16 + lt
            sl = lt % 2
            banks = [p.bank[2 * sl], p.bank[2 * sl + 1]]
            for k in range(8):
                em.op("tensor", lambda t, lt=lt, sl=sl, k=k: t.matmul(
                    p.psum_f[:, 2 * sl + k // 4, (k % 4) * 128:(k % 4 + 1) * 128],
                    yac[:, k, lt * 128:(lt + 1) * 128], p.identf[:],
                    start=True, stop=True),
                    reads=[b_yac, p.b_identf], writes=[banks[k // 4]])
            em.fence("tensor")
            sandwich_tail_tile(p, tt, p.psum_f[:, 2 * sl:2 * sl + 2, :], banks,
                               d_res, p.b_res, p.dram["out"], p.b_out, tails, slot=0)
        em.play_block()
    sb.reset()


def copy_pass(p: P, src_dram, b_src, dst_dram, b_dst):
    em, sb = p.em, p.sb
    save = sb.off
    cp = sb.take([128, 2, C], F32)
    b_cp = [Buf(), Buf()]
    for i in range(NT):
        sl = i % 2
        em.op("sync", lambda s, i=i, sl=sl: s.dma_start(
            out=cp[:, sl], in_=src_dram[i * 128:(i + 1) * 128, :]),
            reads=[b_src], writes=[b_cp[sl]], dma_sem=f"d_cpi{sl}")
        em.op("sync", lambda s, i=i, sl=sl: s.dma_start(
            out=dst_dram[i * 128:(i + 1) * 128, :], in_=cp[:, sl]),
            reads=[b_cp[sl]], writes=[b_dst], dma_sem=f"d_cpp{sl}")
    em.play_block()
    sb.reset(save)


def host_constants():
    cs = {}
    cs["c_ident"] = np.eye(128, dtype=np.float32)
    cs["c_ones"] = np.ones((1, 512), np.float32)
    cs["c_trem"] = (SEQ - np.arange(1, SEQ + 1, dtype=np.float32))[None, :]
    DH, WIN = 64, 512
    half = np.arange(0, DH, 2, dtype=np.float32)
    inv = 1.0 / 10000.0 ** (half / DH)
    t = np.arange(SEQ, dtype=np.float32)
    freqs = t[:, None] * inv[None, :]
    freqs = np.concatenate([freqs, freqs], -1)
    base = (half + 0.4 * DH) / (1.4 * DH)
    power = (t - SEQ // 2) / float(WIN // 2)
    scale = base[None, :] ** power[:, None]
    scale = np.concatenate([scale, scale], -1)
    cos, sin = np.cos(freqs), np.sin(freqs)
    cs_t = (cos * scale).astype(np.float32)
    ss_t = (sin * scale).astype(np.float32)
    csk_t = (cos / scale).astype(np.float32)
    ssk_t = (sin / scale).astype(np.float32)
    def sign_halves(a):
        b = a.copy()
        b[:, 0:32] = -b[:, 0:32]
        return b
    rot = np.stack([np.tile(a, (1, 8)) for a in
                    (cs_t, sign_halves(ss_t), csk_t, sign_halves(ssk_t))])
    cs["c_rot"] = rot.astype(np.float32)
    slopes = 2.0 ** (-(8.0 / 16) * np.arange(1, 17, dtype=np.float32))
    cs["c_slopepos"] = (slopes[:, None] * np.arange(SEQ, dtype=np.float32)[None, :]).astype(np.float32)
    p_ = np.arange(128)[:, None]
    t_ = np.arange(512)[None, :]
    md = np.zeros((4, 128, 512), np.float32)
    for o in range(4):
        md[o] = np.where(t_ < o * 128 + p_, -1e30, 0.0)
    cs["c_maskdiag"] = md
    cs["c_maskfar"] = np.where(t_ > p_, -1e30, 0.0).astype(np.float32)
    mt = np.zeros((4, 128, 512), np.float32)
    for o in range(4):
        mt[o] = (t_ < o * 128 + p_).astype(np.float32)
    cs["c_mtri"] = mt
    cs["c_ftri"] = (t_ > p_).astype(np.float32)
    cs["c_masktri"] = np.zeros((128, 128), np.float32)
    cs["c_one128"] = np.ones((128, 128), np.float32)
    cs["c_eidx"] = np.tile(np.arange(64, dtype=np.float32)[None, :], (128, 1))
    return cs


def build(phases=("p1",), dbg=False, dbg_stop=None, n_experts=64):
    nc = bass.Bass("TRN2", target_bir_lowering=False, debug=False)
    p = P(nc, dbg)
    declare_io(p, phases[-1])
    load_consts(p)
    if "p1" in phases:
        phase1(p)
    else:
        copy_pass(p, p.dram["L0_attn_x"], Buf("xin0d"), p.dram["d_res"], p.b_res)
    if "p2" in phases:
        phase2(p, dbg_stop=dbg_stop)
    if "p3" in phases:
        phase3(p)
    if "p4" in phases:
        phase4(p, n_experts=n_experts, dbg_stop=dbg_stop if dbg_stop in ("routing",) else None)
    if "p4" in phases:
        em = p.em
        em.play_block()
        return nc, p
    em = p.em
    cp = p.sb.take([128, 2, C], F32)
    b_cp = [Buf(), Buf()]
    for i in range(NT):
        sl = i % 2
        em.op("sync", lambda s, i=i, sl=sl: s.dma_start(
            out=cp[:, sl], in_=p.dram["d_res"][i * 128:(i + 1) * 128, :]),
            reads=[p.b_res], writes=[b_cp[sl]], dma_sem=f"d_cp{sl}")
        em.op("sync", lambda s, i=i, sl=sl: s.dma_start(
            out=p.dram["out"][i * 128:(i + 1) * 128, :], in_=cp[:, sl]),
            reads=[b_cp[sl]], writes=[p.b_out], dma_sem=f"d_cpo{sl}")
    em.raw_wait("sync", "d_cpo0", p.em.cnt["d_cpo0"])
    em.raw_wait("sync", "d_cpo1", p.em.cnt["d_cpo1"])
    em.play_block()
    p.sb.reset()
    return nc, p


_NCORES = 8
_cache = {}


def _get_program():
    if "nc" not in _cache:
        nc, p = build(phases=("p1", "p2", "p3", "p4"))
        _cache["nc"] = nc
    return _cache["nc"]


def kernel(**inputs):
    nc = _get_program()
    consts = host_constants()
    x = np.asarray(inputs["L0_attn_x"], dtype=np.float32)
    base = {}
    for k, v in inputs.items():
        if k == "L0_attn_x":
            continue
        base[k] = np.ascontiguousarray(np.asarray(v, dtype=np.float32))
    base.update(consts)
    in_maps = []
    for c in range(_NCORES):
        m = dict(base)
        m["L0_attn_x"] = np.ascontiguousarray(
            x[2 * c:2 * c + 2].reshape(T, C))
        in_maps.append(m)
    res = run_bass_kernel_spmd(nc, in_maps, core_ids=list(range(_NCORES)))
    out = np.concatenate(
        [np.asarray(r["out"]).reshape(2, SEQ, C) for r in res.results], axis=0)
    return out


# version 07 (final)
# speedup vs baseline: 1.1642x; 1.1642x over previous
import numpy as np
import concourse.bass as bass
import concourse.mybir as mybir
from concourse.bass_utils import run_bass_kernel_spmd


from collections import defaultdict

ENGINES = ("sync", "tensor", "vector", "scalar", "gpsimd")


class Buf:
    __slots__ = ("name", "writer", "readers")

    def __init__(self, name="buf"):
        self.name = name
        self.writer = None
        self.readers = []

    def merge_from(self, olds):
        rd = []
        for o in olds:
            rd.extend(o.readers)
            if o.writer is not None:
                rd.append(o.writer)
        self.readers = rd
        self.writer = None
        return self


class OpRec:
    __slots__ = ("engine", "fn", "ins", "sem_updates", "idx", "tok", "is_wait")

    def __init__(self, engine, fn, idx, is_wait=False):
        self.engine = engine
        self.fn = fn
        self.ins = None
        self.sem_updates = []
        self.idx = idx
        self.tok = None
        self.is_wait = is_wait


class Emitter:
    def __init__(self, nc):
        self.nc = nc
        self.q = {e: [] for e in ENGINES}
        self.sems = {}
        self.cnt = defaultdict(int)
        self.last_wait = defaultdict(lambda: -1)
        self.pending = {e: [] for e in ENGINES}
        self.last_real = {e: None for e in ENGINES}
        self._idx = 0
        self.n_ops = 0
        self.dma_sems = defaultdict(set)

    def sem(self, name):
        if name not in self.sems:
            self.sems[name] = self.nc.alloc_semaphore(name)
        return self.sems[name]

    def _queue(self, engine, fn, is_wait=False):
        rec = OpRec(engine, fn, self._idx, is_wait)
        self._idx += 1
        self.q[engine].append(rec)
        if not is_wait:
            self.last_real[engine] = rec
            self.n_ops += 1
        return rec

    def raw_wait(self, engine, sem_name, value):
        if value <= 0 or self.last_wait[(engine, sem_name)] >= value:
            return
        self.last_wait[(engine, sem_name)] = value
        h = self.sem(sem_name)
        def fn(eng, h=h, v=value):
            eng.wait_ge(h, v)
            return None
        self._queue(engine, fn, is_wait=True)

    def _wait_tok(self, engine, tok):
        if tok is not None:
            self.raw_wait(engine, tok[0], tok[1])

    def fence(self, engine, sem_name=None, by=1):
        sem_name = sem_name or ("s_" + engine)
        h = self.sem(sem_name)
        last = self.last_real[engine]
        assert last is not None, f"fence on engine {engine} with no ops"
        self.cnt[sem_name] += by
        tok = (sem_name, self.cnt[sem_name])
        last.sem_updates.append((h, by))
        for rec in self.pending[engine]:
            rec.tok = tok
        self.pending[engine].clear()
        return tok

    def op(self, engine, fn, reads=(), writes=(), sig=None, dma_sem=None):
        for b in reads:
            self._dep(engine, b.writer)
        for b in writes:
            self._dep(engine, b.writer)
            for r in b.readers:
                self._dep(engine, r)
        if dma_sem is not None and dma_sem.startswith("ds_"):
            self.raw_wait(engine, dma_sem, self.cnt[dma_sem])
        rec = self._queue(engine, fn)
        if dma_sem is not None:
            h = self.sem(dma_sem)
            self.cnt[dma_sem] += 16
            rec.sem_updates.append((h, 16))
            rec.tok = (dma_sem, self.cnt[dma_sem])
            self.dma_sems[engine].add(dma_sem)
        else:
            self.pending[engine].append(rec)
        for b in reads:
            b.readers.append((engine, rec))
        for b in writes:
            b.writer = (engine, rec)
            b.readers = []
        if sig:
            return self.fence(engine, sig if sig is not True else None)
        return rec

    def _dep(self, engine, dep):
        if dep is None:
            return
        p_eng, rec = dep
        if rec.tok is None:
            if p_eng == "tensor" and engine == "tensor":
                return
            assert rec in self.pending[p_eng], "dep on unfenced, non-pending op"
            self.fence(p_eng)
        if p_eng == engine == "tensor":
            return
        self._wait_tok(engine, rec.tok)

    def barrier_note(self):
        for e in ENGINES:
            if self.pending[e]:
                self.fence(e)
        for e in ENGINES:
            for s in list(self.cnt):
                if self.last_wait[(e, s)] < self.cnt[s]:
                    self.last_wait[(e, s)] = self.cnt[s]

    def play_block(self):
        emitter = self
        for e in ENGINES:
            for s in sorted(self.dma_sems[e]):
                self.raw_wait(e, s, self.cnt[s])
        for e in ENGINES:
            if self.pending[e]:
                self.fence(e)

        with self.nc.Block() as block:
            for e in ENGINES:
                if not self.q[e]:
                    continue

                def make_body(eng_name):
                    def body(eng):
                        for rec in emitter.q[eng_name]:
                            ins = rec.fn(eng)
                            rec.ins = ins
                            for (h, by) in rec.sem_updates:
                                assert ins is not None, \
                                    f"sem update on wait-op ({eng_name})"
                                ins.then_inc(h, by)
                    return body

                getattr(block, e)(make_body(e))
        for e in ENGINES:
            self.q[e] = []
            self.pending[e] = []
            self.last_real[e] = None
        self.barrier_note()



F32 = mybir.dt.float32
BF16 = mybir.dt.bfloat16
AF = mybir.ActivationFunctionType
ALU = mybir.AluOpType
AXX = mybir.AxisListType.X

T = 4096
SEQ = 2048
C = 1024
NT = T // 128
NSEQ = 2

SB_BASE = 33024
SB_TOP = 229344


class SB:
    def __init__(self, nc):
        self.nc = nc
        self.off = SB_BASE
        self.perm_off = SB_TOP
        self.n = 0

    def _alloc(self, shape, dtype, off):
        name = f"sb{self.n}"
        self.n += 1
        return self.nc.alloc_sbuf_tensor_at(name, list(shape), dtype, offset=off)

    def take(self, shape, dtype):
        itemsize = 2 if dtype == BF16 else 4
        bytes_pp = int(np.prod(shape[1:])) * itemsize
        off = (self.off + 63) // 64 * 64
        self.last_off = off
        self.off = off + bytes_pp
        assert self.off <= self.perm_off, f"SBUF overflow {self.off}"
        return self._alloc(shape, dtype, off)

    def overlay(self, off, shape, dtype):
        itemsize = 2 if dtype == BF16 else 4
        bytes_pp = int(np.prod(shape[1:])) * itemsize
        assert off + bytes_pp <= self.perm_off
        return self._alloc(shape, dtype, off)

    def perm(self, shape, dtype):
        itemsize = 2 if dtype == BF16 else 4
        bytes_pp = int(np.prod(shape[1:])) * itemsize
        self.perm_off = (self.perm_off - bytes_pp) // 64 * 64
        assert self.off <= self.perm_off, "SBUF overflow (perm)"
        return self._alloc(shape, dtype, self.perm_off)

    def reset(self, to=SB_BASE):
        self.off = to


class P:
    def __init__(self, nc, dbg=False):
        self.nc = nc
        self.em = Emitter(nc)
        self.sb = SB(nc)
        self.dbg = dbg
        self.dram = {}
        self.psum_f = nc.alloc_psum_tensor("ps_f", [128, 8, 512], F32)
        self.bank = [Buf(f"bank{i}") for i in range(8)]
        self.identb = self.sb.perm([128, 128], BF16)
        self.b_identb = Buf("identb")
        self.identf = self.sb.perm([128, 128], F32)
        self.b_identf = Buf("identf")
        self.one128 = self.sb.perm([128, 128], F32)
        self.b_one128 = Buf("one128")
        self.onesrow = self.sb.perm([1, 512], F32)
        self.b_onesrow = Buf("onesrow")

    def din(self, name, shape, dtype=F32):
        t = self.nc.dram_tensor(name, list(shape), dtype, kind="ExternalInput")
        self.dram[name] = t.ap()
        return self.dram[name]

    def dout(self, name, shape, dtype=F32):
        t = self.nc.dram_tensor(name, list(shape), dtype, kind="ExternalOutput")
        self.dram[name] = t.ap()
        return self.dram[name]

    def dint(self, name, shape, dtype=F32):
        t = self.nc.dram_tensor(name, list(shape), dtype, kind="Internal")
        self.dram[name] = t.ap()
        return self.dram[name]


def declare_io(p: P, last_phase):
    d = p.din
    d("L0_attn_x", [T, C])
    for w in ["Wr", "Wk", "Wv", "Wo"]:
        d(f"L0_attn_{w}", [C, C])
    for v in ["br", "bk", "bv", "bo", "time_decay", "U"]:
        d(f"L0_attn_{v}", [C])
    d("L0_ffn_wqkv", [2, 1024, 1536]); d("L0_ffn_q_scale", [2, 64])
    d("L0_ffn_k_scale", [2, 64]); d("L0_ffn_wo", [2, 512, 1024])
    d("L0_ffn_ln_g", [2, 1024]); d("L0_ffn_ln_b", [2, 1024])
    d("L0_ffn_w1", [2, 1024, 5460]); d("L0_ffn_b1", [2, 5460])
    d("L0_ffn_w2", [2, 2730, 1024]); d("L0_ffn_b2", [2, 1024])
    d("L1_attn_w_attn", [1024, 3072]); d("L1_attn_b_attn", [3072])
    d("L1_attn_w_proj", [1024, 1024]); d("L1_attn_b_proj", [1024])
    d("L1_ffn_router_w", [64, 1024]); d("L1_ffn_e_bias", [64])
    d("L1_ffn_Wg", [64, 1024, 512]); d("L1_ffn_Wu", [64, 1024, 512])
    d("L1_ffn_Wd", [64, 512, 1024])
    d("L1_ffn_SWg", [1024, 1024]); d("L1_ffn_SWu", [1024, 1024]); d("L1_ffn_SWd", [1024, 1024])
    d("c_ident", [128, 128])
    d("c_ones", [1, 512])
    d("c_trem", [1, SEQ])
    d("c_rot", [4, SEQ, 512])
    d("c_slopepos", [16, SEQ])
    d("c_maskdiag", [4, 128, 512])
    d("c_one128", [128, 128])
    d("c_eidx", [128, 64])
    d("c_maskfar", [128, 512])
    d("c_mtri", [4, 128, 512])
    d("c_ftri", [128, 512])
    d("c_masktri", [128, 128])
    p.dint("d_res", [T, C])
    p.dint("d_xlnT", [C, T], BF16)
    p.dint("d_y", [T, C])
    p.dint("d_coefT", [64, T])
    p.dint("d_wT", [4, C, C], BF16)
    p.dint("d_sT", [C, T], BF16)
    p.dint("d_scoT", [64, T])
    p.dint("d_dbg", [128, 4096])
    p.dint("d_zT", [C, T], BF16)
    p.dout("out", [T, C])
    p.b_res = Buf("d_res")
    p.b_xlnTd = Buf("d_xlnT")
    p.b_out = Buf("out")


def load_consts(p: P):
    if getattr(p, "consts_loaded", False):
        return
    p.consts_loaded = True
    em = p.em
    em.op("gpsimd", lambda g: g.dma_start(out=p.identb[:], in_=p.dram["c_ident"][:]),
          writes=[p.b_identb], dma_sem="ds_gp")
    em.op("sync", lambda s: s.dma_start(out=p.identf[:], in_=p.dram["c_ident"][:]),
          writes=[p.b_identf], dma_sem="ds_sy")
    em.op("sync", lambda s: s.dma_start(out=p.one128[:], in_=p.dram["c_one128"][:]),
          writes=[p.b_one128], dma_sem="ds_sy")
    em.op("sync", lambda s: s.dma_start(out=p.onesrow[:], in_=p.dram["c_ones"][:]),
          writes=[p.b_onesrow], dma_sem="ds_sy")


def emit_ln_tile(p, eng_small, xin_ap, b_xin, sq_ap, b_sq, sm_ap, b_sm):
    em = p.em
    em.op("scalar", lambda e: e.activation(sq_ap, xin_ap, AF.Square,
                                           accum_out=sm_ap[:, 1:2]),
          reads=[b_xin], writes=[b_sq, b_sm], sig=True)
    em.op("vector", lambda e: e.reduce_sum(sm_ap[:, 0:1], xin_ap, axis=AXX),
          reads=[b_xin], writes=[b_sm])
    em.op("vector", lambda e: e.tensor_scalar(sm_ap[:, 2:3], sm_ap[:, 0:1],
                                              1.0 / C, None, ALU.mult),
          reads=[b_sm], writes=[b_sm])
    em.op("vector", lambda e: e.tensor_tensor(sm_ap[:, 3:4], sm_ap[:, 2:3],
                                              sm_ap[:, 2:3], ALU.mult),
          reads=[b_sm], writes=[b_sm])
    em.op("vector", lambda e: e.tensor_scalar(sm_ap[:, 4:5], sm_ap[:, 1:2],
                                              1.0 / C, None, ALU.mult),
          reads=[b_sm], writes=[b_sm])
    em.op("vector", lambda e: e.tensor_tensor(sm_ap[:, 4:5], sm_ap[:, 4:5],
                                              sm_ap[:, 3:4], ALU.subtract),
          reads=[b_sm], writes=[b_sm])
    em.op("vector", lambda e: e.tensor_scalar(sm_ap[:, 4:5], sm_ap[:, 4:5],
                                              1e-5, None, ALU.add),
          reads=[b_sm], writes=[b_sm], sig=True)
    em.op("scalar", lambda e: e.activation(sm_ap[:, 5:6], sm_ap[:, 4:5], AF.Sqrt),
          reads=[b_sm], writes=[b_sm], sig=True)
    em.op("vector", lambda e: e.reciprocal(sm_ap[:, 6:7], sm_ap[:, 5:6]),
          reads=[b_sm], writes=[b_sm], sig=True)


def ln_transpose_store(p: P, src_dram, b_src, ln=True, shift=False,
                       router=None, b_router=None):
    em, sb = p.em, p.sb
    save = sb.off
    xin = sb.take([128, 2, C], F32)
    sq = sb.take([128, 2, C], F32)
    xbf = sb.take([128, 2, C], BF16)
    evT = sb.take([128, 2, C], BF16)
    sm = sb.take([128, 2, 8], F32)
    b_xin = [Buf("xin0"), Buf("xin1")]
    b_sq = [Buf(), Buf()]
    b_xbf = [Buf(), Buf()]
    b_ev = [Buf(), Buf()]
    b_sm = [Buf(), Buf()]
    if router is not None:
        xf = sb.take([128, 2, C], F32)
        xfT = sb.take([128, 2, C], F32)
        scv = sb.take([64, 2, 128], F32)
        b_xf = [Buf(), Buf()]; b_xfT = [Buf(), Buf()]; b_scv = [Buf(), Buf()]
        b_scoT = Buf("d_scoT")
        scoT_d = p.dram["d_scoT"]
    dstT = p.dram["d_xlnT"].rearrange("(k p) t -> p k t", p=128)
    for i in range(NT):
        sl = i % 2
        if not shift:
            em.op("sync", lambda s, i=i, sl=sl: s.dma_start(
                out=xin[:, sl], in_=src_dram[i * 128:(i + 1) * 128, :]),
                reads=[b_src], writes=[b_xin[sl]], dma_sem=f"d_ln{sl}")
        else:
            r0 = i * 128
            em.op("sync", lambda s, r0=r0, sl=sl: s.dma_start(
                out=xin[:, sl, 0:512], in_=src_dram[r0:r0 + 128, 0:512]),
                reads=[b_src], writes=[b_xin[sl]], dma_sem=f"d_ln{sl}")
            if i % 16 == 0:
                em.op("sync", lambda s, r0=r0, sl=sl: s.dma_start(
                    out=xin[1:128, sl, 512:1024], in_=src_dram[r0:r0 + 127, 512:1024]),
                    reads=[b_src], writes=[b_xin[sl]], dma_sem=f"d_ln{sl}")
                em.op("vector", lambda e, sl=sl: e.memset(xin[0:1, sl, 512:1024], 0.0),
                      writes=[b_xin[sl]], sig=True)
            else:
                em.op("sync", lambda s, r0=r0, sl=sl: s.dma_start(
                    out=xin[:, sl, 512:1024], in_=src_dram[r0 - 1:r0 + 127, 512:1024]),
                    reads=[b_src], writes=[b_xin[sl]], dma_sem=f"d_ln{sl}")
        if ln:
            emit_ln_tile(p, None, xin[:, sl], b_xin[sl], sq[:, sl], b_sq[sl],
                         sm[:, sl], b_sm[sl])
            em.op("vector", lambda e, sl=sl: e.tensor_scalar(
                xbf[:, sl], xin[:, sl], sm[:, sl, 2:3], sm[:, sl, 6:7],
                ALU.subtract, ALU.mult),
                reads=[b_xin[sl], b_sm[sl]], writes=[b_xbf[sl]], sig=True)
        else:
            em.op("vector", lambda e, sl=sl: e.tensor_copy(xbf[:, sl], xin[:, sl]),
                  reads=[b_xin[sl]], writes=[b_xbf[sl]], sig=True)
        banks = [p.bank[2 * sl], p.bank[2 * sl + 1]]
        for k in range(8):
            em.op("tensor", lambda t, sl=sl, k=k: t.matmul(
                p.psum_f[:, 2 * sl + k // 4, (k % 4) * 128:(k % 4 + 1) * 128],
                xbf[:, sl, k * 128:(k + 1) * 128], p.identb[:], start=True, stop=True),
                reads=[b_xbf[sl], p.b_identb], writes=[banks[k // 4]])
        em.fence("tensor")
        em.op("vector", lambda e, sl=sl: e.tensor_copy(
            evT[:, sl].rearrange("p (a b) -> p a b", a=2), p.psum_f[:, 2 * sl:2 * sl + 2, :]),
            reads=banks, writes=[b_ev[sl]], sig=True)
        em.op("sync", lambda s, i=i, sl=sl: s.dma_start(
            out=dstT[:, :, i * 128:(i + 1) * 128],
            in_=evT[:, sl].rearrange("p (k f) -> p k f", k=8)),
            reads=[b_ev[sl]], writes=[p.b_xlnTd], dma_sem=f"d_lnst{sl}")
        if router is not None:
            em.op("vector", lambda e, sl=sl: e.tensor_scalar(
                xf[:, sl], xin[:, sl], sm[:, sl, 2:3], sm[:, sl, 6:7],
                ALU.subtract, ALU.mult),
                reads=[b_xin[sl], b_sm[sl]], writes=[b_xf[sl]], sig=True)
            banksf = [p.bank[4 + 2 * sl], p.bank[5 + 2 * sl]]
            for k in range(8):
                em.op("tensor", lambda t, sl=sl, k=k: t.matmul(
                    p.psum_f[:, 4 + 2 * sl + k // 4, (k % 4) * 128:(k % 4 + 1) * 128],
                    xf[:, sl, k * 128:(k + 1) * 128], p.identf[:], start=True, stop=True),
                    reads=[b_xf[sl], p.b_identf], writes=[banksf[k // 4]])
            em.fence("tensor")
            em.op("vector", lambda e, sl=sl: e.tensor_copy(
                xfT[:, sl].rearrange("p (a b) -> p a b", a=2),
                p.psum_f[:, 4 + 2 * sl:6 + 2 * sl, :]),
                reads=banksf, writes=[b_xfT[sl]], sig=True)
            for k in range(8):
                em.op("tensor", lambda t, sl=sl, k=k: t.matmul(
                    p.psum_f[0:64, 4 + 2 * sl, 0:128], router[:, k, :],
                    xfT[:, sl, k * 128:(k + 1) * 128], start=(k == 0), stop=(k == 7)),
                    reads=[b_router, b_xfT[sl]], writes=[banksf[0]], sig=True)
            em.op("vector", lambda e, sl=sl: e.tensor_copy(
                scv[:, sl], p.psum_f[0:64, 4 + 2 * sl, 0:128]),
                reads=[banksf[0]], writes=[b_scv[sl]], sig=True)
            em.op("sync", lambda s, i=i, sl=sl: s.dma_start(
                out=scoT_d[:, i * 128:(i + 1) * 128], in_=scv[:, sl]),
                reads=[b_scv[sl]], writes=[b_scoT], dma_sem=f"d_sco{sl}")
    em.play_block()
    sb.reset(save)


def sandwich_tail_tile(p, tt, psum_view, banks, src_res_dram, b_srcres, dst_dram,
                       b_dst, bufs, extra_reads=(), slot=None):
    em = p.em
    sl = slot if slot is not None else tt % 2
    ao, b_ao = bufs["ao"], bufs["b_ao"]
    sq, b_sq = bufs["sq"], bufs["b_sq"]
    sm, b_sm = bufs["sm"], bufs["b_sm"]
    xo, b_xo = bufs["xo"], bufs["b_xo"]
    em.op("vector", lambda e, sl=sl: e.tensor_copy(ao[:, sl], psum_view),
          reads=list(banks) + list(extra_reads), writes=[b_ao[sl]], sig=True)
    em.op("sync", lambda s, tt=tt, sl=sl: s.dma_start(
        out=xo[:, sl], in_=src_res_dram[tt * 128:(tt + 1) * 128, :]),
        reads=[b_srcres], writes=[b_xo[sl]], dma_sem=f"d_tail{sl}")
    emit_ln_tile(p, None, ao[:, sl].rearrange("p a b -> p (a b)"), b_ao[sl],
                 sq[:, sl], b_sq[sl], sm[:, sl], b_sm[sl])
    aof = ao[:, sl].rearrange("p a b -> p (a b)")
    em.op("vector", lambda e, sl=sl: e.tensor_scalar(
        aof, aof, sm[:, sl, 2:3], sm[:, sl, 6:7], ALU.subtract, ALU.mult),
        reads=[b_ao[sl], b_sm[sl]], writes=[b_ao[sl]], sig=True)
    em.op("vector", lambda e, sl=sl: e.tensor_tensor(
        aof, aof, xo[:, sl], ALU.add),
        reads=[b_ao[sl], b_xo[sl]], writes=[b_ao[sl]], sig=True)
    em.op("sync", lambda s, tt=tt, sl=sl: s.dma_start(
        out=dst_dram[tt * 128:(tt + 1) * 128, :], in_=aof),
        reads=[b_ao[sl]], writes=[b_dst], dma_sem=f"d_tailo{sl}")


def make_tail_bufs(p, slots=2, at=None, guards=()):
    sb = p.sb
    state = {"at": at}

    def grab(shape, dtype):
        if state["at"] is None:
            return sb.take(shape, dtype)
        itemsize = 2 if dtype == BF16 else 4
        off = (state["at"] + 63) // 64 * 64
        t = sb.overlay(off, shape, dtype)
        state["at"] = off + int(np.prod(shape[1:])) * itemsize
        return t

    def mb():
        return [Buf().merge_from(list(guards)) for _ in range(slots)]

    return dict(
        ao=grab([128, slots, 2, 512], F32), b_ao=mb(),
        sq=grab([128, slots, C], F32), b_sq=mb(),
        sm=grab([128, slots, 8], F32), b_sm=mb(),
        xo=grab([128, slots, C], F32), b_xo=mb(),
    )


def phase1(p: P):
    em, sb = p.em, p.sb
    EPSb = float(np.exp(-1.0 / 12.0))
    EPS_R, EPS_K, EPS_V = EPSb / 2.0, EPSb + 0.3 / 11.0, EPSb

    x_in = p.dram["L0_attn_x"]
    b_x = Buf("x_in")
    load_consts(p)
    ln_transpose_store(p, x_in, b_x)

    trep = sb.take([128, SEQ], F32); b_trep = Buf("trep")
    Upp = sb.take([128, 8], F32); b_Upp = Buf("Upp")
    nwpp = sb.take([128, 8], F32); b_nwpp = Buf("nwpp")
    persist = sb.off
    wtmp = sb.take([128, 8, 1024], BF16); b_wtmp = Buf("wtmp")
    wev = sb.take([128, 2, 1024], BF16); b_wev = [Buf(), Buf()]
    trow = sb.take([1, SEQ], F32); b_trow = Buf("trow")
    b_wTd = Buf("d_wT")
    wT_d = p.dram["d_wT"].rearrange("w (k p) c -> w p k c", p=128)

    em.op("sync", lambda s: s.dma_start(
        out=Upp[:], in_=p.dram["L0_attn_U"].rearrange("(k p) -> p k", p=128),
        allow_slow_non_contiguous=True),
        writes=[b_Upp], dma_sem="ds_sy")
    em.op("sync", lambda s: s.dma_start(
        out=nwpp[:], in_=p.dram["L0_attn_time_decay"].rearrange("(k p) -> p k", p=128),
        allow_slow_non_contiguous=True),
        writes=[b_nwpp], dma_sem="ds_sy")
    em.op("vector", lambda e: e.tensor_scalar(
        nwpp[:], nwpp[:], 0.0, -1.0, ALU.max, ALU.mult),
        reads=[b_nwpp], writes=[b_nwpp], sig=True)
    em.op("sync", lambda s: s.dma_start(out=trow[:], in_=p.dram["c_trem"][:]),
          writes=[b_trow], dma_sem="ds_sy")
    for tc in range(4):
        bk = p.bank[tc % 2]
        em.op("tensor", lambda t, tc=tc: t.matmul(
            p.psum_f[:, tc % 2, :], p.onesrow[0:1, 0:128],
            trow[0:1, tc * 512:(tc + 1) * 512], start=True, stop=True),
            reads=[b_trow, p.b_onesrow], writes=[bk], sig=True)
        em.op("vector", lambda e, tc=tc: e.tensor_copy(
            trep[:, tc * 512:(tc + 1) * 512], p.psum_f[:, tc % 2, :]),
            reads=[bk], writes=[b_trep], sig=True)
    for wi, w in enumerate(["Wr", "Wk", "Wv", "Wo"]):
        em.op("gpsimd", lambda g, w=w: g.dma_start(
            out=wtmp[:], in_=p.dram["L0_attn_" + w].rearrange("(k p) c -> p k c", p=128)),
            writes=[b_wtmp], dma_sem="ds_gp")
        for k in range(8):
            sl = k % 2
            banks = [p.bank[4 + 2 * sl], p.bank[5 + 2 * sl]]
            for j in range(8):
                em.op("tensor", lambda t, k=k, j=j, sl=sl: t.matmul(
                    p.psum_f[:, 4 + 2 * sl + j // 4, (j % 4) * 128:(j % 4 + 1) * 128],
                    wtmp[:, k, j * 128:(j + 1) * 128], p.identb[:], start=True, stop=True),
                    reads=[b_wtmp, p.b_identb], writes=[banks[j // 4]])
            em.fence("tensor")
            em.op("vector", lambda e, sl=sl: e.tensor_copy(
                wev[:, sl].rearrange("p (a b) -> p a b", a=2),
                p.psum_f[:, 4 + 2 * sl:6 + 2 * sl, :]),
                reads=banks, writes=[b_wev[sl]], sig=True)
            em.op("sync", lambda s, wi=wi, k=k, sl=sl: s.dma_start(
                out=wT_d[wi, :, :, k * 128:(k + 1) * 128],
                in_=wev[:, sl].rearrange("p (j f) -> p j f", j=8)),
                reads=[b_wev[sl]], writes=[b_wTd], dma_sem=f"d_wst{sl}")
    em.play_block()
    sb.reset(persist)

    xlnT_d = p.dram["d_xlnT"].rearrange("(k p) t -> p k t", p=128)
    sT_d = p.dram["d_sT"].rearrange("(k p) t -> p k t", p=128)
    zT_d = p.dram["d_zT"].rearrange("(k p) t -> p k t", p=128)

    WREG = sb.take([128, 2, 8, 1024], BF16)
    b_W0, b_W1 = Buf("W0"), Buf("W1")
    rows = sb.take([1, 2, 1024], F32)
    b_rows = Buf("rows")
    seq_base = sb.off

    for seq in range(NSEQ):
        sb.reset(seq_base)
        t0 = seq * SEQ
        PR = sb.take([128, 5, SEQ], F32)
        pr_off = sb.last_off
        xlnT = sb.overlay(pr_off, [128, 8, SEQ], BF16)
        ik = sb.take([128, 8, SEQ], BF16); b_ik = Buf("ik")
        iv = sb.take([128, 8, SEQ], BF16); b_iv = Buf("iv")
        kc = sb.take([128, SEQ], F32); b_kc = Buf("kc")
        vc = sb.take([128, SEQ], F32); b_vc = Buf("vc")
        sg = sb.take([128, 2, SEQ], BF16); b_sg = [Buf(), Buf()]
        zc = sb.take([128, 2, SEQ], BF16); b_zc = [Buf(), Buf()]
        sst = sb.take([128, 2, 512], BF16); b_sst = [Buf(), Buf()]
        b_xs = Buf("xlnT_s")
        bp = [Buf(f"pool{i}") for i in range(5)]
        b_sTd = Buf("d_sT"); b_zTd = Buf("d_zT")

        em.op("sync", lambda s, t0=t0: s.dma_start(
            out=xlnT[:], in_=xlnT_d[:, :, t0:t0 + SEQ]),
            reads=[p.b_xlnTd], writes=[b_xs], dma_sem="ds_sy")

        def interp(dst, b_dst, eps, scr_ap, b_scr, scr2_ap, b_scr2):
            a, b = float(eps), float(1.0 - eps)
            for k in range(8):
                em.op("vector", lambda e, k=k: e.tensor_scalar(
                    scr_ap[:, 0:SEQ - 2], xlnT[:, k, 2:SEQ], b, None, ALU.mult),
                    reads=[b_xs], writes=[b_scr], sig=True)
                em.op("vector", lambda e, k=k: e.tensor_scalar(
                    scr2_ap[:, 0:SEQ - 2], xlnT[:, k, 0:SEQ - 2], a, None, ALU.mult),
                    reads=[b_xs], writes=[b_scr2], sig=True)
                em.op("vector", lambda e, k=k: e.tensor_tensor(
                    dst[:, k, 1:SEQ - 1], scr_ap[:, 0:SEQ - 2], scr2_ap[:, 0:SEQ - 2], ALU.add),
                    reads=[b_scr, b_scr2], writes=[b_dst])
                em.op("vector", lambda e, k=k: e.tensor_scalar(
                    dst[:, k, 0:1], xlnT[:, k, 1:2], b, None, ALU.mult),
                    reads=[b_xs], writes=[b_dst])
                em.op("vector", lambda e, k=k: e.tensor_scalar(
                    dst[:, k, SEQ - 1:SEQ], xlnT[:, k, SEQ - 2:SEQ - 1], a, None, ALU.mult),
                    reads=[b_xs], writes=[b_dst], sig=True)

        interp(ik, b_ik, EPS_R, kc, b_kc, vc, b_vc)

        em.op("sync", lambda s: s.dma_start(out=WREG[:, 0], in_=wT_d[0]),
              reads=[b_wTd, b_W0], writes=[b_W0], dma_sem="d_wr0")
        em.op("sync", lambda s: s.dma_start(
            out=rows[0:1, 0], in_=p.dram["L0_attn_br"][None, :]),
            reads=[b_rows], writes=[b_rows], dma_sem="ds_sy")
        for tc in range(4):
            for m in range(8):
                bank = p.bank[m]
                for k in range(8):
                    em.op("tensor", lambda t, m=m, k=k, tc=tc: t.matmul(
                        p.psum_f[:, m, :], WREG[:, 0, k, m * 128:(m + 1) * 128],
                        ik[:, k, tc * 512:(tc + 1) * 512], start=(k == 0), stop=False),
                        reads=[b_W0, b_ik], writes=[bank])
                em.op("tensor", lambda t, m=m: t.matmul(
                    p.psum_f[:, m, :], rows[0:1, 0, m * 128:(m + 1) * 128],
                    p.onesrow[0:1, 0:512], start=False, stop=True),
                    reads=[b_rows, p.b_onesrow], writes=[bank], sig=True)
                sl = m % 2
                em.op("scalar", lambda e, m=m, sl=sl: e.activation(
                    sst[:, sl], p.psum_f[:, m, :], AF.Sigmoid),
                    reads=[bank], writes=[b_sst[sl]], sig=True)
                em.op("sync", lambda s, m=m, tc=tc, sl=sl, t0=t0: s.dma_start(
                    out=sT_d[:, m, t0 + tc * 512:t0 + (tc + 1) * 512], in_=sst[:, sl]),
                    reads=[b_sst[sl]], writes=[b_sTd], dma_sem=f"d_sst{sl}")

        interp(ik, b_ik, EPS_K, kc, b_kc, vc, b_vc)
        interp(iv, b_iv, EPS_V, kc, b_kc, vc, b_vc)

        em.op("sync", lambda s: s.dma_start(out=WREG[:, 0], in_=wT_d[1]),
              reads=[b_wTd, b_W0], writes=[b_W0], dma_sem="d_wr0")
        em.op("sync", lambda s: s.dma_start(out=WREG[:, 1], in_=wT_d[2]),
              reads=[b_wTd, b_W1], writes=[b_W1], dma_sem="d_wr1")
        em.op("sync", lambda s: s.dma_start(
            out=rows[0:1, 0], in_=p.dram["L0_attn_bk"][None, :]),
            reads=[b_rows], writes=[b_rows], dma_sem="ds_sy")
        em.op("sync", lambda s: s.dma_start(
            out=rows[0:1, 1], in_=p.dram["L0_attn_bv"][None, :]),
            reads=[b_rows], writes=[b_rows], dma_sem="ds_sy")

        for ci in range(8):
            sl = ci % 2
            em.op("sync", lambda s, ci=ci, sl=sl, t0=t0: s.dma_start(
                out=sg[:, sl], in_=sT_d[:, ci, t0:t0 + SEQ]),
                reads=[b_sTd], writes=[b_sg[sl]], dma_sem=f"d_sg{sl}")
            for (wsl, b_W, src, b_src, row, bk0) in [
                    (0, b_W0, ik, b_ik, 0, 0), (1, b_W1, iv, b_iv, 1, 4)]:
                for tc in range(4):
                    bank = p.bank[bk0 + tc]
                    for k in range(8):
                        em.op("tensor", lambda t, wsl=wsl, src=src, ci=ci, k=k, tc=tc, bk0=bk0: t.matmul(
                            p.psum_f[:, bk0 + tc, :], WREG[:, wsl, k, ci * 128:(ci + 1) * 128],
                            src[:, k, tc * 512:(tc + 1) * 512], start=(k == 0), stop=False),
                            reads=[b_W, b_src], writes=[bank])
                    em.op("tensor", lambda t, ci=ci, tc=tc, row=row, bk0=bk0: t.matmul(
                        p.psum_f[:, bk0 + tc, :], rows[0:1, row, ci * 128:(ci + 1) * 128],
                        p.onesrow[0:1, 0:512], start=False, stop=True),
                        reads=[b_rows, p.b_onesrow], writes=[bank])
            em.fence("tensor")
            em.op("vector", lambda e: e.tensor_copy(
                kc.rearrange("p (a b) -> p a b", a=4), p.psum_f[:, 0:4, :]),
                reads=[p.bank[i] for i in range(4)], writes=[b_kc], sig=True)
            em.op("vector", lambda e: e.tensor_copy(
                vc.rearrange("p (a b) -> p a b", a=4), p.psum_f[:, 4:8, :]),
                reads=[p.bank[4 + i] for i in range(4)], writes=[b_vc], sig=True)
            P0, P1, P2, P3, P4 = (PR[:, i, :] for i in range(5))
            B0, B1, B2, B3, B4 = bp
            if ci == 0:
                for b in bp:
                    b.merge_from([b_xs])
            em.op("vector", lambda e, ci=ci: e.scalar_tensor_tensor(
                P0, trep[:], nwpp[:, ci:ci + 1], kc[:], ALU.mult, ALU.add),
                reads=[b_trep, b_nwpp, b_kc], writes=[B0])
            em.op("vector", lambda e: e.tensor_scalar(
                P0, P0, 10.0, -20.0, ALU.min, ALU.max),
                reads=[B0], writes=[B0], sig=True)
            em.op("scalar", lambda e: e.activation(P1, P0, AF.Exp),
                  reads=[B0], writes=[B1], sig=True)
            em.op("vector", lambda e, ci=ci: e.tensor_scalar(
                P2, kc[:], Upp[:, ci:ci + 1], 10.0, ALU.add, ALU.min),
                reads=[b_kc, b_Upp], writes=[B2])
            em.op("vector", lambda e: e.tensor_scalar(
                P2, P2, -20.0, None, ALU.max),
                reads=[B2], writes=[B2], sig=True)
            em.op("scalar", lambda e: e.activation(P3, P2, AF.Exp),
                  reads=[B2], writes=[B3], sig=True)
            em.op("vector", lambda e: e.tensor_tensor(P2, P1, vc[:], ALU.mult),
                  reads=[B1, b_vc], writes=[B2], sig=True)
            em.op("vector", lambda e: e.tensor_tensor_scan(
                P4, P2, P2, 0.0, ALU.add, ALU.bypass),
                reads=[B2], writes=[B4], sig=True)
            em.op("vector", lambda e: e.tensor_tensor_scan(
                P0, P1, P1, 0.0, ALU.add, ALU.bypass),
                reads=[B1], writes=[B0], sig=True)
            em.op("vector", lambda e: e.tensor_tensor(P2, P3, vc[:], ALU.mult),
                  reads=[B3, b_vc], writes=[B2], sig=True)
            em.op("vector", lambda e: e.tensor_tensor(
                P1[:, 1:SEQ], P2[:, 1:SEQ], P4[:, 0:SEQ - 1], ALU.add),
                reads=[B2, B4], writes=[B1])
            em.op("vector", lambda e: e.tensor_copy(P1[:, 0:1], P2[:, 0:1]),
                  reads=[B2], writes=[B1], sig=True)
            em.op("vector", lambda e: e.tensor_tensor(
                P2[:, 1:SEQ], P3[:, 1:SEQ], P0[:, 0:SEQ - 1], ALU.add),
                reads=[B3, B0], writes=[B2])
            em.op("vector", lambda e: e.tensor_copy(P2[:, 0:1], P3[:, 0:1]),
                  reads=[B3], writes=[B2], sig=True)
            em.op("vector", lambda e: e.reciprocal(P3, P2),
                  reads=[B2], writes=[B3], sig=True)
            em.op("vector", lambda e: e.tensor_tensor(P0, P1, P3, ALU.mult),
                  reads=[B1, B3], writes=[B0], sig=True)
            em.op("vector", lambda e, ci=ci, sl=sl: e.tensor_tensor(
                zc[:, sl], P0, sg[:, sl], ALU.mult),
                reads=[B0, b_sg[sl]], writes=[b_zc[sl]], sig=True)
            em.op("sync", lambda s, ci=ci, sl=sl, t0=t0: s.dma_start(
                out=zT_d[:, ci, t0:t0 + SEQ], in_=zc[:, sl]),
                reads=[b_zc[sl]], writes=[b_zTd], dma_sem=f"d_zst{sl}")

        em.op("sync", lambda s: s.dma_start(out=WREG[:, 0], in_=wT_d[3]),
              reads=[b_wTd, b_W0], writes=[b_W0], dma_sem="d_wr0")
        em.op("sync", lambda s: s.dma_start(
            out=rows[0:1, 0], in_=p.dram["L0_attn_bo"][None, :]),
            reads=[b_rows], writes=[b_rows], dma_sem="ds_sy")
        tails = dict(
            ao=sb.overlay(pr_off, [128, 2, 2, 512], F32), b_ao=[Buf(), Buf()],
            sq=sb.overlay(pr_off + 8192, [128, 2, C], F32), b_sq=[Buf(), Buf()],
            sm=sb.overlay(pr_off + 16384, [128, 2, 8], F32), b_sm=[Buf(), Buf()],
            xo=sb.overlay(pr_off + 16448, [128, 2, C], F32), b_xo=[Buf(), Buf()],
        )
        for bb in tails["b_ao"] + tails["b_sq"] + tails["b_sm"] + tails["b_xo"]:
            bb.merge_from(bp)
        zt = sb.overlay(pr_off + 24640, [128, 2, 8, 128], BF16)
        b_zt = [Buf(), Buf()]
        for bb in b_zt:
            bb.merge_from(bp)
        for lt in range(16):
            tt = seq * 16 + lt
            sl = lt % 2
            em.op("sync", lambda s, lt=lt, sl=sl, t0=t0: s.dma_start(
                out=zt[:, sl], in_=zT_d[:, :, t0 + lt * 128:t0 + (lt + 1) * 128]),
                reads=[b_zTd], writes=[b_zt[sl]], dma_sem=f"d_zl{sl}")
            banks = [p.bank[2 * sl], p.bank[2 * sl + 1]]
            for h in range(2):
                for k in range(8):
                    em.op("tensor", lambda t, sl=sl, h=h, k=k: t.matmul(
                        p.psum_f[:, 2 * sl + h, :], zt[:, sl, k],
                        WREG[:, 0, k, h * 512:(h + 1) * 512], start=(k == 0), stop=False),
                        reads=[b_zt[sl], b_W0], writes=[banks[h]])
                em.op("tensor", lambda t, sl=sl, h=h: t.matmul(
                    p.psum_f[:, 2 * sl + h, :], p.onesrow[0:1, 0:128],
                    rows[0:1, 0, h * 512:(h + 1) * 512], start=False, stop=True),
                    reads=[p.b_onesrow, b_rows], writes=[banks[h]])
            em.fence("tensor")
            sandwich_tail_tile(
                p, tt, p.psum_f[:, 2 * sl:2 * sl + 2, :], banks,
                x_in, b_x, p.dram["d_res"], p.b_res, tails)
        em.play_block()
    sb.reset()



def ln_pass(p: P, src_dram, b_src, dst_dram, b_dst):
    em, sb = p.em, p.sb
    save = sb.off
    xin = sb.take([128, 2, C], F32)
    sq = sb.take([128, 2, C], F32)
    sm = sb.take([128, 2, 8], F32)
    b_xin = [Buf(), Buf()]; b_sq = [Buf(), Buf()]; b_sm = [Buf(), Buf()]
    for i in range(NT):
        sl = i % 2
        em.op("sync", lambda s, i=i, sl=sl: s.dma_start(
            out=xin[:, sl], in_=src_dram[i * 128:(i + 1) * 128, :]),
            reads=[b_src], writes=[b_xin[sl]], dma_sem=f"d_lp{sl}")
        emit_ln_tile(p, None, xin[:, sl], b_xin[sl], sq[:, sl], b_sq[sl],
                     sm[:, sl], b_sm[sl])
        em.op("vector", lambda e, sl=sl: e.tensor_scalar(
            xin[:, sl], xin[:, sl], sm[:, sl, 2:3], sm[:, sl, 6:7],
            ALU.subtract, ALU.mult),
            reads=[b_xin[sl], b_sm[sl]], writes=[b_xin[sl]], sig=True)
        em.op("sync", lambda s, i=i, sl=sl: s.dma_start(
            out=dst_dram[i * 128:(i + 1) * 128, :], in_=xin[:, sl]),
            reads=[b_xin[sl]], writes=[b_dst], dma_sem=f"d_lpo{sl}")
    em.play_block()
    sb.reset(save)


def ln_add_pass(p: P, f_dram, b_f, res_dram, b_res):
    em, sb = p.em, p.sb
    save = sb.off
    xin = sb.take([128, 2, C], F32)
    xo = sb.take([128, 2, C], F32)
    sq = sb.take([128, 2, C], F32)
    sm = sb.take([128, 2, 8], F32)
    b_xin = [Buf(), Buf()]; b_xo = [Buf(), Buf()]
    b_sq = [Buf(), Buf()]; b_sm = [Buf(), Buf()]
    for i in range(NT):
        sl = i % 2
        em.op("sync", lambda s, i=i, sl=sl: s.dma_start(
            out=xin[:, sl], in_=f_dram[i * 128:(i + 1) * 128, :]),
            reads=[b_f], writes=[b_xin[sl]], dma_sem=f"d_la{sl}")
        em.op("sync", lambda s, i=i, sl=sl: s.dma_start(
            out=xo[:, sl], in_=res_dram[i * 128:(i + 1) * 128, :]),
            reads=[b_res], writes=[b_xo[sl]], dma_sem=f"d_lb{sl}")
        emit_ln_tile(p, None, xin[:, sl], b_xin[sl], sq[:, sl], b_sq[sl],
                     sm[:, sl], b_sm[sl])
        em.op("vector", lambda e, sl=sl: e.tensor_scalar(
            xin[:, sl], xin[:, sl], sm[:, sl, 2:3], sm[:, sl, 6:7],
            ALU.subtract, ALU.mult),
            reads=[b_xin[sl], b_sm[sl]], writes=[b_xin[sl]])
        em.op("vector", lambda e, sl=sl: e.tensor_tensor(
            xin[:, sl], xin[:, sl], xo[:, sl], ALU.add),
            reads=[b_xin[sl], b_xo[sl]], writes=[b_xin[sl]], sig=True)
        em.op("sync", lambda s, i=i, sl=sl: s.dma_start(
            out=res_dram[i * 128:(i + 1) * 128, :], in_=xin[:, sl]),
            reads=[b_xin[sl]], writes=[b_res], dma_sem=f"d_lao{sl}")
    em.play_block()
    sb.reset(save)


def _seg(ap, h=2, u=32):
    return ap.rearrange("p (g h u) -> p g h u", h=h, u=u)


def phase2(p: P, dbg_stop=None):
    em, sb = p.em, p.sb
    HEADS, DH, WIN = 8, 64, 512
    NW = SEQ // WIN
    d_y = p.dram["d_y"]; b_y = Buf("d_y")
    xlnT_d = p.dram["d_xlnT"].rearrange("(k p) t -> p k t", p=128)

    ln_pass(p, p.dram["d_res"], p.b_res, d_y, b_y)

    for l in range(2):
        layer0 = sb.off
        wqkvb = sb.take([128, 8, 1536], BF16); b_wqkv = Buf("wqkv")
        wob = sb.take([128, 4, 1024], BF16); b_wo = Buf("wob")
        qsr = sb.take([128, 512], F32); b_qsr = Buf("qsr")
        ksr = sb.take([128, 512], F32); b_ksr = Buf("ksr")
        srow = sb.take([1, 2, 512], F32); b_srow = Buf("srow")
        mtri2 = sb.take([128, 4, 512], BF16); b_mtri2 = Buf("mtri2")
        ftri2 = sb.take([128, 512], BF16); b_ftri2 = Buf("ftri2")
        idneg2 = sb.take([128, 128], BF16); b_idneg2 = Buf("idneg2")
        prep_end = sb.off

        em.op("gpsimd", lambda g, l=l: g.dma_start(
            out=wqkvb[:], in_=p.dram["L0_ffn_wqkv"][l].rearrange("(k p) n -> p k n", p=128)),
            reads=[b_wqkv], writes=[b_wqkv], dma_sem="ds_gp")
        em.op("gpsimd", lambda g, l=l: g.dma_start(
            out=wob[:], in_=p.dram["L0_ffn_wo"][l].rearrange("(k p) n -> p k n", p=128)),
            reads=[b_wo], writes=[b_wo], dma_sem="ds_gp")
        em.op("gpsimd", lambda g: g.dma_start(
            out=mtri2[:], in_=p.dram["c_mtri"].rearrange("o q t -> q o t")),
            reads=[b_mtri2], writes=[b_mtri2], dma_sem="ds_gp")
        em.op("gpsimd", lambda g: g.dma_start(
            out=ftri2[:], in_=p.dram["c_ftri"][:]),
            reads=[b_ftri2], writes=[b_ftri2], dma_sem="ds_gp")
        em.op("vector", lambda e: e.tensor_scalar(
            idneg2[:], p.identb[:], -1e30, None, ALU.mult),
            reads=[p.b_identb, b_idneg2], writes=[b_idneg2], sig=True)
        for s8 in range(8):
            em.op("sync", lambda s, l=l, s8=s8: s.dma_start(
                out=srow[0:1, 0, s8 * 64:(s8 + 1) * 64],
                in_=p.dram["L0_ffn_q_scale"][l][None, :]),
                reads=[b_srow], writes=[b_srow], dma_sem="ds_sy")
            em.op("sync", lambda s, l=l, s8=s8: s.dma_start(
                out=srow[0:1, 1, s8 * 64:(s8 + 1) * 64],
                in_=p.dram["L0_ffn_k_scale"][l][None, :]),
                reads=[b_srow], writes=[b_srow], dma_sem="ds_sy")
        em.op("vector", lambda e: e.tensor_scalar(
            srow[0:1, 0], srow[0:1, 0], 8.0, None, ALU.mult),
            reads=[b_srow], writes=[b_srow], sig=True)
        for (dst, b_dst, r) in [(qsr, b_qsr, 0), (ksr, b_ksr, 1)]:
            em.op("tensor", lambda t, r=r: t.matmul(
                p.psum_f[:, 6, :], p.onesrow[0:1, 0:128], srow[0:1, r],
                start=True, stop=True),
                reads=[b_srow, p.b_onesrow], writes=[p.bank[6]], sig=True)
            em.op("vector", lambda e, dst=dst: e.tensor_copy(dst[:], p.psum_f[:, 6, :]),
                  reads=[p.bank[6]], writes=[b_dst], sig=True)
        em.play_block()

        ln_transpose_store(p, d_y, b_y, ln=False, shift=True)

        for seq in range(NSEQ):
            sb.reset(prep_end)
            t0 = seq * SEQ
            hT = sb.take([128, 8, 2, 128], BF16); b_hT = [Buf(), Buf()]
            rot = sb.take([128, 2, 4, 512], F32); b_rot = [Buf(), Buf()]
            qf = sb.take([128, 2, 512], F32); b_qf = [Buf(), Buf()]
            kf = sb.take([128, 2, 512], F32); b_kf = [Buf(), Buf()]
            wA = sb.take([128, 2, 512], F32); b_wA = [Buf(), Buf()]
            wB = sb.take([128, 2, 512], F32); b_wB = [Buf(), Buf()]
            wC = sb.take([128, 2, 512], F32); b_wC = [Buf(), Buf()]
            n2 = sb.take([128, 2, 2, 8], F32); b_n2 = [Buf(), Buf()]
            xbt = sb.take([128, 2, 2, 512], BF16); b_xbt = [Buf(), Buf()]
            qTt = sb.take([128, 4, SEQ], BF16); b_qT = Buf("qTt")
            kTt = sb.take([128, 4, SEQ], BF16); b_kT = Buf("kTt")
            Vag = sb.take([128, 16, 8, 65], BF16); b_V = Buf("Vag")
            stA_end = sb.off

            em.op("vector", lambda e: e.memset(Vag[:, :, :, 64:65], 1.0),
                  writes=[b_V], sig=True)

            for lt in range(16):
                sl = lt % 2
                tg = t0 + lt * 128
                banksA = [p.bank[3 * sl + j] for j in range(3)]
                em.op("sync", lambda s, sl=sl, tg=tg: s.dma_start(
                    out=hT[:, :, sl, :], in_=xlnT_d[:, :, tg:tg + 128]),
                    reads=[p.b_xlnTd], writes=[b_hT[sl]], dma_sem=f"d_h{sl}")
                em.op("sync", lambda s, sl=sl, lt=lt: s.dma_start(
                    out=rot[:, sl], in_=p.dram["c_rot"][:, lt * 128 + (0 if True else 0):(lt + 1) * 128, :].rearrange("j t n -> t j n"),
                    allow_slow_non_contiguous=True),
                    reads=[b_rot[sl]], writes=[b_rot[sl]], dma_sem=f"d_r{sl}")
                for n3 in range(3):
                    for k in range(8):
                        em.op("tensor", lambda t, sl=sl, n3=n3, k=k: t.matmul(
                            p.psum_f[:, 3 * sl + n3, :], hT[:, k, sl, :],
                            wqkvb[:, k, n3 * 512:(n3 + 1) * 512],
                            start=(k == 0), stop=(k == 7)),
                            reads=[b_hT[sl], b_wqkv], writes=[banksA[n3]])
                em.fence("tensor")
                em.op("vector", lambda e, sl=sl: e.tensor_copy(qf[:, sl], p.psum_f[:, 3 * sl, :]),
                      reads=[banksA[0]], writes=[b_qf[sl]], sig=True)
                em.op("vector", lambda e, sl=sl: e.tensor_copy(kf[:, sl], p.psum_f[:, 3 * sl + 1, :]),
                      reads=[banksA[1]], writes=[b_kf[sl]], sig=True)
                em.op("vector", lambda e, sl=sl, lt=lt: e.tensor_copy(
                    Vag[:, lt, :, 0:64],
                    p.psum_f[:, 3 * sl + 2, :].rearrange("p (h d) -> p h d", h=8)),
                    reads=[banksA[2]], writes=[b_V], sig=True)
                for (xf, b_xf, scl, b_scl, ri, ns) in [
                        (qf, b_qf, qsr, b_qsr, 0, 0),
                        (kf, b_kf, ksr, b_ksr, 2, 1)]:
                    em.op("gpsimd", lambda e, sl=sl, xf=xf: e.tensor_tensor(
                        wA[:, sl], xf[:, sl], xf[:, sl], ALU.mult),
                        reads=[b_xf[sl]], writes=[b_wA[sl]], sig=True)
                    em.op("vector", lambda e, sl=sl, ns=ns: e.tensor_reduce(
                        n2[:, sl, ns], wA[:, sl].rearrange("p (g u) -> p g u", u=64),
                        axis=AXX, op=ALU.add),
                        reads=[b_wA[sl]], writes=[b_n2[sl]], sig=True)
                    em.op("scalar", lambda e, sl=sl, ns=ns: e.activation(
                        n2[:, sl, ns], n2[:, sl, ns], AF.Sqrt),
                        reads=[b_n2[sl]], writes=[b_n2[sl]], sig=True)
                    em.op("vector", lambda e, sl=sl, ns=ns: e.tensor_scalar(
                        n2[:, sl, ns], n2[:, sl, ns], 1e-12, None, ALU.max),
                        reads=[b_n2[sl]], writes=[b_n2[sl]])
                    em.op("vector", lambda e, sl=sl, ns=ns: e.reciprocal(
                        n2[:, sl, ns], n2[:, sl, ns]),
                        reads=[b_n2[sl]], writes=[b_n2[sl]], sig=True)
                    em.op("vector", lambda e, sl=sl, xf=xf, ns=ns: e.tensor_tensor(
                        wA[:, sl].rearrange("p (g u) -> p g u", u=64),
                        xf[:, sl].rearrange("p (g u) -> p g u", u=64),
                        n2[:, sl, ns].rearrange("p (g u) -> p g u", u=1).broadcast_to([128, 8, 64]),
                        ALU.mult),
                        reads=[b_xf[sl], b_n2[sl]], writes=[b_wA[sl]], sig=True)
                    em.op("vector", lambda e, sl=sl, scl=scl: e.tensor_tensor(
                        wA[:, sl], wA[:, sl], scl[:], ALU.mult),
                        reads=[b_wA[sl], b_scl], writes=[b_wA[sl]], sig=True)
                    em.op("vector", lambda e, sl=sl, ri=ri: e.tensor_tensor(
                        wB[:, sl], wA[:, sl], rot[:, sl, ri, :], ALU.mult),
                        reads=[b_wA[sl], b_rot[sl]], writes=[b_wB[sl]], sig=True)
                    em.op("vector", lambda e, sl=sl, ri=ri: e.tensor_tensor(
                        _seg(wC[:, sl])[:, :, 0, :], _seg(wA[:, sl])[:, :, 1, :],
                        _seg(rot[:, sl, ri + 1, :])[:, :, 0, :], ALU.mult),
                        reads=[b_wA[sl], b_rot[sl]], writes=[b_wC[sl]])
                    em.op("vector", lambda e, sl=sl, ri=ri: e.tensor_tensor(
                        _seg(wC[:, sl])[:, :, 1, :], _seg(wA[:, sl])[:, :, 0, :],
                        _seg(rot[:, sl, ri + 1, :])[:, :, 1, :], ALU.mult),
                        reads=[b_wA[sl], b_rot[sl]], writes=[b_wC[sl]], sig=True)
                    em.op("vector", lambda e, sl=sl, ns=ns: e.tensor_tensor(
                        xbt[:, sl, ns], wB[:, sl], wC[:, sl], ALU.add),
                        reads=[b_wB[sl], b_wC[sl]], writes=[b_xbt[sl]], sig=True)
                bt = p.bank[6 + sl % 2] if False else p.bank[6]
                for (ns, dstT, b_dstT) in [(0, qTt, b_qT), (1, kTt, b_kT)]:
                    bankT = p.bank[6 + ns]
                    for j in range(4):
                        em.op("tensor", lambda t, sl=sl, ns=ns, j=j: t.matmul(
                            p.psum_f[:, 6 + ns, j * 128:(j + 1) * 128],
                            xbt[:, sl, ns, j * 128:(j + 1) * 128], p.identb[:],
                            start=True, stop=True),
                            reads=[b_xbt[sl], p.b_identb], writes=[bankT])
                    em.fence("tensor")
                    em.op("vector", lambda e, ns=ns, lt=lt, dstT=dstT: e.tensor_copy(
                        dstT[:, :, lt * 128:(lt + 1) * 128],
                        p.psum_f[:, 6 + ns, :].rearrange("p (j f) -> p j f", j=4)),
                        reads=[bankT], writes=[b_dstT], sig=True)

            PT = sb.take([128, 6, 512], BF16)
            b_PTs = [Buf(f"PT2{i}") for i in range(6)]
            rr = sb.take([128, 2, 512], F32); b_rr = Buf("rr")
            rrep = sb.take([128, 2, 512], F32); b_rrep = Buf("rrep")
            mx = sb.take([128, 8], F32); b_mx = Buf("mx")
            oTall = sb.take([128, 4, SEQ], BF16); b_oT = Buf("oTall")
            for kt in range(4):
                for w in range(NW):
                    jlist = list(range(max(0, 4 * w - 4), 4 * w + 4))
                    nst = len(jlist)
                    s0 = jlist[0] * 128
                    ns_ = nst * 128
                    nch = (ns_ + 511) // 512
                    for tt4 in range(4):
                        base = 4 * (tt4 % 2)
                        for r in range(2):
                            pb = 64 * r
                            for sc in range(nch):
                                nsc = min(512, ns_ - sc * 512)
                                bk = base + r * nch + sc
                                em.op("tensor", lambda t, w=w, tt4=tt4, sc=sc, nsc=nsc, pb=pb, kt=kt, s0=s0, bk=bk: t.matmul(
                                    p.psum_f[:, bk, 0:nsc],
                                    qTt[pb:pb + 64, kt, w * 512 + tt4 * 128:w * 512 + (tt4 + 1) * 128],
                                    kTt[pb:pb + 64, kt, s0 + sc * 512:s0 + sc * 512 + nsc],
                                    start=True, stop=True),
                                    reads=[b_qT, b_kT], writes=[p.bank[bk]],
                                    sig=(r == 1 and sc == nch - 1))
                        em.op("vector", lambda e, tt4=tt4, base=base, nch=nch: e.reduce_max(
                            mx[:, tt4:tt4 + 1],
                            p.psum_f[:, base:base + 2 * nch, :].rearrange("p a b -> p (a b)"),
                            axis=AXX),
                            reads=[p.bank[base + j] for j in range(2 * nch)],
                            writes=[b_mx], sig=True)
                    em.op("vector", lambda e: e.reduce_max(
                        mx[:, 4:5], mx[:, 0:4], axis=AXX),
                        reads=[b_mx], writes=[b_mx], sig=True)
                    em.op("tensor", lambda t: t.matmul(
                        p.psum_f[0:1, 5, 0:128], mx[:, 4:5], p.identf[:],
                        start=True, stop=True),
                        reads=[b_mx, p.b_identf], writes=[p.bank[5]], sig=True)
                    em.op("vector", lambda e: e.reduce_max(
                        mx[0:1, 5:6], p.psum_f[0:1, 5, 0:128], axis=AXX),
                        reads=[p.bank[5]], writes=[b_mx])
                    em.op("vector", lambda e: e.tensor_scalar(
                        mx[0:1, 6:7], mx[0:1, 5:6], -1.0, None, ALU.mult),
                        reads=[b_mx], writes=[b_mx], sig=True)
                    em.op("tensor", lambda t: t.matmul(
                        p.psum_f[:, 5, 0:1], p.one128[0:1, :], mx[0:1, 6:7],
                        start=True, stop=True),
                        reads=[b_mx, p.b_one128], writes=[p.bank[5]], sig=True)
                    em.op("vector", lambda e: e.tensor_copy(
                        mx[:, 7:8], p.psum_f[:, 5, 0:1]),
                        reads=[p.bank[5]], writes=[b_mx], sig=True)
                    nb = (nst + 2) // 3

                    def jr2(b_):
                        return range(3 * b_, min(3 * b_ + 3, nst))

                    for b_ in range(nb):
                        for idx in jr2(b_):
                            j = jlist[idx]
                            o = j - 4 * w
                            masked = (o >= 0) or (o == -4)
                            for r in range(2):
                                pb = 64 * r
                                sbk = 3 * r + idx % 3
                                em.op("tensor", lambda t, w=w, j=j, pb=pb, kt=kt, sbk=sbk, masked=masked: t.matmul(
                                    p.psum_f[:, sbk, :],
                                    kTt[pb:pb + 64, kt, j * 128:(j + 1) * 128],
                                    qTt[pb:pb + 64, kt, w * 512:(w + 1) * 512],
                                    start=True, stop=not masked),
                                    reads=[b_kT, b_qT], writes=[p.bank[sbk]],
                                    sig=not masked)
                        for idx in jr2(b_):
                            j = jlist[idx]
                            o = j - 4 * w
                            for r in range(2):
                                sbk = 3 * r + idx % 3
                                if o >= 0:
                                    em.op("tensor", lambda t, sbk=sbk, o=o: t.matmul(
                                        p.psum_f[:, sbk, :], idneg2[:], mtri2[:, o, :],
                                        start=False, stop=True),
                                        reads=[b_idneg2, b_mtri2], writes=[p.bank[sbk]],
                                        sig=True)
                                elif o == -4:
                                    em.op("tensor", lambda t, sbk=sbk: t.matmul(
                                        p.psum_f[:, sbk, :], idneg2[:], ftri2[:],
                                        start=False, stop=True),
                                        reads=[b_idneg2, b_ftri2], writes=[p.bank[sbk]],
                                        sig=True)
                        if b_ > 0:
                            for idx in jr2(b_ - 1):
                                j = jlist[idx]
                                for r in range(2):
                                    h = 2 * kt + r
                                    sl = 3 * r + idx % 3
                                    em.op("tensor", lambda t, j=j, h=h, r=r, sl=sl, idx=idx, n=nst: t.matmul(
                                        p.psum_f[0:65, 6 + r, :], Vag[:, j, h, :],
                                        PT[:, sl, :],
                                        start=(idx == 0), stop=(idx == n - 1)),
                                        reads=[b_V, b_PTs[sl]], writes=[p.bank[6 + r]],
                                        sig=(r == 1 and idx == 3 * b_ - 1))
                        for idx in jr2(b_):
                            for r in range(2):
                                sbk = 3 * r + idx % 3
                                em.op("scalar", lambda e, sbk=sbk: e.activation(
                                    PT[:, sbk, :], p.psum_f[:, sbk, :], AF.Exp,
                                    bias=mx[:, 7:8]),
                                    reads=[p.bank[sbk], b_mx], writes=[b_PTs[sbk]],
                                    sig=True)
                    for idx in jr2(nb - 1):
                        j = jlist[idx]
                        for r in range(2):
                            h = 2 * kt + r
                            sl = 3 * r + idx % 3
                            em.op("tensor", lambda t, j=j, h=h, r=r, sl=sl, idx=idx, n=nst: t.matmul(
                                p.psum_f[0:65, 6 + r, :], Vag[:, j, h, :],
                                PT[:, sl, :],
                                start=(idx == 0), stop=(idx == n - 1)),
                                reads=[b_V, b_PTs[sl]], writes=[p.bank[6 + r]],
                                sig=(r == 1 and idx == nst - 1))
                    for r in range(2):
                        pb = 64 * r
                        em.op("vector", lambda e, r=r: e.reciprocal(
                            rr[64:65, r, :], p.psum_f[64:65, 6 + r, :]),
                            reads=[p.bank[6 + r]], writes=[b_rr], sig=True)
                        bb = p.bank[3 * r]
                        em.op("tensor", lambda t, r=r: t.matmul(
                            p.psum_f[:, 3 * r, :], p.one128[64:65, :], rr[64:65, r, :],
                            start=True, stop=True),
                            reads=[b_rr, p.b_one128], writes=[bb], sig=True)
                        em.op("vector", lambda e, r=r: e.tensor_copy(
                            rrep[:, r, :], p.psum_f[:, 3 * r, :]),
                            reads=[bb], writes=[b_rrep], sig=True)
                        em.op("vector", lambda e, r=r, kt=kt, w=w, pb=pb: e.tensor_tensor(
                            oTall[pb:pb + 64, kt, w * 512:(w + 1) * 512],
                            p.psum_f[0:64, 6 + r, :], rrep[0:64, r, :], ALU.mult),
                            reads=[p.bank[6 + r], b_rrep], writes=[b_oT], sig=True)

            yup = sb.take([128, 2, C], F32); b_yup = [Buf(), Buf()]
            for lt in range(16):
                sl = lt % 2
                tg = t0 + lt * 128
                banks = [p.bank[2 * sl], p.bank[2 * sl + 1]]
                em.op("sync", lambda s, sl=sl, tg=tg: s.dma_start(
                    out=yup[:, sl], in_=d_y[tg:tg + 128, :]),
                    reads=[b_y], writes=[b_yup[sl]], dma_sem=f"d_yu{sl}")
                for hh in range(2):
                    for k in range(4):
                        em.op("tensor", lambda t, sl=sl, hh=hh, k=k, lt=lt: t.matmul(
                            p.psum_f[:, 2 * sl + hh, :],
                            oTall[:, k, lt * 128:(lt + 1) * 128],
                            wob[:, k, hh * 512:(hh + 1) * 512],
                            start=(k == 0), stop=(k == 3)),
                            reads=[b_oT, b_wo], writes=[banks[hh]])
                em.fence("tensor")
                em.op("vector", lambda e, sl=sl: e.tensor_tensor(
                    yup[:, sl].rearrange("p (a b) -> p a b", a=2),
                    yup[:, sl].rearrange("p (a b) -> p a b", a=2),
                    p.psum_f[:, 2 * sl:2 * sl + 2, :], ALU.add),
                    reads=[b_yup[sl]] + banks, writes=[b_yup[sl]], sig=True)
                em.op("sync", lambda s, sl=sl, tg=tg: s.dma_start(
                    out=d_y[tg:tg + 128, :], in_=yup[:, sl]),
                    reads=[b_yup[sl]], writes=[b_y], dma_sem=f"d_yo{sl}")
            em.play_block()
        if dbg_stop == "l0_attn":
            sb.reset()
            return

        sb.reset(layer0)
        ln_transpose_store(p, d_y, b_y, ln=True, shift=True)

        HID = 2730
        NCH = 22
        w1b = sb.take([128, 8, 5460], BF16); b_w1b = Buf("w1b")
        w2b = sb.take([128, 22, 1024], BF16); b_w2b = Buf("w2b")
        h2T = sb.take([128, 22, 512], BF16); b_h2T = Buf("h2T")
        hTc = sb.take([128, 8, 2, 512], BF16); b_hTc = [Buf(), Buf()]
        ga = sb.take([128, 2, 512], F32); b_ga = [Buf(), Buf()]
        b1pp = sb.take([128, 44], F32); b_b1pp = Buf("b1pp")
        lgpp = sb.take([128, 8], F32); b_lgpp = Buf("lgpp")
        lbpp = sb.take([128, 8], BF16); b_lbpp = Buf("lbpp")
        b2row = sb.take([1, 1024], F32); b_b2row = Buf("b2row")
        yffn = sb.take([128, 2, C], F32); b_yffn = [Buf(), Buf()]

        em.op("gpsimd", lambda g, l=l: g.dma_start(
            out=w1b[:], in_=p.dram["L0_ffn_w1"][l].rearrange("(k p) n -> p k n", p=128)),
            writes=[b_w1b], dma_sem="ds_gp")
        em.op("gpsimd", lambda g, l=l: g.dma_start(
            out=w2b[:, 0:21, :],
            in_=p.dram["L0_ffn_w2"][l][0:2688].rearrange("(k p) n -> p k n", p=128)),
            writes=[b_w2b], dma_sem="ds_gp")
        em.op("gpsimd", lambda g, l=l: g.dma_start(
            out=w2b[0:42, 21, :], in_=p.dram["L0_ffn_w2"][l][2688:2730]),
            writes=[b_w2b], dma_sem="ds_gp")
        em.op("sync", lambda s, l=l: s.dma_start(
            out=lgpp[:], in_=p.dram["L0_ffn_ln_g"][l].rearrange("(k p) -> p k", p=128),
            allow_slow_non_contiguous=True),
            writes=[b_lgpp], dma_sem="ds_sy")
        em.op("gpsimd", lambda g, l=l: g.dma_start(
            out=lbpp[:], in_=p.dram["L0_ffn_ln_b"][l].rearrange("(k p) -> p k", p=128),
            allow_slow_non_contiguous=True),
            writes=[b_lbpp], dma_sem="ds_gp")
        em.op("sync", lambda s, l=l: s.dma_start(
            out=b2row[:], in_=p.dram["L0_ffn_b2"][l][None, :]),
            writes=[b_b2row], dma_sem="ds_sy")
        em.op("vector", lambda e: e.memset(b1pp[:], 0.0), writes=[b_b1pp], sig=True)
        for m in range(44):
            half, mm = divmod(m, 22)
            c0 = half * HID + mm * 128
            n = min(128, half * HID + HID - c0)
            em.op("sync", lambda s, l=l, m=m, c0=c0, n=n: s.dma_start(
                out=b1pp[0:n, m:m + 1], in_=p.dram["L0_ffn_b1"][l][c0:c0 + n][:, None],
                allow_slow_non_contiguous=True),
                reads=[b_b1pp], writes=[b_b1pp], dma_sem="ds_sy")
        for m in range(44):
            half, mm = divmod(m, 22)
            c0 = half * HID + mm * 128
            n = min(128, half * HID + HID - c0)
            for k in range(8):
                em.op("tensor", lambda t, m=m, k=k, c0=c0, n=n: t.matmul(
                    p.psum_f[0:n, 7, 0:1], w1b[:, k, c0:c0 + n], lbpp[:, k:k + 1],
                    start=(k == 0), stop=(k == 7)),
                    reads=[b_w1b, b_lbpp], writes=[p.bank[7]], sig=(k == 7))
            em.op("vector", lambda e, m=m, n=n: e.tensor_tensor(
                b1pp[0:n, m:m + 1], b1pp[0:n, m:m + 1], p.psum_f[0:n, 7, 0:1], ALU.add),
                reads=[p.bank[7], b_b1pp], writes=[b_b1pp], sig=True)
        for k in range(8):
            em.op("vector", lambda e, k=k: e.tensor_scalar(
                w1b[:, k, :], w1b[:, k, :], lgpp[:, k:k + 1], None, ALU.mult),
                reads=[b_w1b, b_lgpp], writes=[b_w1b], sig=True)

        for tc in range(8):
            sl = tc % 2
            tg = tc * 512
            em.op("sync", lambda s, sl=sl, tg=tg: s.dma_start(
                out=hTc[:, :, sl, :], in_=xlnT_d[:, :, tg:tg + 512]),
                reads=[p.b_xlnTd], writes=[b_hTc[sl]], dma_sem=f"d_hc{sl}")
            for mm in range(NCH):
                n = 128 if mm < 21 else 42
                cp = mm % 2
                bka, bkg = p.bank[2 * cp], p.bank[2 * cp + 1]
                for k in range(8):
                    em.op("tensor", lambda t, cp=cp, mm=mm, k=k, n=n, sl=sl: t.matmul(
                        p.psum_f[0:n, 2 * cp, :], w1b[:, k, mm * 128:mm * 128 + n],
                        hTc[:, k, sl, :], start=(k == 0), stop=(k == 7)),
                        reads=[b_w1b, b_hTc[sl]], writes=[bka])
                for k in range(8):
                    em.op("tensor", lambda t, cp=cp, mm=mm, k=k, n=n, sl=sl: t.matmul(
                        p.psum_f[0:n, 2 * cp + 1, :],
                        w1b[:, k, HID + mm * 128:HID + mm * 128 + n],
                        hTc[:, k, sl, :], start=(k == 0), stop=(k == 7)),
                        reads=[b_w1b, b_hTc[sl]], writes=[bkg], sig=(k == 7))
                em.op("scalar", lambda e, cp=cp, mm=mm, n=n, sl=sl: e.activation(
                    ga[:, cp], p.psum_f[:, 2 * cp + 1, :], AF.Gelu,
                    bias=b1pp[:, 22 + mm:23 + mm]),
                    reads=[bkg, b_b1pp], writes=[b_ga[cp]], sig=True)
                em.op("vector", lambda e, cp=cp, mm=mm, n=n, sl=sl: e.scalar_tensor_tensor(
                    h2T[0:n, mm, :], p.psum_f[0:n, 2 * cp, :], b1pp[0:n, mm:mm + 1],
                    ga[0:n, cp], ALU.add, ALU.mult),
                    reads=[bka, b_b1pp, b_ga[cp]], writes=[b_h2T], sig=True)
            for lt4 in range(4):
                tg2 = tg + lt4 * 128
                sl2 = lt4 % 2
                banks = [p.bank[4 + 2 * sl2], p.bank[5 + 2 * sl2]]
                em.op("sync", lambda s, sl2=sl2, tg2=tg2: s.dma_start(
                    out=yffn[:, sl2], in_=d_y[tg2:tg2 + 128, :]),
                    reads=[b_y], writes=[b_yffn[sl2]], dma_sem=f"d_yf{sl2}")
                for hh in range(2):
                    for kk in range(NCH):
                        n = 128 if kk < 21 else 42
                        em.op("tensor", lambda t, sl2=sl2, hh=hh, kk=kk, lt4=lt4, n=n: t.matmul(
                            p.psum_f[:, 4 + 2 * sl2 + hh, :],
                            h2T[0:n, kk, lt4 * 128:(lt4 + 1) * 128],
                            w2b[0:n, kk, hh * 512:(hh + 1) * 512],
                            start=(kk == 0), stop=False),
                            reads=[b_h2T, b_w2b], writes=[banks[hh]])
                    em.op("tensor", lambda t, sl2=sl2, hh=hh: t.matmul(
                        p.psum_f[:, 4 + 2 * sl2 + hh, :], p.onesrow[0:1, 0:128],
                        b2row[0:1, hh * 512:(hh + 1) * 512], start=False, stop=True),
                        reads=[p.b_onesrow, b_b2row], writes=[banks[hh]],
                        sig=(hh == 1))
                em.op("vector", lambda e, sl2=sl2: e.tensor_tensor(
                    yffn[:, sl2].rearrange("p (a b) -> p a b", a=2),
                    yffn[:, sl2].rearrange("p (a b) -> p a b", a=2),
                    p.psum_f[:, 4 + 2 * sl2:6 + 2 * sl2, :], ALU.add),
                    reads=[b_yffn[sl2]] + banks, writes=[b_yffn[sl2]], sig=True)
                em.op("sync", lambda s, sl2=sl2, tg2=tg2: s.dma_start(
                    out=d_y[tg2:tg2 + 128, :], in_=yffn[:, sl2]),
                    reads=[b_yffn[sl2]], writes=[b_y], dma_sem=f"d_yfo{sl2}")
        em.play_block()
        sb.reset(layer0)
        if dbg_stop == "l0_ffn":
            sb.reset()
            return

    ln_add_pass(p, d_y, b_y, p.dram["d_res"], p.b_res)


def phase3(p: P):
    em, sb = p.em, p.sb
    H3, DH = 16, 64
    d_res = p.dram["d_res"]
    xlnT_d = p.dram["d_xlnT"].rearrange("(k p) t -> p k t", p=128)

    ln_transpose_store(p, d_res, p.b_res, ln=True, shift=False)

    wab = sb.take([128, 8, 3072], BF16); b_wab = Buf("wab")
    wpb = sb.take([128, 8, 1024], BF16); b_wpb = Buf("wpb")
    bapp = sb.take([128, 24], F32); b_bapp = Buf("bapp")
    bprow = sb.take([1, 1024], F32); b_bprow = Buf("bprow")
    bvrep = sb.take([128, 1024], F32); b_bvrep = Buf("bvrep")
    alpp = sb.take([128, 16, 16], F32); b_alpp = Buf("alpp")
    mtri = sb.take([128, 4, 512], BF16); b_mtri = Buf("mtri3")
    idneg = sb.take([128, 128], BF16); b_idneg = Buf("idneg3")
    onesb = sb.take([128, 128], BF16); b_onesb = Buf("onesb3")
    kT = sb.take([128, 8, SEQ], BF16); b_kT = Buf("kT3")
    Vag = sb.take([128, 16, 16, 65], BF16); b_V = Buf("Vag3")
    qTc = sb.take([128, 8, 512], BF16); b_qTc = Buf("qTc")
    oTc = sb.take([128, 8, 512], BF16); b_oTc = Buf("oTc")
    hTc = sb.take([128, 8, 512], BF16); b_hTc = Buf("hTc3")
    ind2 = sb.take([128, 2], BF16); b_ind2 = Buf("ind2")
    knsq = sb.take([1, 2, 8], F32); b_knsq = Buf("knsq")
    knrow = sb.take([1, 2, 8], F32); b_knrow = Buf("knrow")
    b_cnqs = Buf("cnqs")
    qcsq = sb.take([1, 2, 8], F32); b_qcsq = Buf("qcsq")
    cnq = sb.take([1, 2, 8], F32); b_cnq = Buf("cnq")
    mrowZ0 = sb.take([128, 512], BF16)
    mrowZ1 = sb.take([128, 512], BF16)
    b_mz = [Buf("mz0"), Buf("mz1")]
    ov0 = sb.off
    PT = sb.take([128, 6, 512], BF16)
    srow = sb.take([128, 512], F32)
    cnqs = sb.take([128, 8], F32)
    rrep = sb.take([128, 2, 512], F32)
    sqb = sb.take([128, 512], BF16)
    tails = dict(
        ao=sb.overlay(ov0, [128, 1, 2, 512], F32),
        sq=sb.overlay(ov0 + 4096, [128, 1, C], F32),
        sm=sb.overlay(ov0 + 8192, [128, 1, 8], F32),
        xo=sb.overlay(ov0 + 8256, [128, 1, C], F32),
    )

    em.op("gpsimd", lambda g: g.dma_start(
        out=wab[:], in_=p.dram["L1_attn_w_attn"].rearrange("(k p) n -> p k n", p=128)),
        writes=[b_wab], dma_sem="ds_gp")
    em.op("gpsimd", lambda g: g.dma_start(
        out=wpb[:], in_=p.dram["L1_attn_w_proj"].rearrange("(k p) n -> p k n", p=128)),
        writes=[b_wpb], dma_sem="ds_gp")
    em.op("sync", lambda s: s.dma_start(
        out=bapp[:], in_=p.dram["L1_attn_b_attn"].rearrange("(k p) -> p k", p=128),
        allow_slow_non_contiguous=True),
        writes=[b_bapp], dma_sem="ds_sy")
    em.op("sync", lambda s: s.dma_start(
        out=bprow[:], in_=p.dram["L1_attn_b_proj"][None, :]),
        writes=[b_bprow], dma_sem="ds_sy")
    em.op("sync", lambda s: s.dma_start(
        out=bvrep[:], in_=p.dram["L1_attn_b_attn"][None, 2048:3072]
        .partition_broadcast(128)[:, 0, :]),
        writes=[b_bvrep], dma_sem="ds_sy")
    em.op("sync", lambda s: s.dma_start(
        out=alpp[:], in_=p.dram["c_slopepos"].rearrange("h (j q) -> q h j", q=128),
        allow_slow_non_contiguous=True),
        writes=[b_alpp], dma_sem="ds_sy")
    em.op("gpsimd", lambda g: g.dma_start(
        out=mtri[:], in_=p.dram["c_mtri"].rearrange("o q t -> q o t")),
        writes=[b_mtri], dma_sem="ds_gp")
    em.op("vector", lambda e: e.tensor_scalar(
        idneg[:], p.identb[:], -1e30, None, ALU.mult),
        reads=[p.b_identb], writes=[b_idneg], sig=True)
    em.op("vector", lambda e: e.memset(onesb[:], 1.0),
        writes=[b_onesb], sig=True)
    em.op("vector", lambda e: e.memset(mrowZ0[:], 0.0), writes=[b_mz[0]])
    em.op("vector", lambda e: e.memset(mrowZ1[:], 0.0), writes=[b_mz[1]], sig=True)
    em.op("vector", lambda e: e.tensor_scalar(
        bapp[:, 0:8], bapp[:, 0:8], 0.125, None, ALU.mult),
        reads=[b_bapp], writes=[b_bapp], sig=True)
    em.op("vector", lambda e: e.memset(ind2[0:64, 0:1], 1.0),
          writes=[b_ind2])
    em.op("vector", lambda e: e.memset(ind2[64:128, 0:1], 0.0),
          writes=[b_ind2])
    em.op("vector", lambda e: e.memset(ind2[0:64, 1:2], 0.0),
          writes=[b_ind2])
    em.op("vector", lambda e: e.memset(ind2[64:128, 1:2], 1.0),
          writes=[b_ind2], sig=True)

    for seq in range(NSEQ):
        t0 = seq * SEQ
        em.op("vector", lambda e: e.memset(Vag[:, :, :, 64:65], 1.0),
              writes=[b_V], sig=True)
        em.op("vector", lambda e: e.memset(knsq[:], 0.0),
              writes=[b_knsq], sig=True)
        b_tails_prev = None
        for tc in range(4):
            tg = t0 + tc * 512
            b_PT = Buf("PT3"); b_mrep = Buf("mrep3"); b_mcol = Buf("mcol3")
            b_mrow = Buf("mrow3"); b_srow = Buf("srow3")
            b_rr = Buf("rr3"); b_rrep = Buf("rrep3")
            if b_tails_prev is not None:
                for bb in (b_PT, b_mrep, b_mcol, b_mrow, b_srow, b_rr, b_rrep):
                    bb.merge_from(b_tails_prev)

            em.op("sync", lambda s, tg=tg: s.dma_start(
                out=hTc[:], in_=xlnT_d[:, :, tg:tg + 512]),
                reads=[p.b_xlnTd, b_qTc, b_kT], writes=[b_hTc], dma_sem="d_h3")
            for m in range(16):
                bank = p.bank[m % 4]
                for k in range(8):
                    em.op("tensor", lambda t, m=m, k=k: t.matmul(
                        p.psum_f[:, m % 4, :], wab[:, k, m * 128:(m + 1) * 128],
                        hTc[:, k, :], start=(k == 0), stop=(k == 7)),
                        reads=[b_wab, b_hTc], writes=[bank], sig=True)
                if m < 8:
                    em.op("scalar", lambda e, m=m: e.activation(
                        qTc[:, m, :], p.psum_f[:, m % 4, :], AF.Identity,
                        bias=bapp[:, m:m + 1], scale=0.125),
                        reads=[bank, b_bapp], writes=[b_qTc], sig=True)
                else:
                    em.op("scalar", lambda e, m=m, tc=tc: e.activation(
                        kT[:, m - 8, tc * 512:(tc + 1) * 512], p.psum_f[:, m % 4, :],
                        AF.Identity, bias=bapp[:, m:m + 1]),
                        reads=[bank, b_bapp], writes=[b_kT], sig=True)
            for tt4 in range(4):
                for nn in range(2):
                    bank = p.bank[4 + nn]
                    for k in range(8):
                        em.op("tensor", lambda t, tt4=tt4, nn=nn, k=k: t.matmul(
                            p.psum_f[:, 4 + nn, :], hTc[:, k, tt4 * 128:(tt4 + 1) * 128],
                            wab[:, k, 2048 + nn * 512:2048 + (nn + 1) * 512],
                            start=(k == 0), stop=(k == 7)),
                            reads=[b_hTc, b_wab], writes=[bank], sig=True)
                    em.op("vector", lambda e, tt4=tt4, nn=nn, tc=tc: e.tensor_tensor(
                        Vag[:, 4 * tc + tt4, nn * 8:(nn + 1) * 8, 0:64],
                        p.psum_f[:, 4 + nn, :].rearrange("p (h d) -> p h d", h=8),
                        bvrep[:, nn * 512:(nn + 1) * 512].rearrange("p (h d) -> p h d", h=8),
                        ALU.add),
                        reads=[bank, b_bvrep], writes=[b_V], sig=True)

            b_sqb = Buf("sqb3")
            if b_tails_prev is not None:
                b_sqb.merge_from(b_tails_prev)
            for kt in range(8):
                em.op("vector", lambda e, kt=kt: e.tensor_tensor(
                    sqb[:], qTc[:, kt, :], qTc[:, kt, :], ALU.mult),
                    reads=[b_qTc], writes=[b_sqb], sig=True)
                for r in range(2):
                    em.op("tensor", lambda t, r=r: t.matmul(
                        p.psum_f[0:1, 4 + r, :], ind2[64 * r:64 * r + 64, r:r + 1],
                        sqb[64 * r:64 * r + 64, :], start=True, stop=True),
                        reads=[b_ind2, b_sqb], writes=[p.bank[4 + r]], sig=True)
                for r in range(2):
                    em.op("vector", lambda e, kt=kt, r=r: e.reduce_max(
                        qcsq[0:1, r, kt:kt + 1], p.psum_f[0:1, 4 + r, :], axis=AXX),
                        reads=[p.bank[4 + r]], writes=[b_qcsq], sig=True)
                em.op("vector", lambda e, kt=kt, tc=tc: e.tensor_tensor(
                    sqb[:], kT[:, kt, tc * 512:(tc + 1) * 512],
                    kT[:, kt, tc * 512:(tc + 1) * 512], ALU.mult),
                    reads=[b_kT], writes=[b_sqb], sig=True)
                for r in range(2):
                    em.op("tensor", lambda t, r=r: t.matmul(
                        p.psum_f[0:1, 4 + r, :], ind2[64 * r:64 * r + 64, r:r + 1],
                        sqb[64 * r:64 * r + 64, :], start=True, stop=True),
                        reads=[b_ind2, b_sqb], writes=[p.bank[4 + r]], sig=True)
                for r in range(2):
                    em.op("vector", lambda e, kt=kt, r=r: e.reduce_max(
                        knrow[0:1, r, kt:kt + 1], p.psum_f[0:1, 4 + r, :], axis=AXX),
                        reads=[p.bank[4 + r]], writes=[b_knrow], sig=True)
            em.op("vector", lambda e: e.tensor_tensor(
                knsq[:], knsq[:], knrow[:], ALU.max),
                reads=[b_knsq, b_knrow], writes=[b_knsq], sig=True)
            em.op("scalar", lambda e: e.activation(cnq[:], qcsq[:], AF.Sqrt),
                  reads=[b_qcsq], writes=[b_cnq], sig=True)
            em.op("scalar", lambda e: e.activation(knrow[:], knsq[:], AF.Sqrt),
                  reads=[b_knsq], writes=[b_knrow], sig=True)
            em.op("vector", lambda e: e.tensor_tensor(
                cnq[:], cnq[:], knrow[:], ALU.mult),
                reads=[b_cnq, b_knrow], writes=[b_cnq], sig=True)
            em.op("vector", lambda e: e.tensor_copy(
                cnqs[64:65, :], cnq[0:1, 1, :]),
                reads=[b_cnq], writes=[b_cnqs], sig=True)

            nst = 4 * tc + 4
            loc = tc * 512
            b_PTs = [Buf(f"PT3{i}") for i in range(6)]
            b_srow = Buf("srow3")
            b_rrep = Buf("rrep3")
            if b_tails_prev is not None:
                for bb in b_PTs + [b_srow, b_rrep]:
                    bb.merge_from(b_tails_prev)
            for kt in range(8):
                for r in range(2):
                    em.op("sync", lambda s, kt=kt, loc=loc, r=r: s.dma_start(
                        out=srow[64 * r:64 * r + 1, :],
                        in_=p.dram["c_slopepos"][2 * kt + r][None, loc:loc + 512]),
                        reads=[b_srow], writes=[b_srow], dma_sem="d_sr3")
                em.op("vector", lambda e, kt=kt: e.tensor_scalar(
                    mrowZ0[0:1, :], srow[0:1, :], cnq[0:1, 0, kt:kt + 1], -1.0,
                    ALU.add, ALU.mult),
                    reads=[b_srow, b_cnq], writes=[b_mz[0]], sig=True)
                em.op("vector", lambda e, kt=kt: e.tensor_scalar(
                    mrowZ1[64:65, :], srow[64:65, :], cnqs[64:65, kt:kt + 1], -1.0,
                    ALU.add, ALU.mult),
                    reads=[b_srow, b_cnqs], writes=[b_mz[1]], sig=True)
                mzb = [mrowZ0, mrowZ1]
                nb = (nst + 2) // 3

                def jr(b_):
                    return range(3 * b_, min(3 * b_ + 3, nst))

                def bcast_blk(b_):
                    for j in jr(b_):
                        for r in range(2):
                            sbk = 3 * r + j % 3
                            em.op("tensor", lambda t, sbk=sbk, r=r: t.matmul(
                                p.psum_f[:, sbk, :], onesb[:, 0:128], mzb[r][:],
                                start=True, stop=False),
                                reads=[b_onesb, b_mz[r]], writes=[p.bank[sbk]])

                bcast_blk(0)
                for b_ in range(nb):
                    for j in jr(b_):
                        o = j - 4 * tc
                        for r in range(2):
                            pb = 64 * r
                            sbk = 3 * r + j % 3
                            em.op("tensor", lambda t, j=j, pb=pb, kt=kt, sbk=sbk, o=o: t.matmul(
                                p.psum_f[:, sbk, :],
                                kT[pb:pb + 64, kt, j * 128:(j + 1) * 128],
                                qTc[pb:pb + 64, kt, :],
                                start=False, stop=(o < 0)),
                                reads=[b_kT, b_qTc], writes=[p.bank[sbk]],
                                sig=(o < 0))
                    for j in jr(b_):
                        o = j - 4 * tc
                        if o >= 0:
                            for r in range(2):
                                sbk = 3 * r + j % 3
                                em.op("tensor", lambda t, sbk=sbk, o=o: t.matmul(
                                    p.psum_f[:, sbk, :], idneg[:], mtri[:, o, :],
                                    start=False, stop=True),
                                    reads=[b_idneg, b_mtri], writes=[p.bank[sbk]],
                                    sig=True)
                    if b_ > 0:
                        for j in jr(b_ - 1):
                            for r in range(2):
                                h = 2 * kt + r
                                sl = 3 * r + j % 3
                                em.op("tensor", lambda t, j=j, h=h, r=r, sl=sl, nst=nst: t.matmul(
                                    p.psum_f[0:65, 6 + r, :], Vag[:, j, h, :],
                                    PT[:, sl, :],
                                    start=(j == 0), stop=(j == nst - 1)),
                                    reads=[b_V, b_PTs[sl]], writes=[p.bank[6 + r]],
                                    sig=(r == 1 and j == 3 * b_ - 1))
                    for j in jr(b_):
                        for r in range(2):
                            sbk = 3 * r + j % 3
                            h = 2 * kt + r
                            em.op("scalar", lambda e, j=j, h=h, sbk=sbk: e.activation(
                                PT[:, sbk, :], p.psum_f[:, sbk, :], AF.Exp,
                                bias=alpp[:, h, j:j + 1]),
                                reads=[p.bank[sbk], b_alpp], writes=[b_PTs[sbk]],
                                sig=True)
                    if b_ + 1 < nb:
                        bcast_blk(b_ + 1)
                for j in jr(nb - 1):
                    for r in range(2):
                        h = 2 * kt + r
                        sl = 3 * r + j % 3
                        em.op("tensor", lambda t, j=j, h=h, r=r, sl=sl, nst=nst: t.matmul(
                            p.psum_f[0:65, 6 + r, :], Vag[:, j, h, :],
                            PT[:, sl, :],
                            start=(j == 0), stop=(j == nst - 1)),
                            reads=[b_V, b_PTs[sl]], writes=[p.bank[6 + r]],
                            sig=(r == 1 and j == nst - 1))
                for r in range(2):
                    em.op("vector", lambda e, r=r: e.reciprocal(
                        rrep[64:65, r, :], p.psum_f[64:65, 6 + r, :]),
                        reads=[p.bank[6 + r]], writes=[b_rrep], sig=True)
                    bb = p.bank[2 + 3 * r]
                    em.op("tensor", lambda t, r=r: t.matmul(
                        p.psum_f[:, 2 + 3 * r, :], p.one128[64:65, :], rrep[64:65, r, :],
                        start=True, stop=True),
                        reads=[b_rrep, p.b_one128], writes=[bb], sig=True)
                    em.op("vector", lambda e, r=r: e.tensor_copy(
                        rrep[:, r, :], p.psum_f[:, 2 + 3 * r, :]),
                        reads=[bb], writes=[b_rrep], sig=True)
                    em.op("vector", lambda e, r=r, kt=kt: e.tensor_tensor(
                        oTc[64 * r:64 * r + 64, kt, :], p.psum_f[0:64, 6 + r, :],
                        rrep[0:64, r, :], ALU.mult),
                        reads=[p.bank[6 + r], b_rrep], writes=[b_oTc], sig=True)

            b_tails = [Buf("t3a"), Buf("t3b"), Buf("t3c"), Buf("t3d")]
            for bb in b_tails:
                bb.merge_from(b_PTs + [b_srow, b_rrep, b_sqb])
            tl = dict(ao=tails["ao"], b_ao=[b_tails[0]], sq=tails["sq"], b_sq=[b_tails[1]],
                      sm=tails["sm"], b_sm=[b_tails[2]], xo=tails["xo"], b_xo=[b_tails[3]])
            for tt4 in range(4):
                tt = seq * 16 + tc * 4 + tt4
                banks = [p.bank[0], p.bank[1]]
                for hh in range(2):
                    for kk in range(8):
                        em.op("tensor", lambda t, hh=hh, kk=kk, tt4=tt4: t.matmul(
                            p.psum_f[:, hh, :], oTc[:, kk, tt4 * 128:(tt4 + 1) * 128],
                            wpb[:, kk, hh * 512:(hh + 1) * 512],
                            start=(kk == 0), stop=False),
                            reads=[b_oTc, b_wpb], writes=[banks[hh]])
                    em.op("tensor", lambda t, hh=hh: t.matmul(
                        p.psum_f[:, hh, :], p.onesrow[0:1, 0:128],
                        bprow[0:1, hh * 512:(hh + 1) * 512], start=False, stop=True),
                        reads=[p.b_onesrow, b_bprow], writes=[banks[hh]])
                em.fence("tensor")
                sandwich_tail_tile(p, tt, p.psum_f[:, 0:2, :], banks,
                                   d_res, p.b_res, d_res, p.b_res, tl, slot=0)
            b_tails_prev = b_tails
        em.play_block()
    sb.reset()


def phase4(p: P, n_experts=64, dbg_stop=None):
    em, sb = p.em, p.sb
    E = n_experts
    d_res = p.dram["d_res"]
    xlnT_d = p.dram["d_xlnT"].rearrange("(k p) t -> p k t", p=128)
    coefT_d = p.dram["d_coefT"]
    b_coefT = Buf("d_coefT")

    rWTf = sb.take([128, 8, 64], F32); b_rWTf = Buf("rWTf")
    rtmpf = sb.take([64, 8, 128], F32); b_rtmpf = Buf("rtmpf")
    p4base = sb.off
    em.op("sync", lambda s: s.dma_start(
        out=rtmpf[:], in_=p.dram["L1_ffn_router_w"].rearrange("e (k c) -> e k c", k=8)),
        writes=[b_rtmpf], dma_sem="ds_sy")
    for k in range(8):
        em.op("tensor", lambda t, k=k: t.matmul(
            p.psum_f[0:128, 4 + k % 2, 0:64], rtmpf[:, k, :], p.identf[0:64, 0:64],
            start=True, stop=True),
            reads=[b_rtmpf, p.b_identf], writes=[p.bank[4 + k % 2]], sig=True)
        em.op("vector", lambda e, k=k: e.tensor_copy(rWTf[:, k, :], p.psum_f[:, 4 + k % 2, 0:64]),
              reads=[p.bank[4 + k % 2]], writes=[b_rWTf], sig=True)
    em.play_block()

    ln_transpose_store(p, d_res, p.b_res, ln=True, shift=False,
                       router=rWTf, b_router=b_rWTf)

    sb.reset(p4base)
    save0 = sb.off
    ebrep = sb.take([128, 64], F32); b_ebrep = Buf("ebrep")
    ebrow = sb.take([1, 64], F32); b_ebrow = Buf("ebrow")
    eidx = sb.take([128, 64], F32); b_eidx = Buf("eidx")
    hRt = sb.take([64, 8, 128], F32); b_hRt = [Buf() for _ in range(2)]
    sco = sb.take([128, 8, 64], F32); b_sco = [Buf() for _ in range(2)]
    sfc = sb.take([128, 8, 8, 8], F32); b_sfc = [Buf() for _ in range(2)]
    rnk = sb.take([128, 8, 8, 8], F32); b_rnk = [Buf() for _ in range(2)]
    tmpr = sb.take([128, 8, 8, 8], F32); b_tmpr = [Buf() for _ in range(2)]
    gsc = sb.take([128, 8, 8], F32); b_gsc = [Buf() for _ in range(2)]
    grk = sb.take([128, 8, 8], F32); b_grk = [Buf() for _ in range(2)]
    vv = sb.take([128, 8, 64], F32); b_vv = [Buf() for _ in range(2)]
    vk = sb.take([128, 8, 64], F32); b_vk = [Buf() for _ in range(2)]
    th = sb.take([128, 8, 4], F32); b_th = [Buf() for _ in range(2)]
    cf = sb.take([128, 8, 64], F32); b_cf = [Buf() for _ in range(2)]
    cev = sb.take([64, 8, 128], F32); b_cev = [Buf() for _ in range(2)]

    em.op("sync", lambda s: s.dma_start(
        out=ebrow[:], in_=p.dram["L1_ffn_e_bias"][None, :]),
        writes=[b_ebrow], dma_sem="ds_sy")
    em.op("sync", lambda s: s.dma_start(
        out=eidx[:], in_=p.dram["c_eidx"][:]),
        writes=[b_eidx], dma_sem="ds_sy")
    em.op("tensor", lambda t: t.matmul(
        p.psum_f[:, 6, 0:64], p.onesrow[0:1, 0:128], ebrow[0:1, :],
        start=True, stop=True),
        reads=[b_ebrow, p.b_onesrow], writes=[p.bank[6]], sig=True)
    em.op("vector", lambda e: e.tensor_copy(ebrep[:], p.psum_f[:, 6, 0:64]),
          reads=[p.bank[6]], writes=[b_ebrep], sig=True)

    def S4(buf, q0):
        return buf[:, q0:q0 + 4]

    def SG(buf, q0):
        return buf.rearrange("p s g u -> p (s g) u")[:, q0 * 8:(q0 + 4) * 8, :]

    NG = NT // 4
    for g in range(NG):
        q0 = (g % 2) * 4
        gp = g % 2


        em.op("sync", lambda s, g=g, q0=q0: s.dma_start(
            out=hRt[:, q0:q0 + 4].rearrange("p s f -> p (s f)"),
            in_=p.dram["d_scoT"][:, g * 512:(g + 1) * 512]),
            reads=[Buf("d_scoT"), b_hRt[gp]], writes=[b_hRt[gp]], dma_sem=f"d_hr{gp}")
        for i in range(4):
            em.op("tensor", lambda t, q0=q0, i=i: t.matmul(
                p.psum_f[:, i, 0:64], hRt[:, q0 + i], p.identf[0:64, 0:64],
                start=True, stop=True),
                reads=[b_hRt[gp], p.b_identf], writes=[p.bank[i]], sig=True)
            em.op("scalar", lambda e, q0=q0, i=i: e.activation(
                sco[:, q0 + i], p.psum_f[:, i, 0:64], AF.Sigmoid),
                reads=[p.bank[i]], writes=[b_sco[gp]], sig=True)
        em.op("vector", lambda e, q0=q0: e.tensor_tensor(
            S4(sfc, q0).rearrange("p s g u -> p s (g u)"), S4(sco, q0),
            ebrep[:].rearrange("p (s f) -> p s f", s=1).broadcast_to([128, 4, 64]),
            ALU.add),
            reads=[b_sco[gp], b_ebrep], writes=[b_sfc[gp]], sig=True)
        em.op("vector", lambda e, q0=q0: e.memset(
            S4(rnk, q0).rearrange("p s g u -> p s (g u)"), 0.0),
            writes=[b_rnk[gp]], sig=True)
        for d in range(1, 8):
            em.op("vector", lambda e, q0=q0, d=d: e.tensor_tensor(
                SG(tmpr, q0)[:, :, d:8], SG(sfc, q0)[:, :, d:8], SG(sfc, q0)[:, :, 0:8 - d], ALU.is_lt),
                reads=[b_sfc[gp]], writes=[b_tmpr[gp]], sig=True)
            em.op("vector", lambda e, q0=q0, d=d: e.tensor_tensor(
                SG(rnk, q0)[:, :, d:8], SG(rnk, q0)[:, :, d:8], SG(tmpr, q0)[:, :, d:8], ALU.add),
                reads=[b_tmpr[gp], b_rnk[gp]], writes=[b_rnk[gp]], sig=True)
            em.op("vector", lambda e, q0=q0, d=d: e.tensor_tensor(
                SG(tmpr, q0)[:, :, 0:8 - d], SG(sfc, q0)[:, :, 0:8 - d], SG(sfc, q0)[:, :, d:8], ALU.is_lt),
                reads=[b_sfc[gp]], writes=[b_tmpr[gp]], sig=True)
            em.op("vector", lambda e, q0=q0, d=d: e.tensor_tensor(
                SG(rnk, q0)[:, :, 0:8 - d], SG(rnk, q0)[:, :, 0:8 - d], SG(tmpr, q0)[:, :, 0:8 - d], ALU.add),
                reads=[b_tmpr[gp], b_rnk[gp]], writes=[b_rnk[gp]], sig=True)
            em.op("vector", lambda e, q0=q0, d=d: e.tensor_tensor(
                SG(tmpr, q0)[:, :, d:8], SG(sfc, q0)[:, :, d:8], SG(sfc, q0)[:, :, 0:8 - d], ALU.is_equal),
                reads=[b_sfc[gp]], writes=[b_tmpr[gp]], sig=True)
            em.op("vector", lambda e, q0=q0, d=d: e.tensor_tensor(
                SG(rnk, q0)[:, :, d:8], SG(rnk, q0)[:, :, d:8], SG(tmpr, q0)[:, :, d:8], ALU.add),
                reads=[b_tmpr[gp], b_rnk[gp]], writes=[b_rnk[gp]], sig=True)
        em.op("vector", lambda e, q0=q0: e.tensor_scalar(
            SG(tmpr, q0), SG(rnk, q0), 1.0, None, ALU.is_le),
            reads=[b_rnk[gp]], writes=[b_tmpr[gp]])
        em.op("vector", lambda e, q0=q0: e.tensor_tensor(
            SG(tmpr, q0), SG(tmpr, q0), SG(sfc, q0), ALU.mult),
            reads=[b_tmpr[gp], b_sfc[gp]], writes=[b_tmpr[gp]], sig=True)
        em.op("vector", lambda e, q0=q0: e.tensor_reduce(
            S4(gsc, q0).rearrange("p s g -> p (s g)"), SG(tmpr, q0), axis=AXX, op=ALU.add),
            reads=[b_tmpr[gp]], writes=[b_gsc[gp]], sig=True)
        em.op("vector", lambda e, q0=q0: e.memset(S4(grk, q0), 0.0),
              writes=[b_grk[gp]], sig=True)
        for d in range(1, 8):
            em.op("vector", lambda e, q0=q0, d=d: e.tensor_tensor(
                S4(tmpr, q0)[:, :, 0, d:8], S4(gsc, q0)[:, :, d:8], S4(gsc, q0)[:, :, 0:8 - d], ALU.is_lt),
                reads=[b_gsc[gp]], writes=[b_tmpr[gp]], sig=True)
            em.op("vector", lambda e, q0=q0, d=d: e.tensor_tensor(
                S4(grk, q0)[:, :, d:8], S4(grk, q0)[:, :, d:8], S4(tmpr, q0)[:, :, 0, d:8], ALU.add),
                reads=[b_tmpr[gp], b_grk[gp]], writes=[b_grk[gp]], sig=True)
            em.op("vector", lambda e, q0=q0, d=d: e.tensor_tensor(
                S4(tmpr, q0)[:, :, 0, 0:8 - d], S4(gsc, q0)[:, :, 0:8 - d], S4(gsc, q0)[:, :, d:8], ALU.is_lt),
                reads=[b_gsc[gp]], writes=[b_tmpr[gp]], sig=True)
            em.op("vector", lambda e, q0=q0, d=d: e.tensor_tensor(
                S4(grk, q0)[:, :, 0:8 - d], S4(grk, q0)[:, :, 0:8 - d], S4(tmpr, q0)[:, :, 0, 0:8 - d], ALU.add),
                reads=[b_tmpr[gp], b_grk[gp]], writes=[b_grk[gp]], sig=True)
            em.op("vector", lambda e, q0=q0, d=d: e.tensor_tensor(
                S4(tmpr, q0)[:, :, 0, d:8], S4(gsc, q0)[:, :, d:8], S4(gsc, q0)[:, :, 0:8 - d], ALU.is_equal),
                reads=[b_gsc[gp]], writes=[b_tmpr[gp]], sig=True)
            em.op("vector", lambda e, q0=q0, d=d: e.tensor_tensor(
                S4(grk, q0)[:, :, d:8], S4(grk, q0)[:, :, d:8], S4(tmpr, q0)[:, :, 0, d:8], ALU.add),
                reads=[b_tmpr[gp], b_grk[gp]], writes=[b_grk[gp]], sig=True)
        em.op("vector", lambda e, q0=q0: e.tensor_scalar(
            S4(grk, q0), S4(grk, q0), 3.0, None, ALU.is_le),
            reads=[b_grk[gp]], writes=[b_grk[gp]], sig=True)
        em.op("vector", lambda e, q0=q0: e.tensor_tensor(
            SG(vv.rearrange("p s (g u) -> p s g u", u=8), q0),
            SG(sfc, q0),
            S4(grk, q0).rearrange("p s (g u) -> p (s g) u", u=1).broadcast_to([128, 32, 8]),
            ALU.mult),
            reads=[b_sfc[gp], b_grk[gp]], writes=[b_vv[gp]], sig=True)
        em.op("vector", lambda e, q0=q0: e.scalar_tensor_tensor(
            S4(vv, q0), eidx[:].rearrange("p (s f) -> p s f", s=1).broadcast_to([128, 4, 64]),
            -1e-20, S4(vv, q0), ALU.mult, ALU.add),
            reads=[b_eidx, b_vv[gp]], writes=[b_vv[gp]])
        em.op("vector", lambda e, q0=q0: e.tensor_copy(S4(vk, q0), S4(vv, q0)),
              reads=[b_vv[gp]], writes=[b_vk[gp]], sig=True)
        for i8 in range(8):
            em.op("vector", lambda e, q0=q0: e.reduce_max(
                S4(th, q0)[:, :, 0:1].rearrange("p s f -> p (s f)"), S4(vk, q0), axis=AXX),
                reads=[b_vk[gp]], writes=[b_th[gp]], sig=True)
            if i8 < 7:
                em.op("vector", lambda e, q0=q0: e.tensor_tensor(
                    S4(tmpr, q0).rearrange("p s g u -> p s (g u)"), S4(vk, q0),
                    S4(th, q0)[:, :, 0:1].broadcast_to([128, 4, 64]), ALU.is_equal),
                    reads=[b_vk[gp], b_th[gp]], writes=[b_tmpr[gp]], sig=True)
                em.op("vector", lambda e, q0=q0: e.scalar_tensor_tensor(
                    S4(vk, q0), S4(tmpr, q0).rearrange("p s g u -> p s (g u)"), -1e30,
                    S4(vk, q0), ALU.mult, ALU.add),
                    reads=[b_tmpr[gp], b_vk[gp]], writes=[b_vk[gp]], sig=True)
        em.op("vector", lambda e, q0=q0: e.tensor_tensor(
            S4(vv, q0), S4(vv, q0), S4(th, q0)[:, :, 0:1].broadcast_to([128, 4, 64]), ALU.is_ge),
            reads=[b_vv[gp], b_th[gp]], writes=[b_vv[gp]], sig=True)
        em.op("vector", lambda e, q0=q0: e.tensor_tensor(
            S4(cf, q0), S4(sco, q0), S4(vv, q0), ALU.mult),
            reads=[b_sco[gp], b_vv[gp]], writes=[b_cf[gp]], sig=True)
        em.op("vector", lambda e, q0=q0: e.reduce_sum(
            S4(th, q0)[:, :, 1:2].rearrange("p s f -> p (s f)"), S4(cf, q0), axis=AXX),
            reads=[b_cf[gp]], writes=[b_th[gp]])
        em.op("vector", lambda e, q0=q0: e.tensor_scalar(
            S4(th, q0)[:, :, 1:2], S4(th, q0)[:, :, 1:2], 1e-20, None, ALU.add),
            reads=[b_th[gp]], writes=[b_th[gp]], sig=True)
        em.op("vector", lambda e, q0=q0: e.reciprocal(
            S4(th, q0)[:, :, 2:3], S4(th, q0)[:, :, 1:2]),
            reads=[b_th[gp]], writes=[b_th[gp]])
        em.op("vector", lambda e, q0=q0: e.tensor_scalar(
            S4(th, q0)[:, :, 2:3], S4(th, q0)[:, :, 2:3], 2.5, None, ALU.mult),
            reads=[b_th[gp]], writes=[b_th[gp]], sig=True)
        em.op("vector", lambda e, q0=q0: e.tensor_tensor(
            S4(cf, q0), S4(cf, q0), S4(th, q0)[:, :, 2:3].broadcast_to([128, 4, 64]), ALU.mult),
            reads=[b_cf[gp], b_th[gp]], writes=[b_cf[gp]], sig=True)
        for i in range(4):
            em.op("tensor", lambda t, q0=q0, i=i: t.matmul(
                p.psum_f[0:64, 4 + i, 0:128], cf[:, q0 + i], p.identf[:],
                start=True, stop=True),
                reads=[b_cf[gp], p.b_identf], writes=[p.bank[4 + i]],
                sig=(i == 3))
        em.op("vector", lambda e, q0=q0: e.tensor_copy(
            cev[0:64, q0:q0 + 4, :], p.psum_f[0:64, 4:8, 0:128]),
            reads=[p.bank[4 + j] for j in range(4)], writes=[b_cev[gp]], sig=True)
        em.op("sync", lambda s, g=g, q0=q0: s.dma_start(
            out=coefT_d[:, g * 512:(g + 1) * 512],
            in_=cev[0:64, q0:q0 + 4, :].rearrange("p s f -> p (s f)")),
            reads=[b_cev[gp]], writes=[b_coefT], dma_sem=f"d_cfo{gp}")
    em.play_block()
    sb.reset(save0)
    if dbg_stop == "routing":
        sb.reset()
        return

    HTOK = T // 2
    for half in range(2):
        sb.reset(save0)
        h0 = half * HTOK
        yac = sb.take([128, 8, HTOK], F32); b_yac = Buf("yac")
        xh = sb.take([128, 8, HTOK], BF16); b_xh = Buf("xh")
        WREG = sb.take([128, 3, 8, 1024], BF16)
        ovl0 = sb.off
        hS = sb.take([128, 8, 256], BF16)
        t1 = sb.take([128, 4, 512], F32)
        crep2 = sb.take([128, 2, HTOK], F32)
        b_creps = [Buf("crep0"), Buf("crep1")]
        b_we = [Buf("we0"), Buf("we1")]
        b_wd = [Buf("wd0"), Buf("wd1")]
        b_t1s = [Buf("t1a"), Buf("t1b")]
        b_hSs = [Buf("hSa"), Buf("hSb")]

        em.op("sync", lambda s, h0=h0: s.dma_start(
            out=xh[:], in_=xlnT_d[:, :, h0:h0 + HTOK]),
            reads=[p.b_xlnTd], writes=[b_xh], dma_sem="ds_sy")

        NU = E + 2
        NTC = HTOK // 256

        def unit_w(u):
            ws = u % 2
            if u < 2:
                em.op("gpsimd", lambda g, u=u, ws=ws: g.dma_start(
                    out=WREG[:, ws, :, 0:512],
                    in_=p.dram["L1_ffn_SWg"].rearrange(
                        "(k p) n -> p k n", p=128)[:, :, u * 512:(u + 1) * 512]),
                    writes=[b_we[ws]], dma_sem=f"d_we{ws}")
                em.op("gpsimd", lambda g, u=u, ws=ws: g.dma_start(
                    out=WREG[:, ws, :, 512:1024],
                    in_=p.dram["L1_ffn_SWu"].rearrange(
                        "(k p) n -> p k n", p=128)[:, :, u * 512:(u + 1) * 512]),
                    writes=[b_we[ws]], dma_sem=f"d_we{ws}")
                em.op("gpsimd", lambda g, u=u, ws=ws: g.dma_start(
                    out=WREG[:, 2, 4 * ws:4 * ws + 4, :],
                    in_=p.dram["L1_ffn_SWd"][u * 512:(u + 1) * 512, :].rearrange(
                        "(k p) n -> p k n", p=128)),
                    writes=[b_wd[ws]], dma_sem=f"d_wd{ws}")
            else:
                e_ = u - 2
                em.op("gpsimd", lambda g, e_=e_, ws=ws: g.dma_start(
                    out=WREG[:, ws, :, 0:512],
                    in_=p.dram["L1_ffn_Wg"][e_].rearrange("(k p) n -> p k n", p=128)),
                    writes=[b_we[ws]], dma_sem=f"d_we{ws}")
                em.op("gpsimd", lambda g, e_=e_, ws=ws: g.dma_start(
                    out=WREG[:, ws, :, 512:1024],
                    in_=p.dram["L1_ffn_Wu"][e_].rearrange("(k p) n -> p k n", p=128)),
                    writes=[b_we[ws]], dma_sem=f"d_we{ws}")
                em.op("gpsimd", lambda g, e_=e_, ws=ws: g.dma_start(
                    out=WREG[:, 2, 4 * ws:4 * ws + 4, :],
                    in_=p.dram["L1_ffn_Wd"][e_].rearrange("(k p) n -> p k n", p=128)),
                    writes=[b_wd[ws]], dma_sem=f"d_wd{ws}")

        def moe_crep(u):
            e_, sl = u - 2, u % 2
            em.op("sync", lambda s, e_=e_, h0=h0, sl=sl: s.dma_start(
                out=crep2[:, sl], in_=coefT_d[e_, h0:h0 + HTOK][None, :]
                .partition_broadcast(128)[:, 0, :]),
                reads=[b_coefT], writes=[b_creps[sl]], dma_sem=f"d_cr{sl}")

        def moe_gu(u, tc, which):
            ws = u % 2
            b0, off = (0, 0) if which == 0 else (2, 512)
            for m in range(4):
                bank = p.bank[b0 + m // 2]
                for k in range(8):
                    em.op("tensor", lambda t, ws=ws, tc=tc, m=m, k=k, b0=b0, off=off: t.matmul(
                        p.psum_f[:, b0 + m // 2, (m % 2) * 256:(m % 2 + 1) * 256],
                        WREG[:, ws, k, off + m * 128:off + (m + 1) * 128],
                        xh[:, k, tc * 256:(tc + 1) * 256],
                        start=(k == 0), stop=(k == 7)),
                        reads=[b_we[ws], b_xh], writes=[bank],
                        sig=(m == 3 and k == 7))

        def moe_h(u, tc):
            sl = tc % 2
            t1v = t1[:, 2 * sl:2 * sl + 2, :]
            em.op("scalar", lambda e, t1v=t1v: e.activation(
                t1v.rearrange("p a b -> p (a b)"),
                p.psum_f[:, 0:2, :].rearrange("p a b -> p (a b)"), AF.Silu),
                reads=[p.bank[0], p.bank[1]], writes=[b_t1s[sl]], sig=True)
            if u >= 2:
                esl = u % 2
                em.op("vector", lambda e, t1v=t1v, tc=tc, esl=esl: e.tensor_tensor(
                    t1v.rearrange("p a (b f) -> p (a b) f", f=256),
                    t1v.rearrange("p a (b f) -> p (a b) f", f=256),
                    crep2[:, esl, tc * 256:(tc + 1) * 256].rearrange(
                        "p (a f) -> p a f", a=1).broadcast_to([128, 4, 256]),
                    ALU.mult),
                    reads=[b_t1s[sl], b_creps[esl]], writes=[b_t1s[sl]], sig=True)
            em.op("vector", lambda e, t1v=t1v, sl=sl: e.tensor_tensor(
                hS[:, 4 * sl:4 * sl + 4, :],
                t1v.rearrange("p a (b f) -> p (a b) f", f=256),
                p.psum_f[:, 2:4, :].rearrange("p a (b f) -> p (a b) f", f=256),
                ALU.mult),
                reads=[b_t1s[sl], p.bank[2], p.bank[3]], writes=[b_hSs[sl]], sig=True)

        def moe_down(u, tc):
            ws = u % 2
            sl = tc % 2
            for mo in range(8):
                bank = p.bank[4 + mo // 2]
                for k in range(4):
                    em.op("tensor", lambda t, ws=ws, mo=mo, k=k, sl=sl: t.matmul(
                        p.psum_f[:, 4 + mo // 2, (mo % 2) * 256:(mo % 2 + 1) * 256],
                        WREG[:, 2, 4 * ws + k, mo * 128:(mo + 1) * 128],
                        hS[:, 4 * sl + k, :],
                        start=(k == 0), stop=(k == 3)),
                        reads=[b_wd[ws], b_hSs[sl]], writes=[bank],
                        sig=(mo == 7 and k == 3))
            pv = p.psum_f[:, 4:8, :].rearrange("p a (b f) -> p (a b) f", f=256)
            if u == 0:
                em.op("vector", lambda e, tc=tc, pv=pv: e.tensor_copy(
                    yac[:, :, tc * 256:(tc + 1) * 256], pv),
                    reads=[p.bank[4 + j] for j in range(4)],
                    writes=[b_yac], sig=True)
            else:
                em.op("vector", lambda e, tc=tc, pv=pv: e.tensor_tensor(
                    yac[:, :, tc * 256:(tc + 1) * 256],
                    yac[:, :, tc * 256:(tc + 1) * 256], pv, ALU.add),
                    reads=[p.bank[4 + j] for j in range(4)] + [b_yac],
                    writes=[b_yac], sig=True)

        unit_w(0)
        if NU > 1:
            unit_w(1)
        if NU > 2:
            moe_crep(2)
        if NU > 3:
            moe_crep(3)
        prev = None
        for u in range(NU):
            for tc in range(NTC):
                moe_gu(u, tc, 0)
                if prev is not None:
                    moe_down(*prev)
                    prev = None
                    if tc == 0:
                        if u + 1 < NU:
                            unit_w(u + 1)
                        if 4 <= u + 1 < NU:
                            moe_crep(u + 1)
                moe_gu(u, tc, 1)
                moe_h(u, tc)
                prev = (u, tc)
        moe_down(*prev)

        tails = make_tail_bufs(p, slots=1, at=ovl0,
                               guards=b_creps + b_t1s + b_hSs)

        for lt in range(HTOK // 128):
            tt = half * 16 + lt
            sl = lt % 2
            banks = [p.bank[2 * sl], p.bank[2 * sl + 1]]
            for k in range(8):
                em.op("tensor", lambda t, lt=lt, sl=sl, k=k: t.matmul(
                    p.psum_f[:, 2 * sl + k // 4, (k % 4) * 128:(k % 4 + 1) * 128],
                    yac[:, k, lt * 128:(lt + 1) * 128], p.identf[:],
                    start=True, stop=True),
                    reads=[b_yac, p.b_identf], writes=[banks[k // 4]])
            em.fence("tensor")
            sandwich_tail_tile(p, tt, p.psum_f[:, 2 * sl:2 * sl + 2, :], banks,
                               d_res, p.b_res, p.dram["out"], p.b_out, tails, slot=0)
        em.play_block()
    sb.reset()


def copy_pass(p: P, src_dram, b_src, dst_dram, b_dst):
    em, sb = p.em, p.sb
    save = sb.off
    cp = sb.take([128, 2, C], F32)
    b_cp = [Buf(), Buf()]
    for i in range(NT):
        sl = i % 2
        em.op("sync", lambda s, i=i, sl=sl: s.dma_start(
            out=cp[:, sl], in_=src_dram[i * 128:(i + 1) * 128, :]),
            reads=[b_src], writes=[b_cp[sl]], dma_sem=f"d_cpi{sl}")
        em.op("sync", lambda s, i=i, sl=sl: s.dma_start(
            out=dst_dram[i * 128:(i + 1) * 128, :], in_=cp[:, sl]),
            reads=[b_cp[sl]], writes=[b_dst], dma_sem=f"d_cpp{sl}")
    em.play_block()
    sb.reset(save)


def host_constants():
    cs = {}
    cs["c_ident"] = np.eye(128, dtype=np.float32)
    cs["c_ones"] = np.ones((1, 512), np.float32)
    cs["c_trem"] = (SEQ - np.arange(1, SEQ + 1, dtype=np.float32))[None, :]
    DH, WIN = 64, 512
    half = np.arange(0, DH, 2, dtype=np.float32)
    inv = 1.0 / 10000.0 ** (half / DH)
    t = np.arange(SEQ, dtype=np.float32)
    freqs = t[:, None] * inv[None, :]
    freqs = np.concatenate([freqs, freqs], -1)
    base = (half + 0.4 * DH) / (1.4 * DH)
    power = (t - SEQ // 2) / float(WIN // 2)
    scale = base[None, :] ** power[:, None]
    scale = np.concatenate([scale, scale], -1)
    cos, sin = np.cos(freqs), np.sin(freqs)
    cs_t = (cos * scale).astype(np.float32)
    ss_t = (sin * scale).astype(np.float32)
    csk_t = (cos / scale).astype(np.float32)
    ssk_t = (sin / scale).astype(np.float32)
    def sign_halves(a):
        b = a.copy()
        b[:, 0:32] = -b[:, 0:32]
        return b
    rot = np.stack([np.tile(a, (1, 8)) for a in
                    (cs_t, sign_halves(ss_t), csk_t, sign_halves(ssk_t))])
    cs["c_rot"] = rot.astype(np.float32)
    slopes = 2.0 ** (-(8.0 / 16) * np.arange(1, 17, dtype=np.float32))
    cs["c_slopepos"] = (slopes[:, None] * np.arange(SEQ, dtype=np.float32)[None, :]).astype(np.float32)
    p_ = np.arange(128)[:, None]
    t_ = np.arange(512)[None, :]
    md = np.zeros((4, 128, 512), np.float32)
    for o in range(4):
        md[o] = np.where(t_ < o * 128 + p_, -1e30, 0.0)
    cs["c_maskdiag"] = md
    cs["c_maskfar"] = np.where(t_ > p_, -1e30, 0.0).astype(np.float32)
    mt = np.zeros((4, 128, 512), np.float32)
    for o in range(4):
        mt[o] = (t_ < o * 128 + p_).astype(np.float32)
    cs["c_mtri"] = mt
    cs["c_ftri"] = (t_ > p_).astype(np.float32)
    cs["c_masktri"] = np.zeros((128, 128), np.float32)
    cs["c_one128"] = np.ones((128, 128), np.float32)
    cs["c_eidx"] = np.tile(np.arange(64, dtype=np.float32)[None, :], (128, 1))
    return cs


def build(phases=("p1",), dbg=False, dbg_stop=None, n_experts=64):
    nc = bass.Bass("TRN2", target_bir_lowering=False, debug=False)
    p = P(nc, dbg)
    declare_io(p, phases[-1])
    load_consts(p)
    if "p1" in phases:
        phase1(p)
    else:
        copy_pass(p, p.dram["L0_attn_x"], Buf("xin0d"), p.dram["d_res"], p.b_res)
    if "p2" in phases:
        phase2(p, dbg_stop=dbg_stop)
    if "p3" in phases:
        phase3(p)
    if "p4" in phases:
        phase4(p, n_experts=n_experts, dbg_stop=dbg_stop if dbg_stop in ("routing",) else None)
    if "p4" in phases:
        em = p.em
        em.play_block()
        return nc, p
    em = p.em
    cp = p.sb.take([128, 2, C], F32)
    b_cp = [Buf(), Buf()]
    for i in range(NT):
        sl = i % 2
        em.op("sync", lambda s, i=i, sl=sl: s.dma_start(
            out=cp[:, sl], in_=p.dram["d_res"][i * 128:(i + 1) * 128, :]),
            reads=[p.b_res], writes=[b_cp[sl]], dma_sem=f"d_cp{sl}")
        em.op("sync", lambda s, i=i, sl=sl: s.dma_start(
            out=p.dram["out"][i * 128:(i + 1) * 128, :], in_=cp[:, sl]),
            reads=[b_cp[sl]], writes=[p.b_out], dma_sem=f"d_cpo{sl}")
    em.raw_wait("sync", "d_cpo0", p.em.cnt["d_cpo0"])
    em.raw_wait("sync", "d_cpo1", p.em.cnt["d_cpo1"])
    em.play_block()
    p.sb.reset()
    return nc, p


_NCORES = 8
_cache = {}


def _get_program():
    if "nc" not in _cache:
        nc, p = build(phases=("p1", "p2", "p3", "p4"))
        _cache["nc"] = nc
    return _cache["nc"]


def kernel(**inputs):
    nc = _get_program()
    consts = host_constants()
    x = np.asarray(inputs["L0_attn_x"], dtype=np.float32)
    base = {}
    for k, v in inputs.items():
        if k == "L0_attn_x":
            continue
        base[k] = np.ascontiguousarray(np.asarray(v, dtype=np.float32))
    base.update(consts)
    in_maps = []
    for c in range(_NCORES):
        m = dict(base)
        m["L0_attn_x"] = np.ascontiguousarray(
            x[2 * c:2 * c + 2].reshape(T, C))
        in_maps.append(m)
    res = run_bass_kernel_spmd(nc, in_maps, core_ids=list(range(_NCORES)))
    out = np.concatenate(
        [np.asarray(r["out"]).reshape(2, SEQ, C) for r in res.results], axis=0)
    return out
